# version 15
# baseline (speedup 1.0000x reference)
import contextlib
import numpy as np
import concourse.bass as bass
import concourse.mybir as mybir
from concourse.bass_utils import run_bass_kernel_spmd
from concourse.aot_env import allow_ici_all_to_all

F32 = mybir.dt.float32
BF16 = mybir.dt.bfloat16
F16 = mybir.dt.float16
I32 = mybir.dt.int32
AF = mybir.ActivationFunctionType
ALU = mybir.AluOpType
AXX = mybir.AxisListType.X

HID = 2048
NH = 16
HD = 128
IFF = 5632
B = 2
S = 4096
T = 8192
TC = 1024
NCORES = 8
OGU = 2 * IFF
OGU_C = OGU // 8
NGT = OGU_C // 128
DH_C = HID // 8
NDT = DH_C // 128
ISC = float(1.0 / np.sqrt(128.0))
NEG = -1e9


class Cnt:
    __slots__ = ("h", "n")

    def __init__(self, h):
        self.h = h
        self.n = 0

    def tag(self, instr, k=1):
        instr.then_inc(self.h, k)
        self.n += k
        return self.n


def build_program():
    nc = bass.Bass("TRN2", target_bir_lowering=False, debug=False)
    uid_box = [0]

    def uid(s):
        uid_box[0] += 1
        return f"{s}_{uid_box[0]}"

    x_t = nc.dram_tensor("x_t", [HID, TC], F32, kind="ExternalInput").ap()
    wqkv = nc.dram_tensor("wqkv_sl", [HID, 6 * HD], F32, kind="ExternalInput").ap()
    wout = nc.dram_tensor("wout_sl", [HID, HID], F32, kind="ExternalInput").ap()
    gup = nc.dram_tensor("gup_sl", [OGU_C, 256], I32, kind="ExternalInput").ap()
    gus = nc.dram_tensor("gus_sl", [16, OGU_C], F16, kind="ExternalInput").ap()
    dnp = nc.dram_tensor("dnp_sl", [DH_C, 704], I32, kind="ExternalInput").ap()
    dns = nc.dram_tensor("dns_sl", [44, DH_C], F16, kind="ExternalInput").ap()
    out_t = nc.dram_tensor("out_t", [HID, TC], F32, kind="ExternalOutput").ap()

    GU_PARTS = ((0, 4), (4, 8), (8, 11))
    wgu_piece = [nc.dram_tensor(f"wgu_piece{i}",
                                [128, (hi - lo) * 2048], BF16).ap()
                 for i, (lo, hi) in enumerate(GU_PARTS)]
    g_gu = [nc.dram_tensor(f"g_gu{i}", [8 * 128, (hi - lo) * 2048], BF16).ap()
            for i, (lo, hi) in enumerate(GU_PARTS)]
    wd_piece = nc.dram_tensor("wd_piece", [128, NDT * 5632], BF16).ap()
    g_d = nc.dram_tensor("g_d", [8 * 128, NDT * 5632], BF16).ap()
    xh_p = [nc.dram_tensor(f"xh_p{i}", [HID, 512], BF16).ap()
            for i in range(2)]
    g_xs = [nc.dram_tensor(f"g_x{i}", [8 * HID, 512], BF16).ap()
            for i in range(2)]
    a2a_i = [nc.dram_tensor(f"a2a_i{i}", [8 * HD, TC], BF16).ap()
             for i in range(2)]
    a2a_o = [nc.dram_tensor(f"a2a_o{i}", [8 * HD, TC], BF16).ap()
             for i in range(2)]
    x_sp = nc.dram_tensor("x_sp", [HID, TC], F32).ap()
    wq_bf = nc.dram_tensor("wq_bf", [HID, 6 * HD], BF16).ap()
    wo_bf = nc.dram_tensor("wo_bf", [HID, HID], BF16).ap()

    with contextlib.ExitStack() as top:
        E = top.enter_context
        blk = E(nc.Block())

        sD = Cnt(E(nc.semaphore("sD")))
        sDg = Cnt(E(nc.semaphore("sDg")))
        sDs = Cnt(E(nc.semaphore("sDs")))
        sT = Cnt(E(nc.semaphore("sT")))
        sV = Cnt(E(nc.semaphore("sV")))
        sSc = Cnt(E(nc.semaphore("sSc")))
        sG = Cnt(E(nc.semaphore("sG")))
        sCC = Cnt(E(nc.semaphore("sCC")))
        sOut = [Cnt(E(nc.semaphore(f"sOut{i}"))) for i in range(4)]
        sLN = [Cnt(E(nc.semaphore(f"sLN{i}"))) for i in range(4)]
        sWq = Cnt(E(nc.semaphore("sWq")))
        sWo = Cnt(E(nc.semaphore("sWo")))
        sWg = [Cnt(E(nc.semaphore(f"sWg{i}"))) for i in range(2)]
        sWd = [Cnt(E(nc.semaphore(f"sWd{i}"))) for i in range(2)]
        sXg = [Cnt(E(nc.semaphore(f"sXg{i}"))) for i in range(2)]
        sPp = [Cnt(E(nc.semaphore(f"sPp{i}"))) for i in range(2)]
        sPx = [Cnt(E(nc.semaphore(f"sPx{i}"))) for i in range(4)]
        sPz = [Cnt(E(nc.semaphore(f"sPz{i}"))) for i in range(2)]
        sXr = [Cnt(E(nc.semaphore(f"sXr{i}"))) for i in range(4)]
        sSt1 = [Cnt(E(nc.semaphore(f"sSt1{i}"))) for i in range(4)]
        sZr = [Cnt(E(nc.semaphore(f"sZr{i}"))) for i in range(4)]
        sSt2 = [Cnt(E(nc.semaphore(f"sSt2{i}"))) for i in range(4)]

        ALL_ENG = ("tensor", "vector", "scalar", "gpsimd", "sync")
        eng_cnt = {"tensor": sT, "vector": sV, "scalar": sSc, "gpsimd": sG,
                   "sync": sD}
        ALL_CNT_L = ([sD, sDg, sDs, sT, sV, sSc, sG, sCC, sWq, sWo]
                     + sOut + sWg + sWd + sXg + sPp + sPx + sPz + sXr
                     + sSt1 + sZr + sSt2 + sLN)

        def Tq(fn):
            blk.tensor(fn)

        def Vq(fn):
            blk.vector(fn)

        def Sq(fn):
            blk.scalar(fn)

        def Gq(fn):
            blk.gpsimd(fn)

        def Dq(fn):
            blk.sync(fn)


        def barrier(ex=()):
            snap = [(c, c.n) for c in ALL_CNT_L if c.n and c not in ex]
            for en in ALL_ENG:
                def wt(e, en=en, snap=snap):
                    own = eng_cnt[en]
                    for c, v in snap:
                        if c is not own:
                            e.wait_ge(c.h, v)
                getattr(blk, en)(wt)

        cc_box = [None, 0]

        ident_f = E(nc.sbuf_tensor("ident_f", [128, 128], F32))
        ident_b = E(nc.sbuf_tensor("ident_b", [128, 128], BF16))
        mask_t = E(nc.sbuf_tensor("mask_t", [128, 128], F32))
        ones_c = E(nc.sbuf_tensor("ones_c", [128, 1], BF16))
        ones_r = E(nc.sbuf_tensor("ones_r", [1, 128], BF16))
        epst = E(nc.sbuf_tensor("epst", [128, 2], F32))

        def consts(g):
            sG.tag(g.memset(ident_f[:, :], 0.0))
            g.wait_ge(sG.h, sG.n)
            sG.tag(g.affine_select(
                out=ident_f[:, :], in_=ident_f[:, :],
                compare_op=ALU.not_equal, fill=1.0, base=0,
                pattern=[[-1, 128]], channel_multiplier=1))
            sG.tag(g.memset(mask_t[:, :], 0.0))
            g.wait_ge(sG.h, sG.n)
            sG.tag(g.affine_select(
                out=mask_t[:, :], in_=mask_t[:, :],
                compare_op=ALU.is_ge, fill=NEG, base=0,
                pattern=[[1, 128]], channel_multiplier=-1))
            sG.tag(g.memset(ones_c[:, :], 1.0))
            sG.tag(g.memset(ones_r[:, :], 1.0))
            sG.tag(g.memset(epst[:, 0:1], 1e-5))
            sG.tag(g.memset(epst[:, 1:2], 4e-6))
            g.wait_ge(sG.h, sG.n)
        Gq(consts)
        g_consts = sG.n

        def cast_ident(v):
            v.wait_ge(sG.h, g_consts)
            sV.tag(v.tensor_copy(ident_b[:, :], ident_f[:, :]))
        Vq(cast_ident)
        g_after = sG.n
        v_after = sV.n

        def pre_t(t):
            t.wait_ge(sG.h, g_after)
            t.wait_ge(sV.h, v_after)
        Tq(pre_t)

        def pre_s(s):
            s.wait_ge(sG.h, g_after)
        Sq(pre_s)
        barrier()

        xr_st = contextlib.ExitStack()
        x_res = xr_st.enter_context(
            nc.sbuf_tensor("x_res", [128, 16 * TC], F32))

        def x0_load(d):
            for hc in range(16):
                sLN[hc % 4].tag(d.dma_start(
                    out=x_res[:, 1024 * hc:1024 * (hc + 1)],
                    in_=x_t[128 * hc:128 * (hc + 1), :]), 16)
        Dq(x0_load)
        x0_marks = [c.n for c in sLN]

        cst_st = contextlib.ExitStack()
        cwf = cst_st.enter_context(nc.sbuf_tensor("cwf", [128, 1024], F32))
        cwb = cst_st.enter_context(nc.sbuf_tensor("cwb", [128, 1024], BF16))

        with contextlib.ExitStack() as st2:
            E2 = st2.enter_context
            gup_s = E2(nc.sbuf_tensor("gup_s", [128, NGT * 256], I32))
            dnp_s = E2(nc.sbuf_tensor("dnp_s", [128, NDT * 704], I32))
            gus_s = E2(nc.sbuf_tensor("gus_s", [16, OGU_C], F16))
            dns_s = E2(nc.sbuf_tensor("dns_s", [44, DH_C], F16))
            gusf = E2(nc.sbuf_tensor("gusf", [16, OGU_C], F32))
            dnsf = E2(nc.sbuf_tensor("dnsf", [44, DH_C], F32))
            scT_gu = E2(nc.sbuf_tensor("scT_gu", [128, NGT * 16], F32))
            scT_d = E2(nc.sbuf_tensor("scT_d", [128, NDT * 44], F32))
            wnat = E2(nc.sbuf_tensor("wnat", [128, IFF], BF16))
            pstage = [E2(nc.sbuf_tensor(f"pstage{i}", [128, 512], BF16))
                      for i in range(2)]
            wrk = {}
            for en in range(3):
                wrk[en] = {k: E2(nc.sbuf_tensor(f"{k}_{en}", [128, 704], I32))
                           for k in ("w2", "fk2", "fb", "s2", "vs")}
                for k in ("m1", "aa", "fm", "vv"):
                    wrk[en][k] = E2(nc.sbuf_tensor(f"{k}_{en}", [128, 1408],
                                                   BF16))
            psSC = E2(nc.psum_tensor("psSC", [128, 64], F32))
            psW = [E2(nc.psum_tensor(f"psW{i}", [128, 512], F32))
                   for i in range(4)]

            cast_state = {"g": 0, "wq": 0, "wo": 0}

            def cast_job(src_ap, dst_ap, cols):
                def job(g):
                    if cast_state["g"]:
                        g.wait_ge(sG.h, cast_state["g"])
                    sXr[0].tag(g.dma_start(out=cwf[:, :cols], in_=src_ap),
                               16)
                    g.wait_ge(sXr[0].h, sXr[0].n)
                    if cast_state["wo"]:
                        g.wait_ge(sXr[1].h, cast_state["wo"])
                    sG.tag(g.tensor_copy(cwb[:, :cols], cwf[:, :cols]))
                    cast_state["g"] = sG.n
                    g.wait_ge(sG.h, cast_state["g"])
                    sXr[1].tag(g.dma_start(out=dst_ap, in_=cwb[:, :cols]),
                               16)
                    cast_state["wo"] = sXr[1].n
                return job

            filler = []
            for hc in range(16):
                filler.append(cast_job(wqkv[128 * hc:128 * (hc + 1), :],
                                       wq_bf[128 * hc:128 * (hc + 1), :],
                                       6 * HD))
            for hh in range(16):
                for cl in range(2):
                    filler.append(cast_job(
                        wout[128 * hh:128 * (hh + 1),
                             1024 * cl:1024 * (cl + 1)],
                        wo_bf[128 * hh:128 * (hh + 1),
                              1024 * cl:1024 * (cl + 1)], 1024))
            wq_cast_total = 16 * 16
            all_cast_total = 16 * len(filler)

            def fill_one():
                if filler:
                    Gq(filler.pop(0))

            def p2_in(d):
                for ot in range(NGT):
                    sD.tag(d.dma_start(out=gup_s[:, 256 * ot:256 * (ot + 1)],
                                       in_=gup[128 * ot:128 * (ot + 1), :]),
                           16)
                for ht in range(NDT):
                    sD.tag(d.dma_start(out=dnp_s[:, 704 * ht:704 * (ht + 1)],
                                       in_=dnp[128 * ht:128 * (ht + 1), :]),
                           16)
                sD.tag(d.dma_start(out=gus_s[:, :], in_=gus[:, :]), 16)
                sD.tag(d.dma_start(out=dns_s[:, :], in_=dns[:, :]), 16)
            Dq(p2_in)
            d_in = sD.n

            def p2_scv(v):
                v.wait_ge(sD.h, d_in)
                sV.tag(v.tensor_scalar(out=gusf[:, :], in0=gus_s[:, :],
                                       scalar1=256.0, scalar2=None,
                                       op0=ALU.mult))
                sV.tag(v.tensor_scalar(out=dnsf[:, :], in0=dns_s[:, :],
                                       scalar1=256.0, scalar2=None,
                                       op0=ALU.mult))
            Vq(p2_scv)
            v_scf = sV.n

            sct_jobs = ([("gu", ot) for ot in range(NGT)] +
                        [("d", ht) for ht in range(NDT)])
            sct_state = {"v": 0}
            for kind, idx in sct_jobs:
                vprev = sct_state["v"]

                def tmm(t, kind=kind, idx=idx, vprev=vprev):
                    t.wait_ge(sV.h, max(v_scf, vprev))
                    if kind == "gu":
                        sT.tag(t.matmul(psSC[:, :16],
                                        gusf[:, 128 * idx:128 * (idx + 1)],
                                        ident_f[:16, :16],
                                        start=True, stop=True))
                    else:
                        sT.tag(t.matmul(psSC[:, :44],
                                        dnsf[:, 128 * idx:128 * (idx + 1)],
                                        ident_f[:44, :44],
                                        start=True, stop=True))
                Tq(tmm)
                tmark = sT.n

                def vcp(v, kind=kind, idx=idx, tmark=tmark):
                    v.wait_ge(sT.h, tmark)
                    if kind == "gu":
                        sV.tag(v.tensor_copy(
                            scT_gu[:, 16 * idx:16 * (idx + 1)], psSC[:, :16]))
                    else:
                        sV.tag(v.tensor_copy(
                            scT_d[:, 44 * idx:44 * (idx + 1)], psSC[:, :44]))
                Vq(vcp)
                sct_state["v"] = sV.n

            def dq_chain(e, wtile, W, cnt):
                last = [0, 0, 0]

                def step(c, instr):
                    cnt.tag(instr)
                    last[c] = cnt.n

                def wt(c):
                    if last[c]:
                        e.wait_ge(cnt.h, last[c])

                def ops_for(jp):
                    c = jp % 3
                    w = wrk[c]
                    o = []
                    if jp == 0:
                        o.append(lambda: (wt(c), step(c, e.tensor_scalar(
                            out=w["w2"][:, :W], in0=wtile, scalar1=0x70007,
                            scalar2=None, op0=ALU.bitwise_and))))
                    else:
                        o.append(lambda: (wt(c), step(c, e.tensor_scalar(
                            out=w["w2"][:, :W], in0=wtile, scalar1=4 * jp,
                            scalar2=0x70007,
                            op0=ALU.logical_shift_right,
                            op1=ALU.bitwise_and))))
                    o.append(lambda: (wt(c), step(c, e.tensor_scalar(
                        out=w["fk2"][:, :W], in0=w["w2"][:, :W],
                        scalar1=0x43004300, scalar2=None,
                        op0=ALU.bitwise_or))))
                    o.append(lambda: (wt(c), step(c, e.tensor_scalar(
                        out=w["fb"][:, :W], in0=w["w2"][:, :W],
                        scalar1=64, scalar2=0x3B003B00,
                        op0=ALU.mult, op1=ALU.add))))
                    o.append(lambda: (wt(c), step(c, e.tensor_scalar(
                        out=w["s2"][:, :W], in0=wtile, scalar1=12 - 4 * jp,
                        scalar2=-0x7FFF8000,
                        op0=ALU.logical_shift_left,
                        op1=ALU.bitwise_and))))
                    fkb = w["fk2"][:, :W].bitcast(BF16)
                    o.append(lambda: (wt(c), step(c, e.tensor_scalar(
                        out=w["m1"][:, :2 * W], in0=fkb, scalar1=130.0,
                        scalar2=129.0, op0=ALU.min, op1=ALU.subtract))))
                    o.append(lambda: (wt(c), step(c, e.tensor_scalar(
                        out=w["aa"][:, :2 * W], in0=fkb,
                        scalar1=0.001953125, scalar2=0.25,
                        op0=ALU.mult, op1=ALU.subtract))))
                    o.append(lambda: (wt(c), step(c, e.tensor_tensor(
                        out=w["fm"][:, :2 * W],
                        in0=w["fb"][:, :W].bitcast(BF16),
                        in1=w["m1"][:, :2 * W], op=ALU.mult))))
                    o.append(lambda: (wt(c), step(c, e.tensor_tensor(
                        out=w["vv"][:, :2 * W], in0=w["fm"][:, :2 * W],
                        in1=w["aa"][:, :2 * W], op=ALU.max))))
                    o.append(lambda: (wt(c), step(c, e.tensor_tensor(
                        out=w["vs"][:, :W],
                        in0=w["vv"][:, :2 * W].bitcast(I32),
                        in1=w["s2"][:, :W], op=ALU.bitwise_or))))
                    vb = w["vs"][:, :W].bitcast(BF16)
                    o.append(lambda: (wt(c), step(c, e.tensor_copy(
                        wnat[:, jp:8 * W:8], vb[:, 0:2 * W:2]))))
                    o.append(lambda: (wt(c), step(c, e.tensor_copy(
                        wnat[:, jp + 4:8 * W:8], vb[:, 1:2 * W:2]))))
                    return o
                for grp in ((0, 1, 2), (3,)):
                    chains = [ops_for(jp) for jp in grp]
                    for s in range(max(len(ch) for ch in chains)):
                        for ch in chains:
                            if s < len(ch):
                                ch[s]()

            wt_state = {}

            def do_tile(wtile, W, scT, sc_off, ngrp, piece_ap, pcol,
                        tr_prev_v):
                def dv(v, wtile=wtile, W=W, tr_prev_v=tr_prev_v):
                    v.wait_ge(sT.h, tr_prev_v)
                    dq_chain(v, wtile, W, sV)
                Vq(dv)

                def sc(v, scT=scT, sc_off=sc_off, ngrp=ngrp):
                    v.wait_ge(sV.h, sV.n)
                    for g in range(ngrp):
                        sV.tag(v.tensor_scalar(
                            out=wnat[:, 128 * g:128 * (g + 1)],
                            in0=wnat[:, 128 * g:128 * (g + 1)],
                            scalar1=scT[:, sc_off + g:sc_off + g + 1],
                            scalar2=None, op0=ALU.mult))
                        v.wait_ge(sV.h, sV.n)
                Vq(sc)
                scdone = sV.n

                for g0 in range(0, ngrp, 4):
                    par = (g0 // 4) % 2
                    vprev = wt_state.get("evac", 0)

                    def tmm(t, g0=g0, vprev=vprev, scdone=scdone):
                        t.wait_ge(sV.h, max(vprev, scdone))
                        for k in range(4):
                            g = g0 + k
                            sT.tag(t.matmul(psW[k][:, :128],
                                            wnat[:, 128 * g:128 * (g + 1)],
                                            ident_b[:, :],
                                            start=True, stop=True))
                    Tq(tmm)
                    tmark = sT.n
                    dprev = wt_state.get(("dma", par), 0)

                    def vev(v, par=par, tmark=tmark, dprev=dprev):
                        v.wait_ge(sT.h, tmark)
                        if dprev:
                            v.wait_ge(sPp[par].h, dprev)
                        for k in range(4):
                            sV.tag(v.tensor_copy(
                                pstage[par][:, 128 * k:128 * (k + 1)],
                                psW[k][:, :128]))
                    Vq(vev)
                    wt_state["evac"] = sV.n
                    wt_state[("evac", par)] = sV.n
                    vmark = sV.n

                    def dst(d, g0=g0, par=par, vmark=vmark,
                            piece_ap=piece_ap, pcol=pcol):
                        d.wait_ge(sV.h, vmark)
                        sPp[par].tag(d.dma_start(
                            out=piece_ap[:, pcol + 128 * g0:
                                         pcol + 128 * g0 + 512],
                            in_=pstage[par][:, :]), 16)
                    Dq(dst)
                    wt_state[("dma", par)] = sPp[par].n
                return sT.n

            cc_gu_parts = []
            tr_prev = 0
            for ot in range(NGT):
                part = 0 if ot < 4 else (1 if ot < 8 else 2)
                lo_p = GU_PARTS[part][0]
                tr_prev = do_tile(gup_s[:, 256 * ot:256 * (ot + 1)], 256,
                                  scT_gu, 16 * ot, 16, wgu_piece[part],
                                  2048 * (ot - lo_p), tr_prev)
                if ot + 1 in (4, 8):
                    pi = 0 if ot + 1 == 4 else 1
                    pp = [sPp[0].n, sPp[1].n]

                    def p2_ccp(g, pi=pi, pp=pp):
                        g.wait_ge(sPp[0].h, pp[0])
                        g.wait_ge(sPp[1].h, pp[1])
                        sCC.tag(g.collective_compute(
                            "AllGather", ALU.bypass,
                            replica_groups=[list(range(8))],
                            ins=[wgu_piece[pi]], outs=[g_gu[pi]]), 1)
                    Gq(p2_ccp)
                    cc_gu_parts.append((ot + 1, sCC.n))
            gu_pieces = [sPp[0].n, sPp[1].n]

            def p2_cc2(g):
                g.wait_ge(sPp[0].h, gu_pieces[0])
                g.wait_ge(sPp[1].h, gu_pieces[1])
                sCC.tag(g.collective_compute(
                    "AllGather", ALU.bypass, replica_groups=[list(range(8))],
                    ins=[wgu_piece[2]], outs=[g_gu[2]]), 1)
            Gq(p2_cc2)
            cc_gu_parts.append((11, sCC.n))
            for ht in range(NDT):
                tr_prev = do_tile(dnp_s[:, 704 * ht:704 * (ht + 1)], 704,
                                  scT_d, 44 * ht, 44, wd_piece, 5632 * ht,
                                  tr_prev)
            dma_pieces = [sPp[0].n, sPp[1].n]

            def p2_cc(g):
                g.wait_ge(sPp[0].h, dma_pieces[0])
                g.wait_ge(sPp[1].h, dma_pieces[1])
                sCC.tag(g.collective_compute(
                    "AllGather", ALU.bypass, replica_groups=[list(range(8))],
                    ins=[wd_piece], outs=[g_d]), 1)
            Gq(p2_cc)
            cc_box[0] = cc_gu_parts
            cc_box[1] = sCC.n
        barrier(ex=(sCC,))
        def ln_pass(tb, xres, psX, psX2, psM, psR, deps, xh=None,
                    piece_ap=None, sg_T=0, pg_V=0, bg_V=0, ag_T=0):
            res = {}
            with contextlib.ExitStack() as stL:
                EL = stL.enter_context

                def xsl(hc):
                    return xres[:, 1024 * hc + 512 * tb:
                                1024 * hc + 512 * (tb + 1)]
                xb = [EL(nc.sbuf_tensor(uid(f"ln_xb{i}"), [128, 512], BF16))
                      for i in range(4)]
                sq = [EL(nc.sbuf_tensor(uid(f"ln_sq{i}"), [128, 512], BF16))
                      for i in range(4)]
                rw = [EL(nc.sbuf_tensor(uid(f"ln_rw{i}"), [1, 512], F32))
                      for i in range(4)]
                rwb = [EL(nc.sbuf_tensor(uid(f"ln_rwb{i}"), [1, 512], BF16))
                       for i in range(2)]
                tmpf = [EL(nc.sbuf_tensor(uid(f"ln_tmp{i}"), [128, 512], F32))
                        for i in range(2)]
                xst = ([EL(nc.sbuf_tensor(uid(f"ln_xst{i}"), [128, 512],
                                          BF16)) for i in range(2)]
                       if piece_ap is not None else None)
                stats_t = {}
                pc_state = {0: sPx[0].n, 1: sPx[1].n}
                first = {"v": True, "s": True, "t": True}
                for g4 in range(4):
                    hcs = list(range(4 * g4, 4 * g4 + 4))
                    tprev = stats_t.get(g4 - 1, 0)

                    def cst(v, hcs=hcs, tprev=tprev):
                        if first["v"]:
                            first["v"] = False
                            for cnt_, val_ in deps:
                                if val_:
                                    v.wait_ge(cnt_.h, val_)
                            if sg_T:
                                v.wait_ge(sT.h, sg_T)
                        if tprev:
                            v.wait_ge(sT.h, tprev)
                        for hc in hcs:
                            sV.tag(v.tensor_copy(xb[hc % 4][:, :], xsl(hc)))
                    Vq(cst)
                    vmark = sV.n

                    def sqr(s, hcs=hcs, tprev=tprev):
                        if first["s"]:
                            first["s"] = False
                            for cnt_, val_ in deps:
                                if val_:
                                    s.wait_ge(cnt_.h, val_)
                            if sg_T:
                                s.wait_ge(sT.h, sg_T)
                        if tprev:
                            s.wait_ge(sT.h, tprev)
                        for hc in hcs:
                            sSc.tag(s.activation(out=sq[hc % 4][:, :],
                                                 in_=xsl(hc),
                                                 func=AF.Square))
                    Sq(sqr)
                    smark = sSc.n

                    def smm(t, hcs=hcs, vmark=vmark, smark=smark):
                        if first["t"]:
                            first["t"] = False
                            if pg_V:
                                t.wait_ge(sV.h, pg_V)
                        t.wait_ge(sV.h, vmark)
                        t.wait_ge(sSc.h, smark)
                        for hc in hcs:
                            sT.tag(t.matmul(psX, ones_c[:, :],
                                            xb[hc % 4][:, :],
                                            start=(hc == 0),
                                            stop=(hc == 15)))
                            sT.tag(t.matmul(psX2, ones_c[:, :],
                                            sq[hc % 4][:, :],
                                            start=(hc == 0),
                                            stop=(hc == 15)))
                    Tq(smm)
                    stats_t[g4] = sT.n
                res["sq"] = sSc.n
                res["statsT"] = sT.n
                tstats = sT.n

                def cmb_v(v, tstats=tstats):
                    v.wait_ge(sT.h, tstats)
                    sV.tag(v.tensor_scalar(out=rw[0][:, :], in0=psX,
                                           scalar1=1.0 / 2048,
                                           scalar2=None, op0=ALU.mult))
                    sV.tag(v.tensor_scalar(out=rw[1][:, :], in0=psX2,
                                           scalar1=1.0 / 2048,
                                           scalar2=None, op0=ALU.mult))
                    v.wait_ge(sV.h, sV.n)
                    sV.tag(v.tensor_tensor(out=rw[2][:, :], in0=rw[0][:, :],
                                           in1=rw[0][:, :], op=ALU.mult))
                    v.wait_ge(sV.h, sV.n)
                    sV.tag(v.tensor_tensor(out=rw[2][:, :], in0=rw[1][:, :],
                                           in1=rw[2][:, :],
                                           op=ALU.subtract))
                Vq(cmb_v)
                vvar = sV.n

                def cmb_s(s, vvar=vvar):
                    s.wait_ge(sV.h, vvar)
                    sSc.tag(s.activation(out=rw[3][:, :], in_=rw[2][:, :],
                                         func=AF.Sqrt,
                                         bias=epst[:1, 0:1], scale=1.0))
                Sq(cmb_s)
                ssqrt = sSc.n

                def cmb_v2(v, ssqrt=ssqrt):
                    v.wait_ge(sSc.h, ssqrt)
                    sV.tag(v.reciprocal(rw[3][:, :], rw[3][:, :]))
                    v.wait_ge(sV.h, sV.n)
                    sV.tag(v.tensor_copy(rwb[0][:, :], rw[3][:, :]))
                    sV.tag(v.scalar_tensor_tensor(out=rw[0][:, :],
                                                  in0=rw[0][:, :],
                                                  scalar=-1.0,
                                                  in1=rw[3][:, :],
                                                  op0=ALU.mult,
                                                  op1=ALU.mult))
                    v.wait_ge(sV.h, sV.n)
                    sV.tag(v.tensor_copy(rwb[1][:, :], rw[0][:, :]))
                Vq(cmb_v2)
                vrows = sV.n
                res["cmb"] = vrows

                def bcast(t, vrows=vrows):
                    if bg_V:
                        t.wait_ge(sV.h, bg_V)
                    t.wait_ge(sV.h, vrows)
                    sT.tag(t.matmul(psR, ones_r[:, :], rwb[0][:, :],
                                    start=True, stop=True))
                    sT.tag(t.matmul(psM, ones_r[:, :], rwb[1][:, :],
                                    start=True, stop=True))
                Tq(bcast)
                tbc = sT.n

                firstA = [True]
                for hc in range(16):
                    par = hc % 2

                    def app(v, hc=hc, par=par, tbc=tbc,
                            pprev=pc_state.get(par, 0)):
                        if firstA[0]:
                            firstA[0] = False
                            if ag_T:
                                v.wait_ge(sT.h, ag_T)
                        v.wait_ge(sT.h, tbc)
                        if piece_ap is not None and pprev:
                            v.wait_ge(sPx[par].h, pprev)
                        sV.tag(v.tensor_tensor(
                            out=tmpf[par][:, :], in0=xsl(hc),
                            in1=psR, op=ALU.mult))
                        v.wait_ge(sV.h, sV.n)
                        if xh is not None:
                            dst = xh[:, 512 * hc:512 * (hc + 1)]
                        else:
                            dst = xst[par][:, :]
                        sV.tag(v.tensor_tensor(out=dst, in0=tmpf[par][:, :],
                                               in1=psM, op=ALU.add))
                    Vq(app)
                    vap = sV.n
                    if piece_ap is not None:
                        def pdma(d, hc=hc, par=par, vap=vap):
                            d.wait_ge(sV.h, vap)
                            sPx[par].tag(d.dma_start(
                                out=piece_ap[128 * hc:128 * (hc + 1), :],
                                in_=xst[par][:, :]), 16)
                        Sq(pdma)
                        pc_state[par] = sPx[par].n
                res["done"] = sV.n
            return res

        def ffn_phase(xres, out_ap, deps0):
            with contextlib.ExitStack() as stFF:
                EFF = stFF.enter_context
                xh = EFF(nc.sbuf_tensor(uid("xh"), [128, 16 * 512], BF16))
                h_all = EFF(nc.sbuf_tensor(uid("h"), [128, 44 * 512], BF16))
                wg = [EFF(nc.sbuf_tensor(uid(f"wg{i}"), [128, HID], BF16))
                      for i in range(4)]
                sg = [EFF(nc.sbuf_tensor(uid(f"sg{i}"), [128, 512], BF16))
                      for i in range(2)]
                wd = [EFF(nc.sbuf_tensor(uid(f"wdn{i}"), [128, IFF], BF16))
                      for i in range(2)]
                psGU = [EFF(nc.psum_tensor(uid(f"psGU{i}"), [128, 512], F32))
                        for i in range(4)]
                psD_ = [EFF(nc.psum_tensor(uid(f"psD{i}"), [128, 512], F32))
                        for i in range(2)]
                psSR = EFF(nc.psum_tensor(uid("psSR"), [128, 512], F32))

                GU_ORDER = ([i for i in range(44) if i % NGT < 4]
                            + [i for i in range(44) if 4 <= i % NGT < 8]
                            + [i for i in range(44) if i % NGT >= 8])
                wdma, mmt, ept = {}, {}, {}
                ddma, dmmt, dept = {}, {}, {}
                ln_res, h_done, state = {}, {}, {}

                def wdma_it(d, pos):
                    par = pos % 2
                    if pos >= 2:
                        d.wait_ge(sT.h, mmt[pos - 2])
                    itp = GU_ORDER[pos % 44]
                    need = 0
                    for half in range(2):
                        j = itp + 44 * half
                        ol = j % NGT
                        for hi, mark in cc_box[0]:
                            if ol < hi:
                                need = max(need, mark)
                                break
                    d.wait_ge(sCC.h, need)
                    for half in range(2):
                        j = itp + 44 * half
                        r, ol = j // NGT, j % NGT
                        part = 0 if ol < 4 else (1 if ol < 8 else 2)
                        olp = ol - (0, 4, 8)[part]
                        sWg[par].tag(d.dma_start(
                            out=wg[2 * par + half][:, :],
                            in_=g_gu[part][128 * r:128 * (r + 1),
                                           2048 * olp:2048 * (olp + 1)]), 16)
                    wdma[pos] = sWg[par].n

                def ddma_m(d, m):
                    par = m % 2
                    if m < 2:
                        d.wait_ge(sCC.h, cc_box[1])
                    if m >= 2:
                        d.wait_ge(sT.h, dmmt[m - 2])
                    mp = m % 16
                    r, hl = mp // 2, mp % 2
                    sWd[par].tag(d.dma_start(
                        out=wd[par][:, :],
                        in_=g_d[128 * r:128 * (r + 1),
                                5632 * hl:5632 * (hl + 1)]), 16)
                    ddma[m] = sWd[par].n

                def wq_emit(it):
                    (Dq if it % 2 == 0 else Gq)(
                        lambda d, it=it: wdma_it(d, it))

                def dq_emit(m):
                    (Dq if m % 2 == 0 else Gq)(
                        lambda d, m=m: ddma_m(d, m))

                def gu_pass(p):
                    ln_done = ln_res[p]["done"]
                    h_guard = state.get("down_T", 0)
                    if p == 0:
                        wq_emit(0)
                        wq_emit(1)
                    for posp in range(44):
                        pos = 44 * p + posp
                        itp = GU_ORDER[posp]
                        par = pos % 2

                        def mm(t, pos=pos, posp=posp, par=par,
                               ln_done=ln_done):
                            t.wait_ge(sWg[par].h, wdma[pos])
                            if posp < 2:
                                t.wait_ge(sV.h, ln_done)
                            if pos >= 2:
                                t.wait_ge(sV.h, ept[pos - 2])
                            for hc in range(16):
                                for half in range(2):
                                    i_ = t.matmul(
                                        psGU[2 * par + half][:, :],
                                        wg[2 * par + half][
                                            :, 128 * hc:128 * (hc + 1)],
                                        xh[:, 512 * hc:512 * (hc + 1)],
                                        start=(hc == 0), stop=(hc == 15))
                                    if hc == 15:
                                        sT.tag(i_)
                            mmt[pos] = sT.n
                        Tq(mm)
                        nxt = pos + 2
                        if (p == 0 and nxt < 46) or (p == 1 and
                                                     46 <= nxt < 88):
                            wq_emit(nxt)
                        if pos % 2 == 1:
                            fill_one()
                        if p == 1 and posp == 10:
                            dq_emit(16)
                            dq_emit(17)

                        def ep_s(s, pos=pos, par=par):
                            s.wait_ge(sT.h, mmt[pos])
                            sSc.tag(s.activation(out=sg[par][:, :],
                                                 in_=psGU[2 * par][:, :],
                                                 func=AF.Silu))
                        Sq(ep_s)
                        smark = sSc.n

                        def ep_v(v, pos=pos, posp=posp, itp=itp, par=par,
                                 smark=smark, h_guard=h_guard):
                            v.wait_ge(sSc.h, smark)
                            if posp == 0 and h_guard:
                                v.wait_ge(sT.h, h_guard)
                            sV.tag(v.tensor_tensor(
                                out=h_all[:, 512 * itp:512 * (itp + 1)],
                                in0=sg[par][:, :],
                                in1=psGU[2 * par + 1][:, :], op=ALU.mult))
                        Vq(ep_v)
                        ept[pos] = sV.n
                    h_done[p] = sV.n

                def down_pass(p):
                    m0 = 16 * p
                    if p == 0:
                        dq_emit(0)
                        dq_emit(1)
                    stq = {}
                    for mp in range(16):
                        m = m0 + mp
                        par = m % 2

                        def mm(t, m=m, mp=mp, par=par, p=p):
                            t.wait_ge(sWd[par].h, ddma[m])
                            if mp == 0:
                                t.wait_ge(sV.h, h_done[p])
                                if p == 0:
                                    t.wait_ge(sV.h, ln_res[1]["cmb"])
                            if m >= 2:
                                t.wait_ge(sV.h, dept[m - 2][0])
                                t.wait_ge(sSc.h, dept[m - 2][1])
                            for ic in range(44):
                                i_ = t.matmul(
                                    psD_[par][:, :],
                                    wd[par][:, 128 * ic:128 * (ic + 1)],
                                    h_all[:, 512 * ic:512 * (ic + 1)],
                                    start=(ic == 0), stop=(ic == 43))
                                if ic == 43:
                                    sT.tag(i_)
                            dmmt[m] = sT.n
                        Tq(mm)
                        nxt = m + 2
                        if (p == 0 and nxt < 16) or (p == 1 and
                                                     18 <= nxt < 32):
                            dq_emit(nxt)

                        def ep_s(s, m=m, par=par):
                            s.wait_ge(sT.h, dmmt[m])
                            sSc.tag(s.activation(out=sqd[par][:, :],
                                                 in_=psD_[par][:, :],
                                                 func=AF.Square))
                        Sq(ep_s)
                        smark = sSc.n

                        def ep_v(v, m=m, mp=mp, par=par,
                                 rg=state.get("radd_G", 0) if p else 0):
                            v.wait_ge(sT.h, dmmt[m])
                            if mp == 0 and rg:
                                v.wait_ge(sG.h, rg)
                            sV.tag(v.tensor_copy(
                                y_all[:, 512 * mp:512 * (mp + 1)],
                                psD_[par][:, :]))
                        Vq(ep_v)
                        dept[m] = (sV.n, smark)
                        stq[mp] = (par, smark)

                        if mp >= 1:
                            def st_mm(t, mp=mp, stq=stq):
                                par_, smark_ = stq[mp - 1]
                                t.wait_ge(sSc.h, smark_)
                                if mp - 1 == 0:
                                    rg = state.get("rr_ev", 0)
                                    if rg:
                                        t.wait_ge(sV.h, rg)
                                sT.tag(t.matmul(psSR[0:1, :], ones_c[:, :],
                                                sqd[par_][:, :],
                                                start=(mp - 1 == 0),
                                                stop=False))
                            Tq(st_mm)

                    def st_last(t, stq=stq):
                        par_, smark_ = stq[15]
                        t.wait_ge(sSc.h, smark_)
                        sT.tag(t.matmul(psSR[0:1, :], ones_c[:, :],
                                        sqd[par_][:, :],
                                        start=False, stop=True))
                    Tq(st_last)
                    tstats = sT.n
                    state["down_T"] = dmmt[m0 + 15]

                    def rms_s(s, tstats=tstats):
                        s.wait_ge(sT.h, tstats)
                        sSc.tag(s.activation(out=rrow[:, :],
                                             in_=psSR[0:1, :],
                                             func=AF.Sqrt,
                                             bias=epst[:1, 1:2],
                                             scale=4.0 / 2048))
                    Sq(rms_s)
                    srms = sSc.n

                    def rms_v(v, srms=srms):
                        v.wait_ge(sSc.h, srms)
                        sV.tag(v.reciprocal(rrow[:, :], rrow[:, :]))
                        v.wait_ge(sV.h, sV.n)
                        sV.tag(v.tensor_copy(rrowb[:, :], rrow[:, :]))
                    Vq(rms_v)
                    vrms = sV.n

                    def rms_t(t, vrms=vrms):
                        t.wait_ge(sV.h, vrms)
                        sT.tag(t.matmul(psSR[:, :], ones_r[:, :],
                                        rrowb[:, :], start=True, stop=True))
                    Tq(rms_t)
                    trr = sT.n

                    def rr_ev(v, trr=trr):
                        v.wait_ge(sT.h, trr)
                        sV.tag(v.tensor_copy(rrb_s[:, :], psSR[:, :]))
                    Vq(rr_ev)
                    state["rr_ev"] = sV.n

                def radd_pass(p):
                    vrr2 = state["rr_ev"]
                    lsq = ln_res[p]["sq"]
                    firste = {"v": True, "g": True}
                    for hc in range(16):
                        use_g = (hc % 2 == 1)
                        ecnt = sG if use_g else sV
                        tp = tpfG if use_g else tpfV

                        def radd(e, hc=hc, p=p, use_g=use_g, ecnt=ecnt,
                                 tp=tp, vrr2=vrr2, lsq=lsq):
                            kk = "g" if use_g else "v"
                            if firste[kk]:
                                firste[kk] = False
                                e.wait_ge(sSc.h, lsq)
                                if use_g:
                                    e.wait_ge(sV.h, vrr2)
                            else:
                                e.wait_ge(ecnt.h, ecnt.n)
                            ecnt.tag(e.tensor_tensor(
                                out=tp[:, :],
                                in0=y_all[:, 512 * hc:512 * (hc + 1)],
                                in1=rrb_s[:, :], op=ALU.mult))
                            e.wait_ge(ecnt.h, ecnt.n)
                            ecnt.tag(e.tensor_tensor(
                                out=xres[:, 1024 * hc + 512 * p:
                                         1024 * hc + 512 * (p + 1)],
                                in0=tp[:, :],
                                in1=xres[:, 1024 * hc + 512 * p:
                                         1024 * hc + 512 * (p + 1)],
                                op=ALU.add))
                        (Gq if use_g else Vq)(radd)
                        emark = ecnt.n
                        if out_ap is not None:
                            def xs(d, hc=hc, p=p, emark=emark, ecnt=ecnt):
                                d.wait_ge(ecnt.h, emark)
                                sOut[hc % 4].tag(d.dma_start(
                                    out=out_ap[128 * hc:128 * (hc + 1),
                                               512 * p:512 * (p + 1)],
                                    in_=xres[:, 1024 * hc + 512 * p:
                                             1024 * hc + 512 * (p + 1)]),
                                    16)
                            Sq(xs)
                    state["radd_G"] = sG.n

                ln_res[0] = ln_pass(0, xres, psD_[0][0:1, :],
                                    psD_[1][0:1, :], psGU[2][:, :],
                                    psGU[3][:, :], deps0, xh=xh)
                gu_pass(0)
                ln_res[1] = ln_pass(1, xres, psD_[0][0:1, :],
                                    psD_[1][0:1, :], psGU[2][:, :],
                                    psGU[3][:, :], deps0, xh=xh,
                                    sg_T=ln_res[0]["statsT"],
                                    pg_V=ln_res[0]["cmb"],
                                    bg_V=ept[43], ag_T=mmt[43])
                with contextlib.ExitStack() as stY:
                    EY = stY.enter_context
                    y_all = EY(nc.sbuf_tensor(uid("y"), [128, 16 * 512],
                                              BF16))
                    sqd = [EY(nc.sbuf_tensor(uid(f"sqd{i}"), [128, 512],
                                             BF16)) for i in range(2)]
                    rrow = EY(nc.sbuf_tensor(uid("rrow"), [1, 512], F32))
                    rrowb = EY(nc.sbuf_tensor(uid("rrowb"), [1, 512], BF16))
                    rrb_s = EY(nc.sbuf_tensor(uid("rrb"), [128, 512], F32))
                    tpfV = EY(nc.sbuf_tensor(uid("tpfV"), [128, 512], F32))
                    tpfG = EY(nc.sbuf_tensor(uid("tpfG"), [128, 512], F32))
                    down_pass(0)
                    radd_pass(0)
                    gu_pass(1)
                    down_pass(1)
                    radd_pass(1)
                return sV.n, sG.n

        res1_v, res1_g = ffn_phase(
            x_res, None, [(sLN[i], x0_marks[i]) for i in range(4)])
        while filler:
            fill_one()
        cst_st.close()
        barrier()

        def spill(d):
            for hc in range(16):
                sPz[1].tag(d.dma_start(
                    out=x_sp[128 * hc:128 * (hc + 1), :],
                    in_=x_res[:, 1024 * hc:1024 * (hc + 1)]), 16)
        Dq(spill)

        cc_x = [0, 0]
        with contextlib.ExitStack() as stL2:
            EL2 = stL2.enter_context
            psX_ = EL2(nc.psum_tensor("ps2X", [1, 512], F32))
            psX2_ = EL2(nc.psum_tensor("ps2X2", [1, 512], F32))
            psM_ = EL2(nc.psum_tensor("ps2M", [128, 512], F32))
            psR_ = EL2(nc.psum_tensor("ps2R", [128, 512], F32))
            l20 = ln_pass(0, x_res, psX_[:, :], psX2_[:, :], psM_[:, :],
                          psR_[:, :], [], piece_ap=xh_p[0])
            dp0 = [sPx[0].n, sPx[1].n]

            def p6_cc0(g):
                for i_ in range(2):
                    g.wait_ge(sPx[i_].h, dp0[i_])
                sCC.tag(g.collective_compute(
                    "AllGather", ALU.bypass,
                    replica_groups=[list(range(8))],
                    ins=[xh_p[0]], outs=[g_xs[0]]), 1)
            Gq(p6_cc0)
            cc_x[0] = sCC.n
            l21 = ln_pass(1, x_res, psX_[:, :], psX2_[:, :], psM_[:, :],
                          psR_[:, :], [], piece_ap=xh_p[1],
                          sg_T=l20["statsT"], pg_V=l20["cmb"],
                          bg_V=l20["done"])
        dp1 = [sPx[0].n, sPx[1].n]

        def p6_cc1(g):
            for i_ in range(2):
                g.wait_ge(sPx[i_].h, dp1[i_])
            sCC.tag(g.collective_compute(
                "AllGather", ALU.bypass, replica_groups=[list(range(8))],
                ins=[xh_p[1]], outs=[g_xs[1]]), 1)
        Gq(p6_cc1)
        cc_x[1] = sCC.n
        barrier(ex=(sCC,))
        xr_st.close()

        cc_a2a = [0, 0]
        with contextlib.ExitStack() as stO:
            EO = stO.enter_context
            o_tm = [EO(nc.sbuf_tensor(f"otm{i}", [128, 4096], BF16))
                    for i in range(4)]
            with contextlib.ExitStack() as stAtt:
                EA = stAtt.enter_context
                qt = [EA(nc.sbuf_tensor(f"qt{i}", [128, T], BF16))
                      for i in range(2)]
                kt = [EA(nc.sbuf_tensor(f"kt{i}", [128, T], BF16))
                      for i in range(2)]
                vtm = [EA(nc.sbuf_tensor(f"vtm{i}", [128, T], BF16))
                       for i in range(2)]
                bias_u = [EA(nc.sbuf_tensor(f"biasu{i}", [128, 1], F32))
                          for i in range(4)]

                with contextlib.ExitStack() as stC1:
                    EC = stC1.enter_context
                    wqb = EC(nc.sbuf_tensor("wqb", [128, 16 * 6 * HD], BF16))
                    xg = [EC(nc.sbuf_tensor(f"xg{i}", [128, 16 * 512], BF16))
                          for i in range(2)]
                    vstg = EC(nc.sbuf_tensor("vstg", [128, 1024], BF16))
                    psQ = [EC(nc.psum_tensor(f"psQ{i}", [128, 512], F32))
                           for i in range(2)]
                    psVT = [EC(nc.psum_tensor(f"psVT{i}", [128, 512], F32))
                            for i in range(4)]

                    def wql(d):
                        d.wait_ge(sXr[1].h, wq_cast_total)
                        for hc in range(16):
                            sWq.tag(d.dma_start(
                                out=wqb[:, 6 * HD * hc:6 * HD * (hc + 1)],
                                in_=wq_bf[128 * hc:128 * (hc + 1), :]), 16)
                    Dq(wql)
                    wq_marks = sWq.n

                    qkv_mm = {}
                    ev_state = {}
                    xg_dma = {}
                    XG_ORDER = [2 * r + tb for tb in range(2)
                                for r in range(8)]

                    def xg_tt(d, pos):
                        par = pos % 2
                        if pos >= 2:
                            d.wait_ge(sT.h, qkv_mm[pos - 2])
                        tt = XG_ORDER[pos]
                        r, tb = tt // 2, tt % 2
                        d.wait_ge(sCC.h, cc_x[tb])
                        for hc in range(16):
                            sXg[par].tag(d.dma_start(
                                out=xg[par][:, 512 * hc:512 * (hc + 1)],
                                in_=g_xs[tb][2048 * r + 128 * hc:
                                             2048 * r + 128 * (hc + 1),
                                             :]), 16)
                        xg_dma[pos] = sXg[par].n

                    def xg_pre(d):
                        xg_tt(d, 0)
                        xg_tt(d, 1)
                    Dq(xg_pre)

                    for pos in range(16):
                        tt = XG_ORDER[pos]
                        par = pos % 2
                        for o in range(6):
                            opar = o % 2
                            eprev = ev_state.get(opar, 0)

                            def mm(t, tt=tt, par=par, o=o, opar=opar,
                                   eprev=eprev, wq_marks=wq_marks,
                                   pos=pos):
                                t.wait_ge(sXg[par].h, xg_dma[pos])
                                t.wait_ge(sWq.h, wq_marks)
                                if eprev:
                                    t.wait_ge(sV.h, eprev)
                                for hc in range(16):
                                    i_ = t.matmul(
                                        psQ[opar][:, :],
                                        wqb[:, 6 * HD * hc + 128 * o:
                                            6 * HD * hc + 128 * (o + 1)],
                                        xg[par][:, 512 * hc:512 * (hc + 1)],
                                        start=(hc == 0), stop=(hc == 15))
                                    if hc == 15:
                                        sT.tag(i_)
                            Tq(mm)
                            tmark = sT.n
                            if o == 5:
                                qkv_mm[pos] = tmark

                            def ev(v, tt=tt, o=o, opar=opar, tmark=tmark):
                                v.wait_ge(sT.h, tmark)
                                hl = o % 2
                                if o < 2:
                                    dd = qt[hl][:, 512 * tt:512 * (tt + 1)]
                                elif o < 4:
                                    dd = kt[hl][:, 512 * tt:512 * (tt + 1)]
                                else:
                                    dd = vstg[:, 512 * hl:512 * (hl + 1)]
                                sV.tag(v.tensor_copy(dd, psQ[opar][:, :]))
                            Vq(ev)
                            ev_state[opar] = sV.n
                        if pos + 2 < 16:
                            Dq(lambda d, pos=pos: xg_tt(d, pos + 2))
                        vmark = sV.n
                        for hl in range(2):
                            vtprev = ev_state.get("vt", 0)

                            def vtr(t, tt=tt, hl=hl, vmark=vmark,
                                    vtprev=vtprev):
                                t.wait_ge(sV.h, max(vmark, vtprev))
                                for ss in range(4):
                                    sT.tag(t.matmul(
                                        psVT[ss][:, :128],
                                        vstg[:, 512 * hl + 128 * ss:
                                             512 * hl + 128 * (ss + 1)],
                                        ident_b[:, :],
                                        start=True, stop=True))
                            Tq(vtr)
                            tmark2 = sT.n

                            def vev(v, tt=tt, hl=hl, tmark2=tmark2):
                                v.wait_ge(sT.h, tmark2)
                                for ss in range(4):
                                    sV.tag(v.tensor_copy(
                                        vtm[hl][:, 512 * tt + 128 * ss:
                                                512 * tt + 128 * (ss + 1)],
                                        psVT[ss][:, :128]))
                            Vq(vev)
                            ev_state["vt"] = sV.n
                    barrier()

                with contextlib.ExitStack() as stN:
                    EN = stN.enter_context
                    nsq = [EN(nc.sbuf_tensor(f"nsq{i}", [128, 512], BF16))
                           for i in range(4)]
                    nrow = EN(nc.sbuf_tensor("nrow", [1, 8], F32))
                    psN4 = [EN(nc.psum_tensor(f"psN4_{i}", [1, 512], F32))
                            for i in range(4)]
                    nmx = EN(nc.sbuf_tensor("nmx", [1, 16], F32))
                    ub1 = EN(nc.sbuf_tensor("ub1", [1, 4], F32))
                    ub1b = EN(nc.sbuf_tensor("ub1b", [1, 4], BF16))
                    psN = EN(nc.psum_tensor("psN", [1, 512], F32))
                    psB = EN(nc.psum_tensor("psB", [128, 4], F32))
                    n_state = {}
                    for u in range(4):
                        hl, b = u // 2, u % 2
                        for which, tens in ((0, qt[hl]), (1, kt[hl])):
                            for g4 in range(2):
                                kbs = list(range(4 * g4, 4 * g4 + 4))
                                tprev = n_state.get("nT", 0)

                                def nq(s, kbs=kbs, tens=tens, b=b,
                                       tprev=tprev):
                                    if tprev:
                                        s.wait_ge(sT.h, tprev)
                                    for kblk in kbs:
                                        sSc.tag(s.activation(
                                            out=nsq[kblk % 4][:, :],
                                            in_=tens[:, 4096 * b +
                                                     512 * kblk:
                                                     4096 * b +
                                                     512 * (kblk + 1)],
                                            func=AF.Square))
                                Sq(nq)
                                smark = sSc.n
                                vprev = n_state.get("nmax", 0)

                                def nt(t, kbs=kbs, smark=smark,
                                       vprev=vprev):
                                    t.wait_ge(sSc.h, smark)
                                    if vprev:
                                        t.wait_ge(sV.h, vprev)
                                    for kblk in kbs:
                                        sT.tag(t.matmul(
                                            psN4[kblk % 4][:, :],
                                            ones_c[:, :],
                                            nsq[kblk % 4][:, :],
                                            start=True, stop=True))
                                Tq(nt)
                                n_state["nT"] = sT.n
                                tmark = sT.n

                                def nv(v, kbs=kbs, tmark=tmark):
                                    v.wait_ge(sT.h, tmark)
                                    v.wait_ge(sV.h, sV.n)
                                    for kblk in kbs:
                                        sV.tag(v.reduce_max(
                                            nrow[:, kblk:kblk + 1],
                                            psN4[kblk % 4][:, :],
                                            axis=AXX))
                                Vq(nv)
                                n_state["nmax"] = sV.n

                            def nred(v, u=u, which=which):
                                v.wait_ge(sV.h, sV.n)
                                sV.tag(v.reduce_max(
                                    nmx[:, 8 * which + u:8 * which + u + 1],
                                    nrow[:, :], axis=AXX))
                            Vq(nred)
                    vn = sV.n

                    def ubv(v, vn=vn):
                        v.wait_ge(sV.h, vn)
                        sV.tag(v.tensor_tensor(out=ub1[:, :], in0=nmx[:, 0:4],
                                               in1=nmx[:, 8:12], op=ALU.mult))
                    Vq(ubv)
                    vub = sV.n

                    def ubs(s, vub=vub):
                        s.wait_ge(sV.h, vub)
                        sSc.tag(s.activation(out=ub1[:, :], in_=ub1[:, :],
                                             func=AF.Sqrt))
                    Sq(ubs)
                    sub = sSc.n

                    def ubv2(v, sub=sub):
                        v.wait_ge(sSc.h, sub)
                        sV.tag(v.tensor_scalar(out=ub1b[:, :], in0=ub1[:, :],
                                               scalar1=-ISC, scalar2=None,
                                               op0=ALU.mult))
                    Vq(ubv2)
                    vub2 = sV.n

                    def ubt(t, vub2=vub2):
                        t.wait_ge(sV.h, vub2)
                        sT.tag(t.matmul(psB[:, :4], ones_r[:, :],
                                        ub1b[:, :], start=True, stop=True))
                    Tq(ubt)
                    tub = sT.n

                    def ubv3(v, tub=tub):
                        v.wait_ge(sT.h, tub)
                        for u in range(4):
                            sV.tag(v.tensor_copy(bias_u[u][:, :],
                                                 psB[:, u:u + 1]))
                    Vq(ubv3)
                barrier()

                with contextlib.ExitStack() as stA:
                    EA2 = stA.enter_context
                    pt = [EA2(nc.sbuf_tensor(f"pt{i}", [128, 512], BF16))
                          for i in range(4)]
                    srow = EA2(nc.sbuf_tensor("srow", [1, 512], F32))
                    rrow_a = EA2(nc.sbuf_tensor("rrow_a", [1, 512], BF16))
                    rb_s = EA2(nc.sbuf_tensor("rb_s", [128, 512], BF16))
                    psST = [EA2(nc.psum_tensor(f"psST{i}", [128, 512], F32))
                            for i in range(4)]
                    psAV = EA2(nc.psum_tensor("psAV", [128, 512], F32))
                    psSS = EA2(nc.psum_tensor("psSS", [1, 512], F32))
                    psRB = EA2(nc.psum_tensor("psRB", [128, 512], F32))
                    a_state = {}

                    for u in range(4):
                        hl, b = u // 2, u % 2
                        for qc in range(8):
                            nkc = 4 * qc + 4
                            ngr = nkc // 4
                            st_t = {}
                            exp_t = {}
                            zero_t = {}
                            mask_v = {}

                            def consume(t, kc, u=u, qc=qc, hl=hl, b=b,
                                        nkc=nkc, st_t=st_t, exp_t=exp_t,
                                        zero_t=zero_t):
                                par = kc % 4
                                t.wait_ge(sSc.h, exp_t[kc])
                                if kc in zero_t:
                                    t.wait_ge(sG.h, zero_t[kc])
                                if kc == 0:
                                    pv = a_state.get("av", 0)
                                    if pv:
                                        t.wait_ge(sV.h, pv)
                                    pv = a_state.get("ss", 0)
                                    if pv:
                                        t.wait_ge(sV.h, pv)
                                sT.tag(t.matmul(psSS[:, :], ones_c[:, :],
                                                pt[par][:, :],
                                                start=(kc == 0),
                                                stop=(kc == nkc - 1)))
                                sT.tag(t.matmul(
                                    psAV[:, :],
                                    vtm[hl][:, 4096 * b + 128 * kc:
                                            4096 * b + 128 * (kc + 1)],
                                    pt[par][:, :],
                                    start=(kc == 0),
                                    stop=(kc == nkc - 1)))

                            for g in range(ngr):
                                kcs = list(range(4 * g, 4 * g + 4))

                                def tsec(t, g=g, kcs=kcs, hl=hl, b=b, qc=qc,
                                         st_t=st_t):
                                    if g >= 1:
                                        for kc2 in range(4 * (g - 1), 4 * g):
                                            consume(t, kc2)
                                        a_state["ptg"] = sT.n
                                    for kc in kcs:
                                        prev = a_state.get(("st", kc % 4), 0)
                                        if prev:
                                            t.wait_ge(sSc.h, prev)
                                        sT.tag(t.matmul(
                                            psST[kc % 4][:, :],
                                            kt[hl][:, 4096 * b + 128 * kc:
                                                   4096 * b +
                                                   128 * (kc + 1)],
                                            qt[hl][:, 4096 * b + 512 * qc:
                                                   4096 * b +
                                                   512 * (qc + 1)],
                                            start=True, stop=True))
                                        st_t[kc] = sT.n
                                Tq(tsec)

                                if g == ngr - 1:
                                    def msec(v, kcs=kcs, qc=qc, st_t=st_t,
                                             mask_v=mask_v):
                                        v.wait_ge(sT.h, st_t[kcs[-1]])
                                        for kc in kcs:
                                            jd = kc - 4 * qc
                                            sV.tag(v.tensor_tensor(
                                                out=psST[kc % 4][
                                                    :, 128 * jd:
                                                    128 * (jd + 1)],
                                                in0=psST[kc % 4][
                                                    :, 128 * jd:
                                                    128 * (jd + 1)],
                                                in1=mask_t[:, :],
                                                op=ALU.add))
                                            mask_v[kc] = sV.n
                                    Vq(msec)

                                def esec(s, g=g, kcs=kcs, u=u, st_t=st_t,
                                         mask_v=mask_v, exp_t=exp_t):
                                    s.wait_ge(sT.h, st_t[kcs[-1]])
                                    if kcs[-1] in mask_v:
                                        s.wait_ge(sV.h, mask_v[kcs[-1]])
                                    prev = a_state.get("ptg", 0)
                                    if prev:
                                        s.wait_ge(sT.h, prev)
                                    for kc in kcs:
                                        sSc.tag(s.activation(
                                            out=pt[kc % 4][:, :],
                                            in_=psST[kc % 4][:, :],
                                            func=AF.Exp,
                                            bias=bias_u[u][:, 0:1],
                                            scale=ISC))
                                        a_state[("st", kc % 4)] = sSc.n
                                        exp_t[kc] = sSc.n
                                Sq(esec)

                                if g == ngr - 1:
                                    def zsec(gp, kcs=kcs, qc=qc, exp_t=exp_t,
                                             zero_t=zero_t):
                                        gp.wait_ge(sSc.h, exp_t[kcs[-1]])
                                        for kc in kcs:
                                            jd = kc - 4 * qc
                                            if 1 <= jd <= 3:
                                                sG.tag(gp.memset(
                                                    pt[kc % 4][
                                                        :, 0:128 * jd],
                                                    0.0))
                                                zero_t[kc] = sG.n
                                    Gq(zsec)

                            def lastc(t, nkc=nkc, consume=consume):
                                for kc2 in range(nkc - 4, nkc):
                                    consume(t, kc2)
                                a_state["ptg"] = sT.n
                            Tq(lastc)
                            tdone = sT.n

                            def ep_v1(v, tdone=tdone):
                                v.wait_ge(sT.h, tdone)
                                prev = a_state.get("rb_t", 0)
                                if prev:
                                    v.wait_ge(sT.h, prev)
                                sV.tag(v.tensor_copy(srow[:, :], psSS[:, :]))
                                v.wait_ge(sV.h, sV.n)
                                sV.tag(v.reciprocal(srow[:, :], srow[:, :]))
                                v.wait_ge(sV.h, sV.n)
                                sV.tag(v.tensor_copy(rrow_a[:, :],
                                                     srow[:, :]))
                            Vq(ep_v1)
                            a_state["ss"] = sV.n
                            vrr = sV.n

                            def ep_t(t, vrr=vrr):
                                t.wait_ge(sV.h, vrr)
                                prev = a_state.get("rb_v", 0)
                                if prev:
                                    t.wait_ge(sV.h, prev)
                                sT.tag(t.matmul(psRB[:, :], ones_r[:, :],
                                                rrow_a[:, :],
                                                start=True, stop=True))
                            Tq(ep_t)
                            a_state["rb_t"] = sT.n
                            trb = sT.n

                            def ep_v2(v, trb=trb, u=u, qc=qc, tdone=tdone):
                                v.wait_ge(sT.h, trb)
                                v.wait_ge(sT.h, tdone)
                                sV.tag(v.tensor_copy(rb_s[:, :], psRB[:, :]))
                                v.wait_ge(sV.h, sV.n)
                                sV.tag(v.scalar_tensor_tensor(
                                    out=o_tm[u][:, 512 * qc:512 * (qc + 1)],
                                    in0=psAV[:, :], scalar=1.0,
                                    in1=rb_s[:, :],
                                    op0=ALU.mult, op1=ALU.mult))
                                a_state["av"] = sV.n
                            Vq(ep_v2)
                            a_state["rb_v"] = sV.n
                        if u % 2 == 1:
                            hl_ = u // 2
                            o_mk = sV.n

                            def a2aw(d, hl_=hl_, o_mk=o_mk):
                                d.wait_ge(sV.h, o_mk)
                                for rd in range(8):
                                    b_, sl = rd // 4, rd % 4
                                    sPz[0].tag(d.dma_start(
                                        out=a2a_i[hl_][128 * rd:
                                                       128 * (rd + 1), :],
                                        in_=o_tm[2 * hl_ + b_][
                                            :, 1024 * sl:
                                            1024 * (sl + 1)]), 16)
                            Dq(a2aw)
                            pz_mk = sPz[0].n

                            def a2ac(g, hl_=hl_, pz_mk=pz_mk):
                                g.wait_ge(sPz[0].h, pz_mk)
                                with allow_ici_all_to_all():
                                    sCC.tag(g.collective_compute(
                                        "AllToAll", ALU.bypass,
                                        replica_groups=[list(range(8))],
                                        ins=[a2a_i[hl_]],
                                        outs=[a2a_o[hl_]]), 1)
                            Gq(a2ac)
                            cc_a2a[hl_] = sCC.n
                barrier()
            barrier(ex=(sCC,))

        xr2_st = contextlib.ExitStack()
        x_res2 = xr2_st.enter_context(
            nc.sbuf_tensor("x_res2", [128, 16 * TC], F32))

        def x2_load(d):
            for hc in range(16):
                sLN[0].tag(d.dma_start(
                    out=x_res2[:, 1024 * hc:1024 * (hc + 1)],
                    in_=x_sp[128 * hc:128 * (hc + 1), :]), 16)
        Dq(x2_load)
        x2_marks = sLN[0].n

        with contextlib.ExitStack() as stP9:
            EP = stP9.enter_context
            wob = EP(nc.sbuf_tensor("wob", [128, 16 * HID], BF16))
            og = [EP(nc.sbuf_tensor(f"og{i}", [128, 16 * 512], BF16))
                  for i in range(2)]
            psZ = [EP(nc.psum_tensor(f"psZ{i}", [128, 512], F32))
                   for i in range(2)]
            def wol(d):
                d.wait_ge(sXr[1].h, all_cast_total)
                for hh in range(16):
                    sWo.tag(d.dma_start(
                        out=wob[:, HID * hh:HID * (hh + 1)],
                        in_=wo_bf[128 * hh:128 * (hh + 1), :]), 16)
            Dq(wol)
            wo_marks = sWo.n

            ogd = {}
            for tb in range(2):
                def ogl(d, tb=tb):
                    for par in range(2):
                        d.wait_ge(sCC.h, cc_a2a[par])
                        for hh in range(par, 16, 2):
                            sXg[tb].tag(d.dma_start(
                                out=og[tb][:, 512 * hh:512 * (hh + 1)],
                                in_=a2a_o[par][128 * (hh // 2):
                                               128 * (hh // 2) + 128,
                                               512 * tb:512 * (tb + 1)]),
                                16)
                Sq(ogl)
                ogd[tb] = sXg[tb].n
            z_state = {}
            first_z = [True]
            for tb in range(2):
                for ho in range(16):
                    par = ho % 2
                    eprev = z_state.get(("v", par), 0)

                    def zmm(t, tb=tb, ho=ho, par=par, eprev=eprev,
                            wo_marks=wo_marks):
                        t.wait_ge(sXg[tb].h, ogd[tb])
                        t.wait_ge(sWo.h, wo_marks)
                        if eprev:
                            t.wait_ge(sV.h, eprev)
                        for hh in range(16):
                            i_ = t.matmul(
                                psZ[par][:, :],
                                wob[:, HID * hh + 128 * ho:
                                    HID * hh + 128 * (ho + 1)],
                                og[tb][:, 512 * hh:512 * (hh + 1)],
                                start=(hh == 0), stop=(hh == 15))
                            if hh == 15:
                                sT.tag(i_)
                    Tq(zmm)
                    tmark = sT.n

                    def zadd(v, tb=tb, ho=ho, par=par, tmark=tmark):
                        if first_z[0]:
                            first_z[0] = False
                            v.wait_ge(sLN[0].h, x2_marks)
                        v.wait_ge(sT.h, tmark)
                        sV.tag(v.tensor_tensor(
                            out=x_res2[:, 1024 * ho + 512 * tb:
                                       1024 * ho + 512 * (tb + 1)],
                            in0=psZ[par][:, :],
                            in1=x_res2[:, 1024 * ho + 512 * tb:
                                       1024 * ho + 512 * (tb + 1)],
                            op=ALU.add))
                    Vq(zadd)
                    z_state[("v", par)] = sV.n
            x2_v_done = sV.n
        barrier()

        ffn_phase(x_res2, out_t, [(sV, x2_v_done)])
        xr2_st.close()

        nout = [c.n for c in sOut]

        def fin(g, nout=nout):
            for i in range(4):
                g.wait_ge(sOut[i].h, nout[i])
        Gq(fin)

    return nc


_PROG = None


def kernel(L0_attn_x, L0_attn_w_qkv, L0_attn_w_out, L0_ffn_gate_up_packed,
           L0_ffn_gate_up_scales, L0_ffn_down_packed, L0_ffn_down_scales):
    global _PROG
    x = np.asarray(L0_attn_x, dtype=np.float32).reshape(T, HID)
    wqkv = np.asarray(L0_attn_w_qkv, dtype=np.float32)
    wout = np.asarray(L0_attn_w_out, dtype=np.float32)
    gup = np.asarray(L0_ffn_gate_up_packed, dtype=np.int32)
    gus = np.asarray(L0_ffn_gate_up_scales)
    dnp = np.asarray(L0_ffn_down_packed, dtype=np.int32)
    dns = np.asarray(L0_ffn_down_scales)

    in_maps = []
    for c in range(NCORES):
        cols = []
        for which in range(3):
            for hl in range(2):
                h = 2 * c + hl
                base = (which * NH + h) * HD
                cols.extend(range(base, base + HD))
        in_maps.append({
            "x_t": np.ascontiguousarray(x[TC * c:TC * (c + 1)].T),
            "wqkv_sl": np.ascontiguousarray(wqkv[:, cols]),
            "wout_sl": wout,
            "gup_sl": np.ascontiguousarray(gup[OGU_C * c:OGU_C * (c + 1)]),
            "gus_sl": np.ascontiguousarray(gus[:, OGU_C * c:OGU_C * (c + 1)]),
            "dnp_sl": np.ascontiguousarray(dnp[DH_C * c:DH_C * (c + 1)]),
            "dns_sl": np.ascontiguousarray(dns[:, DH_C * c:DH_C * (c + 1)]),
        })

    if _PROG is None:
        _PROG = build_program()
    res = run_bass_kernel_spmd(_PROG, in_maps, core_ids=list(range(NCORES)))
    out = np.empty((T, HID), dtype=np.float32)
    for c in range(NCORES):
        out[TC * c:TC * (c + 1)] = res.results[c]["out_t"].T
    return out.reshape(B, S, HID)


# version 16
# speedup vs baseline: 1.0050x; 1.0050x over previous
import contextlib
import numpy as np
import concourse.bass as bass
import concourse.mybir as mybir
from concourse.bass_utils import run_bass_kernel_spmd
from concourse.aot_env import allow_ici_all_to_all

F32 = mybir.dt.float32
BF16 = mybir.dt.bfloat16
F16 = mybir.dt.float16
I32 = mybir.dt.int32
AF = mybir.ActivationFunctionType
ALU = mybir.AluOpType
AXX = mybir.AxisListType.X

HID = 2048
NH = 16
HD = 128
IFF = 5632
B = 2
S = 4096
T = 8192
TC = 1024
NCORES = 8
OGU = 2 * IFF
OGU_C = OGU // 8
NGT = OGU_C // 128
DH_C = HID // 8
NDT = DH_C // 128
ISC = float(1.0 / np.sqrt(128.0))
NEG = -1e9


class Cnt:
    __slots__ = ("h", "n")

    def __init__(self, h):
        self.h = h
        self.n = 0

    def tag(self, instr, k=1):
        instr.then_inc(self.h, k)
        self.n += k
        return self.n


def build_program():
    nc = bass.Bass("TRN2", target_bir_lowering=False, debug=False)
    uid_box = [0]

    def uid(s):
        uid_box[0] += 1
        return f"{s}_{uid_box[0]}"

    x_t = nc.dram_tensor("x_t", [HID, TC], F32, kind="ExternalInput").ap()
    wqkv = nc.dram_tensor("wqkv_sl", [HID, 6 * HD], F32, kind="ExternalInput").ap()
    wout = nc.dram_tensor("wout_sl", [HID, HID], F32, kind="ExternalInput").ap()
    gup = nc.dram_tensor("gup_sl", [OGU_C, 256], I32, kind="ExternalInput").ap()
    gus = nc.dram_tensor("gus_sl", [16, OGU_C], F16, kind="ExternalInput").ap()
    dnp = nc.dram_tensor("dnp_sl", [DH_C, 704], I32, kind="ExternalInput").ap()
    dns = nc.dram_tensor("dns_sl", [44, DH_C], F16, kind="ExternalInput").ap()
    out_t = nc.dram_tensor("out_t", [HID, TC], F32, kind="ExternalOutput").ap()

    GU_PARTS = ((0, 4), (4, 8), (8, 11))
    wgu_piece = [nc.dram_tensor(f"wgu_piece{i}",
                                [128, (hi - lo) * 2048], BF16).ap()
                 for i, (lo, hi) in enumerate(GU_PARTS)]
    g_gu = [nc.dram_tensor(f"g_gu{i}", [8 * 128, (hi - lo) * 2048], BF16).ap()
            for i, (lo, hi) in enumerate(GU_PARTS)]
    wd_piece = nc.dram_tensor("wd_piece", [128, NDT * 5632], BF16).ap()
    g_d = nc.dram_tensor("g_d", [8 * 128, NDT * 5632], BF16).ap()
    xh_p = [nc.dram_tensor(f"xh_p{i}", [HID, 512], BF16).ap()
            for i in range(2)]
    g_xs = [nc.dram_tensor(f"g_x{i}", [8 * HID, 512], BF16).ap()
            for i in range(2)]
    a2a_i = [nc.dram_tensor(f"a2a_i{i}", [8 * HD, TC], BF16).ap()
             for i in range(2)]
    a2a_o = [nc.dram_tensor(f"a2a_o{i}", [8 * HD, TC], BF16).ap()
             for i in range(2)]
    x_sp = nc.dram_tensor("x_sp", [HID, TC], F32).ap()
    wq_bf = nc.dram_tensor("wq_bf", [HID, 6 * HD], BF16).ap()
    wo_bf = nc.dram_tensor("wo_bf", [HID, HID], BF16).ap()

    with contextlib.ExitStack() as top:
        E = top.enter_context
        blk = E(nc.Block())

        sD = Cnt(E(nc.semaphore("sD")))
        sDg = Cnt(E(nc.semaphore("sDg")))
        sDs = Cnt(E(nc.semaphore("sDs")))
        sT = Cnt(E(nc.semaphore("sT")))
        sV = Cnt(E(nc.semaphore("sV")))
        sSc = Cnt(E(nc.semaphore("sSc")))
        sG = Cnt(E(nc.semaphore("sG")))
        sCC = Cnt(E(nc.semaphore("sCC")))
        sOut = [Cnt(E(nc.semaphore(f"sOut{i}"))) for i in range(4)]
        sLN = [Cnt(E(nc.semaphore(f"sLN{i}"))) for i in range(4)]
        sWq = Cnt(E(nc.semaphore("sWq")))
        sWo = Cnt(E(nc.semaphore("sWo")))
        sWg = [Cnt(E(nc.semaphore(f"sWg{i}"))) for i in range(2)]
        sWd = [Cnt(E(nc.semaphore(f"sWd{i}"))) for i in range(2)]
        sXg = [Cnt(E(nc.semaphore(f"sXg{i}"))) for i in range(2)]
        sPp = [Cnt(E(nc.semaphore(f"sPp{i}"))) for i in range(2)]
        sPx = [Cnt(E(nc.semaphore(f"sPx{i}"))) for i in range(4)]
        sPz = [Cnt(E(nc.semaphore(f"sPz{i}"))) for i in range(2)]
        sXr = [Cnt(E(nc.semaphore(f"sXr{i}"))) for i in range(4)]
        sSt1 = [Cnt(E(nc.semaphore(f"sSt1{i}"))) for i in range(4)]
        sZr = [Cnt(E(nc.semaphore(f"sZr{i}"))) for i in range(4)]
        sSt2 = [Cnt(E(nc.semaphore(f"sSt2{i}"))) for i in range(4)]

        ALL_ENG = ("tensor", "vector", "scalar", "gpsimd", "sync")
        eng_cnt = {"tensor": sT, "vector": sV, "scalar": sSc, "gpsimd": sG,
                   "sync": sD}
        ALL_CNT_L = ([sD, sDg, sDs, sT, sV, sSc, sG, sCC, sWq, sWo]
                     + sOut + sWg + sWd + sXg + sPp + sPx + sPz + sXr
                     + sSt1 + sZr + sSt2 + sLN)

        def Tq(fn):
            blk.tensor(fn)

        def Vq(fn):
            blk.vector(fn)

        def Sq(fn):
            blk.scalar(fn)

        def Gq(fn):
            blk.gpsimd(fn)

        def Dq(fn):
            blk.sync(fn)


        def barrier(ex=()):
            snap = [(c, c.n) for c in ALL_CNT_L if c.n and c not in ex]
            for en in ALL_ENG:
                def wt(e, en=en, snap=snap):
                    own = eng_cnt[en]
                    for c, v in snap:
                        if c is not own:
                            e.wait_ge(c.h, v)
                getattr(blk, en)(wt)

        cc_box = [None, 0]

        ident_f = E(nc.sbuf_tensor("ident_f", [128, 128], F32))
        ident_b = E(nc.sbuf_tensor("ident_b", [128, 128], BF16))
        mask_t = E(nc.sbuf_tensor("mask_t", [128, 128], F32))
        ones_c = E(nc.sbuf_tensor("ones_c", [128, 1], BF16))
        ones_r = E(nc.sbuf_tensor("ones_r", [1, 128], BF16))
        epst = E(nc.sbuf_tensor("epst", [128, 2], F32))

        def consts(g):
            sG.tag(g.memset(ident_f[:, :], 0.0))
            g.wait_ge(sG.h, sG.n)
            sG.tag(g.affine_select(
                out=ident_f[:, :], in_=ident_f[:, :],
                compare_op=ALU.not_equal, fill=1.0, base=0,
                pattern=[[-1, 128]], channel_multiplier=1))
            sG.tag(g.memset(mask_t[:, :], 0.0))
            g.wait_ge(sG.h, sG.n)
            sG.tag(g.affine_select(
                out=mask_t[:, :], in_=mask_t[:, :],
                compare_op=ALU.is_ge, fill=NEG, base=0,
                pattern=[[1, 128]], channel_multiplier=-1))
            sG.tag(g.memset(ones_c[:, :], 1.0))
            sG.tag(g.memset(ones_r[:, :], 1.0))
            sG.tag(g.memset(epst[:, 0:1], 1e-5))
            sG.tag(g.memset(epst[:, 1:2], 4e-6))
            g.wait_ge(sG.h, sG.n)
        Gq(consts)
        g_consts = sG.n

        def cast_ident(v):
            v.wait_ge(sG.h, g_consts)
            sV.tag(v.tensor_copy(ident_b[:, :], ident_f[:, :]))
        Vq(cast_ident)
        g_after = sG.n
        v_after = sV.n

        def pre_t(t):
            t.wait_ge(sG.h, g_after)
            t.wait_ge(sV.h, v_after)
        Tq(pre_t)

        def pre_s(s):
            s.wait_ge(sG.h, g_after)
        Sq(pre_s)
        barrier()

        xr_st = contextlib.ExitStack()
        x_res = xr_st.enter_context(
            nc.sbuf_tensor("x_res", [128, 16 * TC], F32))

        def x0_load(d):
            for hc in range(16):
                sLN[hc % 4].tag(d.dma_start(
                    out=x_res[:, 1024 * hc:1024 * (hc + 1)],
                    in_=x_t[128 * hc:128 * (hc + 1), :]), 16)
        Dq(x0_load)
        x0_marks = [c.n for c in sLN]

        cst_st = contextlib.ExitStack()
        cwf = cst_st.enter_context(nc.sbuf_tensor("cwf", [128, 1024], F32))
        cwb = cst_st.enter_context(nc.sbuf_tensor("cwb", [128, 1024], BF16))

        with contextlib.ExitStack() as st2:
            E2 = st2.enter_context
            gup_s = E2(nc.sbuf_tensor("gup_s", [128, NGT * 256], I32))
            dnp_s = E2(nc.sbuf_tensor("dnp_s", [128, NDT * 704], I32))
            gus_s = E2(nc.sbuf_tensor("gus_s", [16, OGU_C], F16))
            dns_s = E2(nc.sbuf_tensor("dns_s", [44, DH_C], F16))
            gusf = E2(nc.sbuf_tensor("gusf", [16, OGU_C], F32))
            dnsf = E2(nc.sbuf_tensor("dnsf", [44, DH_C], F32))
            scT_gu = E2(nc.sbuf_tensor("scT_gu", [128, NGT * 16], F32))
            scT_d = E2(nc.sbuf_tensor("scT_d", [128, NDT * 44], F32))
            wnat = E2(nc.sbuf_tensor("wnat", [128, IFF], BF16))
            pstage = [E2(nc.sbuf_tensor(f"pstage{i}", [128, 512], BF16))
                      for i in range(2)]
            wrk = {}
            for en in range(3):
                wrk[en] = {k: E2(nc.sbuf_tensor(f"{k}_{en}", [128, 704], I32))
                           for k in ("w2", "fk2", "fb", "s2", "vs")}
                for k in ("m1", "aa", "fm", "vv"):
                    wrk[en][k] = E2(nc.sbuf_tensor(f"{k}_{en}", [128, 1408],
                                                   BF16))
            psSC = E2(nc.psum_tensor("psSC", [128, 64], F32))
            psW = [E2(nc.psum_tensor(f"psW{i}", [128, 512], F32))
                   for i in range(4)]

            cast_state = {"g": 0, "wq": 0, "wo": 0}

            def cast_job(src_ap, dst_ap, cols):
                def job(g):
                    if cast_state["g"]:
                        g.wait_ge(sG.h, cast_state["g"])
                    sXr[0].tag(g.dma_start(out=cwf[:, :cols], in_=src_ap),
                               16)
                    g.wait_ge(sXr[0].h, sXr[0].n)
                    if cast_state["wo"]:
                        g.wait_ge(sXr[1].h, cast_state["wo"])
                    sG.tag(g.tensor_copy(cwb[:, :cols], cwf[:, :cols]))
                    cast_state["g"] = sG.n
                    g.wait_ge(sG.h, cast_state["g"])
                    sXr[1].tag(g.dma_start(out=dst_ap, in_=cwb[:, :cols]),
                               16)
                    cast_state["wo"] = sXr[1].n
                return job

            filler = []
            for hc in range(16):
                filler.append(cast_job(wqkv[128 * hc:128 * (hc + 1), :],
                                       wq_bf[128 * hc:128 * (hc + 1), :],
                                       6 * HD))
            for hh in range(16):
                for cl in range(2):
                    filler.append(cast_job(
                        wout[128 * hh:128 * (hh + 1),
                             1024 * cl:1024 * (cl + 1)],
                        wo_bf[128 * hh:128 * (hh + 1),
                              1024 * cl:1024 * (cl + 1)], 1024))
            wq_cast_total = 16 * 16
            all_cast_total = 16 * len(filler)

            def fill_one():
                if filler:
                    Gq(filler.pop(0))

            def p2_in(d):
                for ot in range(NGT):
                    sD.tag(d.dma_start(out=gup_s[:, 256 * ot:256 * (ot + 1)],
                                       in_=gup[128 * ot:128 * (ot + 1), :]),
                           16)
                for ht in range(NDT):
                    sD.tag(d.dma_start(out=dnp_s[:, 704 * ht:704 * (ht + 1)],
                                       in_=dnp[128 * ht:128 * (ht + 1), :]),
                           16)
                sD.tag(d.dma_start(out=gus_s[:, :], in_=gus[:, :]), 16)
                sD.tag(d.dma_start(out=dns_s[:, :], in_=dns[:, :]), 16)
            Dq(p2_in)
            d_in = sD.n

            def p2_scv(v):
                v.wait_ge(sD.h, d_in)
                sV.tag(v.tensor_scalar(out=gusf[:, :], in0=gus_s[:, :],
                                       scalar1=256.0, scalar2=None,
                                       op0=ALU.mult))
                sV.tag(v.tensor_scalar(out=dnsf[:, :], in0=dns_s[:, :],
                                       scalar1=256.0, scalar2=None,
                                       op0=ALU.mult))
            Vq(p2_scv)
            v_scf = sV.n

            sct_jobs = ([("gu", ot) for ot in range(NGT)] +
                        [("d", ht) for ht in range(NDT)])
            sct_state = {"v": 0}
            for kind, idx in sct_jobs:
                vprev = sct_state["v"]

                def tmm(t, kind=kind, idx=idx, vprev=vprev):
                    t.wait_ge(sV.h, max(v_scf, vprev))
                    if kind == "gu":
                        sT.tag(t.matmul(psSC[:, :16],
                                        gusf[:, 128 * idx:128 * (idx + 1)],
                                        ident_f[:16, :16],
                                        start=True, stop=True))
                    else:
                        sT.tag(t.matmul(psSC[:, :44],
                                        dnsf[:, 128 * idx:128 * (idx + 1)],
                                        ident_f[:44, :44],
                                        start=True, stop=True))
                Tq(tmm)
                tmark = sT.n

                def vcp(v, kind=kind, idx=idx, tmark=tmark):
                    v.wait_ge(sT.h, tmark)
                    if kind == "gu":
                        sV.tag(v.tensor_copy(
                            scT_gu[:, 16 * idx:16 * (idx + 1)], psSC[:, :16]))
                    else:
                        sV.tag(v.tensor_copy(
                            scT_d[:, 44 * idx:44 * (idx + 1)], psSC[:, :44]))
                Vq(vcp)
                sct_state["v"] = sV.n

            def dq_chain(e, wtile, W, cnt):
                last = [0, 0, 0]

                def step(c, instr):
                    cnt.tag(instr)
                    last[c] = cnt.n

                def wt(c):
                    if last[c]:
                        e.wait_ge(cnt.h, last[c])

                def ops_for(jp):
                    c = jp % 3
                    w = wrk[c]
                    o = []
                    if jp == 0:
                        o.append(lambda: (wt(c), step(c, e.tensor_scalar(
                            out=w["w2"][:, :W], in0=wtile, scalar1=0x70007,
                            scalar2=None, op0=ALU.bitwise_and))))
                    else:
                        o.append(lambda: (wt(c), step(c, e.tensor_scalar(
                            out=w["w2"][:, :W], in0=wtile, scalar1=4 * jp,
                            scalar2=0x70007,
                            op0=ALU.logical_shift_right,
                            op1=ALU.bitwise_and))))
                    o.append(lambda: (wt(c), step(c, e.tensor_scalar(
                        out=w["fk2"][:, :W], in0=w["w2"][:, :W],
                        scalar1=0x43004300, scalar2=None,
                        op0=ALU.bitwise_or))))
                    o.append(lambda: (wt(c), step(c, e.tensor_scalar(
                        out=w["fb"][:, :W], in0=w["w2"][:, :W],
                        scalar1=64, scalar2=0x3B003B00,
                        op0=ALU.mult, op1=ALU.add))))
                    o.append(lambda: (wt(c), step(c, e.tensor_scalar(
                        out=w["s2"][:, :W], in0=wtile, scalar1=12 - 4 * jp,
                        scalar2=-0x7FFF8000,
                        op0=ALU.logical_shift_left,
                        op1=ALU.bitwise_and))))
                    fkb = w["fk2"][:, :W].bitcast(BF16)
                    o.append(lambda: (wt(c), step(c, e.tensor_scalar(
                        out=w["m1"][:, :2 * W], in0=fkb, scalar1=130.0,
                        scalar2=129.0, op0=ALU.min, op1=ALU.subtract))))
                    o.append(lambda: (wt(c), step(c, e.tensor_scalar(
                        out=w["aa"][:, :2 * W], in0=fkb,
                        scalar1=0.001953125, scalar2=0.25,
                        op0=ALU.mult, op1=ALU.subtract))))
                    o.append(lambda: (wt(c), step(c, e.tensor_tensor(
                        out=w["fm"][:, :2 * W],
                        in0=w["fb"][:, :W].bitcast(BF16),
                        in1=w["m1"][:, :2 * W], op=ALU.mult))))
                    o.append(lambda: (wt(c), step(c, e.tensor_tensor(
                        out=w["vv"][:, :2 * W], in0=w["fm"][:, :2 * W],
                        in1=w["aa"][:, :2 * W], op=ALU.max))))
                    o.append(lambda: (wt(c), step(c, e.tensor_tensor(
                        out=w["vs"][:, :W],
                        in0=w["vv"][:, :2 * W].bitcast(I32),
                        in1=w["s2"][:, :W], op=ALU.bitwise_or))))
                    vb = w["vs"][:, :W].bitcast(BF16)
                    o.append(lambda: (wt(c), step(c, e.tensor_copy(
                        wnat[:, jp:8 * W:8], vb[:, 0:2 * W:2]))))
                    o.append(lambda: (wt(c), step(c, e.tensor_copy(
                        wnat[:, jp + 4:8 * W:8], vb[:, 1:2 * W:2]))))
                    return o
                for grp in ((0, 1, 2), (3,)):
                    chains = [ops_for(jp) for jp in grp]
                    for s in range(max(len(ch) for ch in chains)):
                        for ch in chains:
                            if s < len(ch):
                                ch[s]()

            wt_state = {}

            def do_tile(wtile, W, scT, sc_off, ngrp, piece_ap, pcol,
                        tr_prev_v):
                def dv(v, wtile=wtile, W=W, tr_prev_v=tr_prev_v):
                    v.wait_ge(sT.h, tr_prev_v)
                    dq_chain(v, wtile, W, sV)
                Vq(dv)

                def sc(v, scT=scT, sc_off=sc_off, ngrp=ngrp):
                    v.wait_ge(sV.h, sV.n)
                    for g in range(ngrp):
                        sV.tag(v.tensor_scalar(
                            out=wnat[:, 128 * g:128 * (g + 1)],
                            in0=wnat[:, 128 * g:128 * (g + 1)],
                            scalar1=scT[:, sc_off + g:sc_off + g + 1],
                            scalar2=None, op0=ALU.mult))
                        v.wait_ge(sV.h, sV.n)
                Vq(sc)
                scdone = sV.n

                for g0 in range(0, ngrp, 4):
                    par = (g0 // 4) % 2
                    vprev = wt_state.get("evac", 0)

                    def tmm(t, g0=g0, vprev=vprev, scdone=scdone):
                        t.wait_ge(sV.h, max(vprev, scdone))
                        for k in range(4):
                            g = g0 + k
                            sT.tag(t.matmul(psW[k][:, :128],
                                            wnat[:, 128 * g:128 * (g + 1)],
                                            ident_b[:, :],
                                            start=True, stop=True))
                    Tq(tmm)
                    tmark = sT.n
                    dprev = wt_state.get(("dma", par), 0)

                    def vev(v, par=par, tmark=tmark, dprev=dprev):
                        v.wait_ge(sT.h, tmark)
                        if dprev:
                            v.wait_ge(sPp[par].h, dprev)
                        for k in range(4):
                            sV.tag(v.tensor_copy(
                                pstage[par][:, 128 * k:128 * (k + 1)],
                                psW[k][:, :128]))
                    Vq(vev)
                    wt_state["evac"] = sV.n
                    wt_state[("evac", par)] = sV.n
                    vmark = sV.n

                    def dst(d, g0=g0, par=par, vmark=vmark,
                            piece_ap=piece_ap, pcol=pcol):
                        d.wait_ge(sV.h, vmark)
                        sPp[par].tag(d.dma_start(
                            out=piece_ap[:, pcol + 128 * g0:
                                         pcol + 128 * g0 + 512],
                            in_=pstage[par][:, :]), 16)
                    Dq(dst)
                    wt_state[("dma", par)] = sPp[par].n
                return sT.n

            cc_gu_parts = []
            tr_prev = 0
            for ot in range(NGT):
                part = 0 if ot < 4 else (1 if ot < 8 else 2)
                lo_p = GU_PARTS[part][0]
                tr_prev = do_tile(gup_s[:, 256 * ot:256 * (ot + 1)], 256,
                                  scT_gu, 16 * ot, 16, wgu_piece[part],
                                  2048 * (ot - lo_p), tr_prev)
                if ot + 1 in (4, 8):
                    pi = 0 if ot + 1 == 4 else 1
                    pp = [sPp[0].n, sPp[1].n]

                    def p2_ccp(g, pi=pi, pp=pp):
                        g.wait_ge(sPp[0].h, pp[0])
                        g.wait_ge(sPp[1].h, pp[1])
                        sCC.tag(g.collective_compute(
                            "AllGather", ALU.bypass,
                            replica_groups=[list(range(8))],
                            ins=[wgu_piece[pi]], outs=[g_gu[pi]]), 1)
                    Gq(p2_ccp)
                    cc_gu_parts.append((ot + 1, sCC.n))
            gu_pieces = [sPp[0].n, sPp[1].n]

            def p2_cc2(g):
                g.wait_ge(sPp[0].h, gu_pieces[0])
                g.wait_ge(sPp[1].h, gu_pieces[1])
                sCC.tag(g.collective_compute(
                    "AllGather", ALU.bypass, replica_groups=[list(range(8))],
                    ins=[wgu_piece[2]], outs=[g_gu[2]]), 1)
            Gq(p2_cc2)
            cc_gu_parts.append((11, sCC.n))
            for ht in range(NDT):
                tr_prev = do_tile(dnp_s[:, 704 * ht:704 * (ht + 1)], 704,
                                  scT_d, 44 * ht, 44, wd_piece, 5632 * ht,
                                  tr_prev)
            dma_pieces = [sPp[0].n, sPp[1].n]

            def p2_cc(g):
                g.wait_ge(sPp[0].h, dma_pieces[0])
                g.wait_ge(sPp[1].h, dma_pieces[1])
                sCC.tag(g.collective_compute(
                    "AllGather", ALU.bypass, replica_groups=[list(range(8))],
                    ins=[wd_piece], outs=[g_d]), 1)
            Gq(p2_cc)
            cc_box[0] = cc_gu_parts
            cc_box[1] = sCC.n
        barrier(ex=(sCC,))
        def ln_pass(tb, xres, psX, psX2, psM, psR, deps, xh=None,
                    piece_ap=None, sg_T=0, pg_V=0, bg_V=0, ag_T=0):
            res = {}
            with contextlib.ExitStack() as stL:
                EL = stL.enter_context

                def xsl(hc):
                    return xres[:, 1024 * hc + 512 * tb:
                                1024 * hc + 512 * (tb + 1)]
                xb = [EL(nc.sbuf_tensor(uid(f"ln_xb{i}"), [128, 512], BF16))
                      for i in range(4)]
                sq = [EL(nc.sbuf_tensor(uid(f"ln_sq{i}"), [128, 512], BF16))
                      for i in range(4)]
                rw = [EL(nc.sbuf_tensor(uid(f"ln_rw{i}"), [1, 512], F32))
                      for i in range(4)]
                rwb = [EL(nc.sbuf_tensor(uid(f"ln_rwb{i}"), [1, 512], BF16))
                       for i in range(2)]
                tmpf = [EL(nc.sbuf_tensor(uid(f"ln_tmp{i}"), [128, 512], F32))
                        for i in range(2)]
                xst = ([EL(nc.sbuf_tensor(uid(f"ln_xst{i}"), [128, 512],
                                          BF16)) for i in range(2)]
                       if piece_ap is not None else None)
                stats_t = {}
                pc_state = {0: sPx[0].n, 1: sPx[1].n}
                first = {"v": True, "s": True, "t": True}
                for g4 in range(4):
                    hcs = list(range(4 * g4, 4 * g4 + 4))
                    tprev = stats_t.get(g4 - 1, 0)

                    def cst(v, hcs=hcs, tprev=tprev):
                        if first["v"]:
                            first["v"] = False
                            for cnt_, val_ in deps:
                                if val_:
                                    v.wait_ge(cnt_.h, val_)
                            if sg_T:
                                v.wait_ge(sT.h, sg_T)
                        if tprev:
                            v.wait_ge(sT.h, tprev)
                        for hc in hcs:
                            sV.tag(v.tensor_copy(xb[hc % 4][:, :], xsl(hc)))
                    Vq(cst)
                    vmark = sV.n

                    def sqr(s, hcs=hcs, tprev=tprev):
                        if first["s"]:
                            first["s"] = False
                            for cnt_, val_ in deps:
                                if val_:
                                    s.wait_ge(cnt_.h, val_)
                            if sg_T:
                                s.wait_ge(sT.h, sg_T)
                        if tprev:
                            s.wait_ge(sT.h, tprev)
                        for hc in hcs:
                            sSc.tag(s.activation(out=sq[hc % 4][:, :],
                                                 in_=xsl(hc),
                                                 func=AF.Square))
                    Sq(sqr)
                    smark = sSc.n

                    def smm(t, hcs=hcs, vmark=vmark, smark=smark):
                        if first["t"]:
                            first["t"] = False
                            if pg_V:
                                t.wait_ge(sV.h, pg_V)
                        t.wait_ge(sV.h, vmark)
                        t.wait_ge(sSc.h, smark)
                        for hc in hcs:
                            sT.tag(t.matmul(psX, ones_c[:, :],
                                            xb[hc % 4][:, :],
                                            start=(hc == 0),
                                            stop=(hc == 15)))
                            sT.tag(t.matmul(psX2, ones_c[:, :],
                                            sq[hc % 4][:, :],
                                            start=(hc == 0),
                                            stop=(hc == 15)))
                    Tq(smm)
                    stats_t[g4] = sT.n
                res["sq"] = sSc.n
                res["statsT"] = sT.n
                tstats = sT.n

                def cmb_v(v, tstats=tstats):
                    v.wait_ge(sT.h, tstats)
                    sV.tag(v.tensor_scalar(out=rw[0][:, :], in0=psX,
                                           scalar1=1.0 / 2048,
                                           scalar2=None, op0=ALU.mult))
                    sV.tag(v.tensor_scalar(out=rw[1][:, :], in0=psX2,
                                           scalar1=1.0 / 2048,
                                           scalar2=None, op0=ALU.mult))
                    v.wait_ge(sV.h, sV.n)
                    sV.tag(v.tensor_tensor(out=rw[2][:, :], in0=rw[0][:, :],
                                           in1=rw[0][:, :], op=ALU.mult))
                    v.wait_ge(sV.h, sV.n)
                    sV.tag(v.tensor_tensor(out=rw[2][:, :], in0=rw[1][:, :],
                                           in1=rw[2][:, :],
                                           op=ALU.subtract))
                Vq(cmb_v)
                vvar = sV.n

                def cmb_s(s, vvar=vvar):
                    s.wait_ge(sV.h, vvar)
                    sSc.tag(s.activation(out=rw[3][:, :], in_=rw[2][:, :],
                                         func=AF.Sqrt,
                                         bias=epst[:1, 0:1], scale=1.0))
                Sq(cmb_s)
                ssqrt = sSc.n

                def cmb_v2(v, ssqrt=ssqrt):
                    v.wait_ge(sSc.h, ssqrt)
                    sV.tag(v.reciprocal(rw[3][:, :], rw[3][:, :]))
                    v.wait_ge(sV.h, sV.n)
                    sV.tag(v.tensor_copy(rwb[0][:, :], rw[3][:, :]))
                    sV.tag(v.scalar_tensor_tensor(out=rw[0][:, :],
                                                  in0=rw[0][:, :],
                                                  scalar=-1.0,
                                                  in1=rw[3][:, :],
                                                  op0=ALU.mult,
                                                  op1=ALU.mult))
                    v.wait_ge(sV.h, sV.n)
                    sV.tag(v.tensor_copy(rwb[1][:, :], rw[0][:, :]))
                Vq(cmb_v2)
                vrows = sV.n
                res["cmb"] = vrows

                def bcast(t, vrows=vrows):
                    if bg_V:
                        t.wait_ge(sV.h, bg_V)
                    t.wait_ge(sV.h, vrows)
                    sT.tag(t.matmul(psR, ones_r[:, :], rwb[0][:, :],
                                    start=True, stop=True))
                    sT.tag(t.matmul(psM, ones_r[:, :], rwb[1][:, :],
                                    start=True, stop=True))
                Tq(bcast)
                tbc = sT.n

                firstA = [True]
                for hc in range(16):
                    par = hc % 2

                    def app(v, hc=hc, par=par, tbc=tbc,
                            pprev=pc_state.get(par, 0)):
                        if firstA[0]:
                            firstA[0] = False
                            if ag_T:
                                v.wait_ge(sT.h, ag_T)
                        v.wait_ge(sT.h, tbc)
                        if piece_ap is not None and pprev:
                            v.wait_ge(sPx[par].h, pprev)
                        sV.tag(v.tensor_tensor(
                            out=tmpf[par][:, :], in0=xsl(hc),
                            in1=psR, op=ALU.mult))
                        v.wait_ge(sV.h, sV.n)
                        if xh is not None:
                            dst = xh[:, 512 * hc:512 * (hc + 1)]
                        else:
                            dst = xst[par][:, :]
                        sV.tag(v.tensor_tensor(out=dst, in0=tmpf[par][:, :],
                                               in1=psM, op=ALU.add))
                    Vq(app)
                    vap = sV.n
                    if piece_ap is not None:
                        def pdma(d, hc=hc, par=par, vap=vap):
                            d.wait_ge(sV.h, vap)
                            sPx[par].tag(d.dma_start(
                                out=piece_ap[128 * hc:128 * (hc + 1), :],
                                in_=xst[par][:, :]), 16)
                        Sq(pdma)
                        pc_state[par] = sPx[par].n
                res["done"] = sV.n
            return res

        def ffn_phase(xres, out_ap, deps0):
            with contextlib.ExitStack() as stFF:
                EFF = stFF.enter_context
                xh = EFF(nc.sbuf_tensor(uid("xh"), [128, 16 * 512], BF16))
                h_all = EFF(nc.sbuf_tensor(uid("h"), [128, 44 * 512], BF16))
                wg = [EFF(nc.sbuf_tensor(uid(f"wg{i}"), [128, HID], BF16))
                      for i in range(4)]
                sg = [EFF(nc.sbuf_tensor(uid(f"sg{i}"), [128, 512], BF16))
                      for i in range(2)]
                wd = [EFF(nc.sbuf_tensor(uid(f"wdn{i}"), [128, IFF], BF16))
                      for i in range(2)]
                psGU = [EFF(nc.psum_tensor(uid(f"psGU{i}"), [128, 512], F32))
                        for i in range(4)]
                psD_ = [EFF(nc.psum_tensor(uid(f"psD{i}"), [128, 512], F32))
                        for i in range(2)]
                psSR = EFF(nc.psum_tensor(uid("psSR"), [128, 512], F32))

                GU_ORDER = ([i for i in range(44) if i % NGT < 4]
                            + [i for i in range(44) if 4 <= i % NGT < 8]
                            + [i for i in range(44) if i % NGT >= 8])
                wdma, mmt, ept = {}, {}, {}
                ddma, dmmt, dept = {}, {}, {}
                ln_res, h_done, state = {}, {}, {}

                def wdma_it(d, pos):
                    par = pos % 2
                    if pos >= 2:
                        d.wait_ge(sT.h, mmt[pos - 2])
                    itp = GU_ORDER[pos % 44]
                    need = 0
                    for half in range(2):
                        j = itp + 44 * half
                        ol = j % NGT
                        for hi, mark in cc_box[0]:
                            if ol < hi:
                                need = max(need, mark)
                                break
                    d.wait_ge(sCC.h, need)
                    for half in range(2):
                        j = itp + 44 * half
                        r, ol = j // NGT, j % NGT
                        part = 0 if ol < 4 else (1 if ol < 8 else 2)
                        olp = ol - (0, 4, 8)[part]
                        sWg[par].tag(d.dma_start(
                            out=wg[2 * par + half][:, :],
                            in_=g_gu[part][128 * r:128 * (r + 1),
                                           2048 * olp:2048 * (olp + 1)]), 16)
                    wdma[pos] = sWg[par].n

                def ddma_m(d, m):
                    par = m % 2
                    if m < 2:
                        d.wait_ge(sCC.h, cc_box[1])
                    if m >= 2:
                        d.wait_ge(sT.h, dmmt[m - 2])
                    mp = m % 16
                    r, hl = mp // 2, mp % 2
                    sWd[par].tag(d.dma_start(
                        out=wd[par][:, :],
                        in_=g_d[128 * r:128 * (r + 1),
                                5632 * hl:5632 * (hl + 1)]), 16)
                    ddma[m] = sWd[par].n

                def wq_emit(it):
                    (Dq if it % 2 == 0 else Gq)(
                        lambda d, it=it: wdma_it(d, it))

                def dq_emit(m):
                    (Dq if m % 2 == 0 else Gq)(
                        lambda d, m=m: ddma_m(d, m))

                def gu_pass(p):
                    ln_done = ln_res[p]["done"]
                    h_guard = state.get("down_T", 0)
                    if p == 0:
                        wq_emit(0)
                        wq_emit(1)
                    for posp in range(44):
                        pos = 44 * p + posp
                        itp = GU_ORDER[posp]
                        par = pos % 2

                        def mm(t, pos=pos, posp=posp, par=par,
                               ln_done=ln_done):
                            t.wait_ge(sWg[par].h, wdma[pos])
                            if posp < 2:
                                t.wait_ge(sV.h, ln_done)
                            if pos >= 2:
                                t.wait_ge(sV.h, ept[pos - 2])
                            for hc in range(16):
                                for half in range(2):
                                    i_ = t.matmul(
                                        psGU[2 * par + half][:, :],
                                        wg[2 * par + half][
                                            :, 128 * hc:128 * (hc + 1)],
                                        xh[:, 512 * hc:512 * (hc + 1)],
                                        start=(hc == 0), stop=(hc == 15))
                                    if hc == 15:
                                        sT.tag(i_)
                            mmt[pos] = sT.n
                        Tq(mm)
                        nxt = pos + 2
                        if (p == 0 and nxt < 46) or (p == 1 and
                                                     46 <= nxt < 88):
                            wq_emit(nxt)
                        if pos % 2 == 1:
                            fill_one()
                        if p == 1 and posp == 10:
                            dq_emit(16)
                            dq_emit(17)

                        def ep_s(s, pos=pos, par=par):
                            s.wait_ge(sT.h, mmt[pos])
                            sSc.tag(s.activation(out=sg[par][:, :],
                                                 in_=psGU[2 * par][:, :],
                                                 func=AF.Silu))
                        Sq(ep_s)
                        smark = sSc.n

                        def ep_v(v, pos=pos, posp=posp, itp=itp, par=par,
                                 smark=smark, h_guard=h_guard):
                            v.wait_ge(sSc.h, smark)
                            if posp == 0 and h_guard:
                                v.wait_ge(sT.h, h_guard)
                            sV.tag(v.tensor_tensor(
                                out=h_all[:, 512 * itp:512 * (itp + 1)],
                                in0=sg[par][:, :],
                                in1=psGU[2 * par + 1][:, :], op=ALU.mult))
                        Vq(ep_v)
                        ept[pos] = sV.n
                    h_done[p] = sV.n

                def down_pass(p):
                    m0 = 16 * p
                    if p == 0:
                        dq_emit(0)
                        dq_emit(1)
                    stq = {}
                    for mp in range(16):
                        m = m0 + mp
                        par = m % 2

                        def mm(t, m=m, mp=mp, par=par, p=p):
                            t.wait_ge(sWd[par].h, ddma[m])
                            if mp == 0:
                                t.wait_ge(sV.h, h_done[p])
                                if p == 0:
                                    t.wait_ge(sV.h, ln_res[1]["cmb"])
                            if m >= 2:
                                t.wait_ge(sV.h, dept[m - 2][0])
                                t.wait_ge(sSc.h, dept[m - 2][1])
                            for ic in range(44):
                                i_ = t.matmul(
                                    psD_[par][:, :],
                                    wd[par][:, 128 * ic:128 * (ic + 1)],
                                    h_all[:, 512 * ic:512 * (ic + 1)],
                                    start=(ic == 0), stop=(ic == 43))
                                if ic == 43:
                                    sT.tag(i_)
                            dmmt[m] = sT.n
                        Tq(mm)
                        nxt = m + 2
                        if (p == 0 and nxt < 16) or (p == 1 and
                                                     18 <= nxt < 32):
                            dq_emit(nxt)

                        def ep_s(s, m=m, par=par):
                            s.wait_ge(sT.h, dmmt[m])
                            sSc.tag(s.activation(out=sqd[par][:, :],
                                                 in_=psD_[par][:, :],
                                                 func=AF.Square))
                        Sq(ep_s)
                        smark = sSc.n

                        def ep_v(v, m=m, mp=mp, par=par,
                                 rg=state.get("radd_G", 0) if p else 0):
                            v.wait_ge(sT.h, dmmt[m])
                            if mp == 0 and rg:
                                v.wait_ge(sG.h, rg)
                            sV.tag(v.tensor_copy(
                                y_all[:, 512 * mp:512 * (mp + 1)],
                                psD_[par][:, :]))
                        Vq(ep_v)
                        dept[m] = (sV.n, smark)
                        stq[mp] = (par, smark)

                        if mp >= 1:
                            def st_mm(t, mp=mp, stq=stq):
                                par_, smark_ = stq[mp - 1]
                                t.wait_ge(sSc.h, smark_)
                                if mp - 1 == 0:
                                    rg = state.get("rr_ev", 0)
                                    if rg:
                                        t.wait_ge(sV.h, rg)
                                sT.tag(t.matmul(psSR[0:1, :], ones_c[:, :],
                                                sqd[par_][:, :],
                                                start=(mp - 1 == 0),
                                                stop=False))
                            Tq(st_mm)

                    def st_last(t, stq=stq):
                        par_, smark_ = stq[15]
                        t.wait_ge(sSc.h, smark_)
                        sT.tag(t.matmul(psSR[0:1, :], ones_c[:, :],
                                        sqd[par_][:, :],
                                        start=False, stop=True))
                    Tq(st_last)
                    tstats = sT.n
                    state["down_T"] = dmmt[m0 + 15]

                    def rms_s(s, tstats=tstats):
                        s.wait_ge(sT.h, tstats)
                        sSc.tag(s.activation(out=rrow[:, :],
                                             in_=psSR[0:1, :],
                                             func=AF.Sqrt,
                                             bias=epst[:1, 1:2],
                                             scale=4.0 / 2048))
                    Sq(rms_s)
                    srms = sSc.n

                    def rms_v(v, srms=srms):
                        v.wait_ge(sSc.h, srms)
                        sV.tag(v.reciprocal(rrow[:, :], rrow[:, :]))
                        v.wait_ge(sV.h, sV.n)
                        sV.tag(v.tensor_copy(rrowb[:, :], rrow[:, :]))
                    Vq(rms_v)
                    vrms = sV.n

                    def rms_t(t, vrms=vrms):
                        t.wait_ge(sV.h, vrms)
                        sT.tag(t.matmul(psSR[:, :], ones_r[:, :],
                                        rrowb[:, :], start=True, stop=True))
                    Tq(rms_t)
                    trr = sT.n

                    def rr_ev(v, trr=trr):
                        v.wait_ge(sT.h, trr)
                        sV.tag(v.tensor_copy(rrb_s[:, :], psSR[:, :]))
                    Vq(rr_ev)
                    state["rr_ev"] = sV.n

                def radd_pass(p):
                    vrr2 = state["rr_ev"]
                    lsq = ln_res[p]["sq"]
                    firste = {"v": True, "g": True}
                    for hc in range(16):
                        use_g = (hc % 2 == 1)
                        ecnt = sG if use_g else sV
                        tp = tpfG if use_g else tpfV

                        def radd(e, hc=hc, p=p, use_g=use_g, ecnt=ecnt,
                                 tp=tp, vrr2=vrr2, lsq=lsq):
                            kk = "g" if use_g else "v"
                            if firste[kk]:
                                firste[kk] = False
                                e.wait_ge(sSc.h, lsq)
                                if use_g:
                                    e.wait_ge(sV.h, vrr2)
                            else:
                                e.wait_ge(ecnt.h, ecnt.n)
                            ecnt.tag(e.tensor_tensor(
                                out=tp[:, :],
                                in0=y_all[:, 512 * hc:512 * (hc + 1)],
                                in1=rrb_s[:, :], op=ALU.mult))
                            e.wait_ge(ecnt.h, ecnt.n)
                            ecnt.tag(e.tensor_tensor(
                                out=xres[:, 1024 * hc + 512 * p:
                                         1024 * hc + 512 * (p + 1)],
                                in0=tp[:, :],
                                in1=xres[:, 1024 * hc + 512 * p:
                                         1024 * hc + 512 * (p + 1)],
                                op=ALU.add))
                        (Gq if use_g else Vq)(radd)
                        emark = ecnt.n
                        if out_ap is not None:
                            def xs(d, hc=hc, p=p, emark=emark, ecnt=ecnt):
                                d.wait_ge(ecnt.h, emark)
                                sOut[hc % 4].tag(d.dma_start(
                                    out=out_ap[128 * hc:128 * (hc + 1),
                                               512 * p:512 * (p + 1)],
                                    in_=xres[:, 1024 * hc + 512 * p:
                                             1024 * hc + 512 * (p + 1)]),
                                    16)
                            Sq(xs)
                    state["radd_G"] = sG.n

                ln_res[0] = ln_pass(0, xres, psD_[0][0:1, :],
                                    psD_[1][0:1, :], psGU[2][:, :],
                                    psGU[3][:, :], deps0, xh=xh)
                gu_pass(0)
                ln_res[1] = ln_pass(1, xres, psD_[0][0:1, :],
                                    psD_[1][0:1, :], psGU[2][:, :],
                                    psGU[3][:, :], deps0, xh=xh,
                                    sg_T=ln_res[0]["statsT"],
                                    pg_V=ln_res[0]["cmb"],
                                    bg_V=ept[43], ag_T=mmt[43])
                with contextlib.ExitStack() as stY:
                    EY = stY.enter_context
                    y_all = EY(nc.sbuf_tensor(uid("y"), [128, 16 * 512],
                                              BF16))
                    sqd = [EY(nc.sbuf_tensor(uid(f"sqd{i}"), [128, 512],
                                             BF16)) for i in range(2)]
                    rrow = EY(nc.sbuf_tensor(uid("rrow"), [1, 512], F32))
                    rrowb = EY(nc.sbuf_tensor(uid("rrowb"), [1, 512], BF16))
                    rrb_s = EY(nc.sbuf_tensor(uid("rrb"), [128, 512], F32))
                    tpfV = EY(nc.sbuf_tensor(uid("tpfV"), [128, 512], F32))
                    tpfG = EY(nc.sbuf_tensor(uid("tpfG"), [128, 512], F32))
                    down_pass(0)
                    radd_pass(0)
                    gu_pass(1)
                    down_pass(1)
                    radd_pass(1)
                return sV.n, sG.n

        res1_v, res1_g = ffn_phase(
            x_res, None, [(sLN[i], x0_marks[i]) for i in range(4)])
        while filler:
            fill_one()
        cst_st.close()
        barrier()

        def spill(d):
            for hc in range(16):
                sPz[1].tag(d.dma_start(
                    out=x_sp[128 * hc:128 * (hc + 1), :],
                    in_=x_res[:, 1024 * hc:1024 * (hc + 1)]), 16)
        Dq(spill)

        cc_x = [0, 0]
        with contextlib.ExitStack() as stL2:
            EL2 = stL2.enter_context
            psX_ = EL2(nc.psum_tensor("ps2X", [1, 512], F32))
            psX2_ = EL2(nc.psum_tensor("ps2X2", [1, 512], F32))
            psM_ = EL2(nc.psum_tensor("ps2M", [128, 512], F32))
            psR_ = EL2(nc.psum_tensor("ps2R", [128, 512], F32))
            l20 = ln_pass(0, x_res, psX_[:, :], psX2_[:, :], psM_[:, :],
                          psR_[:, :], [], piece_ap=xh_p[0])
            dp0 = [sPx[0].n, sPx[1].n]

            def p6_cc0(g):
                for i_ in range(2):
                    g.wait_ge(sPx[i_].h, dp0[i_])
                sCC.tag(g.collective_compute(
                    "AllGather", ALU.bypass,
                    replica_groups=[list(range(8))],
                    ins=[xh_p[0]], outs=[g_xs[0]]), 1)
            Gq(p6_cc0)
            cc_x[0] = sCC.n
            l21 = ln_pass(1, x_res, psX_[:, :], psX2_[:, :], psM_[:, :],
                          psR_[:, :], [], piece_ap=xh_p[1],
                          sg_T=l20["statsT"], pg_V=l20["cmb"],
                          bg_V=l20["done"])
        dp1 = [sPx[0].n, sPx[1].n]

        def p6_cc1(g):
            for i_ in range(2):
                g.wait_ge(sPx[i_].h, dp1[i_])
            sCC.tag(g.collective_compute(
                "AllGather", ALU.bypass, replica_groups=[list(range(8))],
                ins=[xh_p[1]], outs=[g_xs[1]]), 1)
        Gq(p6_cc1)
        cc_x[1] = sCC.n
        barrier(ex=(sCC,))
        xr_st.close()

        cc_a2a = [0, 0]
        with contextlib.ExitStack() as stO:
            EO = stO.enter_context
            o_tm = [EO(nc.sbuf_tensor(f"otm{i}", [128, 4096], BF16))
                    for i in range(4)]
            with contextlib.ExitStack() as stAtt:
                EA = stAtt.enter_context
                qt = [EA(nc.sbuf_tensor(f"qt{i}", [128, T], BF16))
                      for i in range(2)]
                kt = [EA(nc.sbuf_tensor(f"kt{i}", [128, T], BF16))
                      for i in range(2)]
                vtm = [EA(nc.sbuf_tensor(f"vtm{i}", [128, T], BF16))
                       for i in range(2)]
                bias_u = [EA(nc.sbuf_tensor(f"biasu{i}", [128, 1], F32))
                          for i in range(4)]

                with contextlib.ExitStack() as stC1:
                    EC = stC1.enter_context
                    wqb = EC(nc.sbuf_tensor("wqb", [128, 16 * 6 * HD], BF16))
                    xg = [EC(nc.sbuf_tensor(f"xg{i}", [128, 16 * 512], BF16))
                          for i in range(2)]
                    vstg = EC(nc.sbuf_tensor("vstg", [128, 1024], BF16))
                    psQ = [EC(nc.psum_tensor(f"psQ{i}", [128, 512], F32))
                           for i in range(2)]
                    psVT = [EC(nc.psum_tensor(f"psVT{i}", [128, 512], F32))
                            for i in range(4)]

                    def wql(d):
                        d.wait_ge(sXr[1].h, wq_cast_total)
                        for hc in range(16):
                            sWq.tag(d.dma_start(
                                out=wqb[:, 6 * HD * hc:6 * HD * (hc + 1)],
                                in_=wq_bf[128 * hc:128 * (hc + 1), :]), 16)
                    Dq(wql)
                    wq_marks = sWq.n

                    qkv_mm = {}
                    ev_state = {}
                    xg_dma = {}
                    XG_ORDER = [2 * r + tb for tb in range(2)
                                for r in range(8)]

                    def xg_tt(d, pos):
                        par = pos % 2
                        if pos >= 2:
                            d.wait_ge(sT.h, qkv_mm[pos - 2])
                        tt = XG_ORDER[pos]
                        r, tb = tt // 2, tt % 2
                        d.wait_ge(sCC.h, cc_x[tb])
                        for hc in range(16):
                            sXg[par].tag(d.dma_start(
                                out=xg[par][:, 512 * hc:512 * (hc + 1)],
                                in_=g_xs[tb][2048 * r + 128 * hc:
                                             2048 * r + 128 * (hc + 1),
                                             :]), 16)
                        xg_dma[pos] = sXg[par].n

                    def xg_pre(d):
                        xg_tt(d, 0)
                        xg_tt(d, 1)
                    Dq(xg_pre)

                    for pos in range(16):
                        tt = XG_ORDER[pos]
                        par = pos % 2
                        for o in range(6):
                            opar = o % 2
                            eprev = ev_state.get(opar, 0)

                            def mm(t, tt=tt, par=par, o=o, opar=opar,
                                   eprev=eprev, wq_marks=wq_marks,
                                   pos=pos):
                                t.wait_ge(sXg[par].h, xg_dma[pos])
                                t.wait_ge(sWq.h, wq_marks)
                                if eprev:
                                    t.wait_ge(sV.h, eprev)
                                for hc in range(16):
                                    i_ = t.matmul(
                                        psQ[opar][:, :],
                                        wqb[:, 6 * HD * hc + 128 * o:
                                            6 * HD * hc + 128 * (o + 1)],
                                        xg[par][:, 512 * hc:512 * (hc + 1)],
                                        start=(hc == 0), stop=(hc == 15))
                                    if hc == 15:
                                        sT.tag(i_)
                            Tq(mm)
                            tmark = sT.n
                            if o == 5:
                                qkv_mm[pos] = tmark

                            def ev(v, tt=tt, o=o, opar=opar, tmark=tmark):
                                v.wait_ge(sT.h, tmark)
                                hl = o % 2
                                if o < 2:
                                    dd = qt[hl][:, 512 * tt:512 * (tt + 1)]
                                elif o < 4:
                                    dd = kt[hl][:, 512 * tt:512 * (tt + 1)]
                                else:
                                    dd = vstg[:, 512 * hl:512 * (hl + 1)]
                                sV.tag(v.tensor_copy(dd, psQ[opar][:, :]))
                            Vq(ev)
                            ev_state[opar] = sV.n
                        if pos + 2 < 16:
                            Dq(lambda d, pos=pos: xg_tt(d, pos + 2))
                        vmark = sV.n
                        for hl in range(2):
                            vtprev = ev_state.get("vt", 0)

                            def vtr(t, tt=tt, hl=hl, vmark=vmark,
                                    vtprev=vtprev):
                                t.wait_ge(sV.h, max(vmark, vtprev))
                                for ss in range(4):
                                    sT.tag(t.matmul(
                                        psVT[ss][:, :128],
                                        vstg[:, 512 * hl + 128 * ss:
                                             512 * hl + 128 * (ss + 1)],
                                        ident_b[:, :],
                                        start=True, stop=True))
                            Tq(vtr)
                            tmark2 = sT.n

                            def vev(v, tt=tt, hl=hl, tmark2=tmark2):
                                v.wait_ge(sT.h, tmark2)
                                for ss in range(4):
                                    sV.tag(v.tensor_copy(
                                        vtm[hl][:, 512 * tt + 128 * ss:
                                                512 * tt + 128 * (ss + 1)],
                                        psVT[ss][:, :128]))
                            Vq(vev)
                            ev_state["vt"] = sV.n
                    barrier()

                with contextlib.ExitStack() as stN:
                    EN = stN.enter_context
                    nsq = [EN(nc.sbuf_tensor(f"nsq{i}", [128, 512], BF16))
                           for i in range(4)]
                    nrow = EN(nc.sbuf_tensor("nrow", [1, 8], F32))
                    psN4 = [EN(nc.psum_tensor(f"psN4_{i}", [1, 512], F32))
                            for i in range(4)]
                    nmx = EN(nc.sbuf_tensor("nmx", [1, 16], F32))
                    ub1 = EN(nc.sbuf_tensor("ub1", [1, 4], F32))
                    ub1b = EN(nc.sbuf_tensor("ub1b", [1, 4], BF16))
                    psN = EN(nc.psum_tensor("psN", [1, 512], F32))
                    psB = EN(nc.psum_tensor("psB", [128, 4], F32))
                    n_state = {}
                    for u in range(4):
                        hl, b = u // 2, u % 2
                        for which, tens in ((0, qt[hl]), (1, kt[hl])):
                            for g4 in range(2):
                                kbs = list(range(4 * g4, 4 * g4 + 4))
                                tprev = n_state.get("nT", 0)

                                def nq(s, kbs=kbs, tens=tens, b=b,
                                       tprev=tprev):
                                    if tprev:
                                        s.wait_ge(sT.h, tprev)
                                    for kblk in kbs:
                                        sSc.tag(s.activation(
                                            out=nsq[kblk % 4][:, :],
                                            in_=tens[:, 4096 * b +
                                                     512 * kblk:
                                                     4096 * b +
                                                     512 * (kblk + 1)],
                                            func=AF.Square))
                                Sq(nq)
                                smark = sSc.n
                                vprev = n_state.get("nmax", 0)

                                def nt(t, kbs=kbs, smark=smark,
                                       vprev=vprev):
                                    t.wait_ge(sSc.h, smark)
                                    if vprev:
                                        t.wait_ge(sV.h, vprev)
                                    for kblk in kbs:
                                        sT.tag(t.matmul(
                                            psN4[kblk % 4][:, :],
                                            ones_c[:, :],
                                            nsq[kblk % 4][:, :],
                                            start=True, stop=True))
                                Tq(nt)
                                n_state["nT"] = sT.n
                                tmark = sT.n

                                def nv(v, kbs=kbs, tmark=tmark):
                                    v.wait_ge(sT.h, tmark)
                                    v.wait_ge(sV.h, sV.n)
                                    for kblk in kbs:
                                        sV.tag(v.reduce_max(
                                            nrow[:, kblk:kblk + 1],
                                            psN4[kblk % 4][:, :],
                                            axis=AXX))
                                Vq(nv)
                                n_state["nmax"] = sV.n

                            def nred(v, u=u, which=which):
                                v.wait_ge(sV.h, sV.n)
                                sV.tag(v.reduce_max(
                                    nmx[:, 8 * which + u:8 * which + u + 1],
                                    nrow[:, :], axis=AXX))
                            Vq(nred)
                    vn = sV.n

                    def ubv(v, vn=vn):
                        v.wait_ge(sV.h, vn)
                        sV.tag(v.tensor_tensor(out=ub1[:, :], in0=nmx[:, 0:4],
                                               in1=nmx[:, 8:12], op=ALU.mult))
                    Vq(ubv)
                    vub = sV.n

                    def ubs(s, vub=vub):
                        s.wait_ge(sV.h, vub)
                        sSc.tag(s.activation(out=ub1[:, :], in_=ub1[:, :],
                                             func=AF.Sqrt))
                    Sq(ubs)
                    sub = sSc.n

                    def ubv2(v, sub=sub):
                        v.wait_ge(sSc.h, sub)
                        sV.tag(v.tensor_scalar(out=ub1b[:, :], in0=ub1[:, :],
                                               scalar1=-ISC, scalar2=None,
                                               op0=ALU.mult))
                    Vq(ubv2)
                    vub2 = sV.n

                    def ubt(t, vub2=vub2):
                        t.wait_ge(sV.h, vub2)
                        sT.tag(t.matmul(psB[:, :4], ones_r[:, :],
                                        ub1b[:, :], start=True, stop=True))
                    Tq(ubt)
                    tub = sT.n

                    def ubv3(v, tub=tub):
                        v.wait_ge(sT.h, tub)
                        for u in range(4):
                            sV.tag(v.tensor_copy(bias_u[u][:, :],
                                                 psB[:, u:u + 1]))
                    Vq(ubv3)
                barrier()

                with contextlib.ExitStack() as stA:
                    EA2 = stA.enter_context
                    pt = [EA2(nc.sbuf_tensor(f"pt{i}", [128, 512], BF16))
                          for i in range(4)]
                    srow = EA2(nc.sbuf_tensor("srow", [1, 512], F32))
                    rrow_a = EA2(nc.sbuf_tensor("rrow_a", [1, 512], BF16))
                    rb_s = EA2(nc.sbuf_tensor("rb_s", [128, 512], BF16))
                    psST = [EA2(nc.psum_tensor(f"psST{i}", [128, 512], F32))
                            for i in range(4)]
                    psAV = EA2(nc.psum_tensor("psAV", [128, 512], F32))
                    psSS = EA2(nc.psum_tensor("psSS", [1, 512], F32))
                    psRB = EA2(nc.psum_tensor("psRB", [128, 512], F32))
                    a_state = {}

                    for u in range(4):
                        hl, b = u // 2, u % 2
                        for qc in range(8):
                            nkc = 4 * qc + 4
                            ngr = nkc // 4
                            st_t = {}
                            exp_t = {}
                            zero_t = {}
                            mask_v = {}

                            def consume(t, kc, u=u, qc=qc, hl=hl, b=b,
                                        nkc=nkc, st_t=st_t, exp_t=exp_t,
                                        zero_t=zero_t):
                                par = kc % 4
                                t.wait_ge(sSc.h, exp_t[kc])
                                if kc in zero_t:
                                    t.wait_ge(sG.h, zero_t[kc])
                                if kc == 0:
                                    pv = a_state.get("av", 0)
                                    if pv:
                                        t.wait_ge(sV.h, pv)
                                    pv = a_state.get("ss", 0)
                                    if pv:
                                        t.wait_ge(sV.h, pv)
                                sT.tag(t.matmul(psSS[:, :], ones_c[:, :],
                                                pt[par][:, :],
                                                start=(kc == 0),
                                                stop=(kc == nkc - 1)))
                                sT.tag(t.matmul(
                                    psAV[:, :],
                                    vtm[hl][:, 4096 * b + 128 * kc:
                                            4096 * b + 128 * (kc + 1)],
                                    pt[par][:, :],
                                    start=(kc == 0),
                                    stop=(kc == nkc - 1)))

                            for g in range(ngr):
                                kcs = list(range(4 * g, 4 * g + 4))

                                def tsec(t, g=g, kcs=kcs, hl=hl, b=b, qc=qc,
                                         st_t=st_t):
                                    if g >= 1:
                                        for kc2 in range(4 * (g - 1), 4 * g):
                                            consume(t, kc2)
                                        a_state["ptg"] = sT.n
                                    for kc in kcs:
                                        prev = a_state.get(("st", kc % 4), 0)
                                        if prev:
                                            t.wait_ge(sSc.h, prev)
                                        sT.tag(t.matmul(
                                            psST[kc % 4][:, :],
                                            kt[hl][:, 4096 * b + 128 * kc:
                                                   4096 * b +
                                                   128 * (kc + 1)],
                                            qt[hl][:, 4096 * b + 512 * qc:
                                                   4096 * b +
                                                   512 * (qc + 1)],
                                            start=True, stop=True))
                                        st_t[kc] = sT.n
                                Tq(tsec)

                                if g == ngr - 1:
                                    def msec(v, kcs=kcs, qc=qc, st_t=st_t,
                                             mask_v=mask_v):
                                        v.wait_ge(sT.h, st_t[kcs[-1]])
                                        for kc in kcs:
                                            jd = kc - 4 * qc
                                            sV.tag(v.tensor_tensor(
                                                out=psST[kc % 4][
                                                    :, 128 * jd:
                                                    128 * (jd + 1)],
                                                in0=psST[kc % 4][
                                                    :, 128 * jd:
                                                    128 * (jd + 1)],
                                                in1=mask_t[:, :],
                                                op=ALU.add))
                                            mask_v[kc] = sV.n
                                    Vq(msec)

                                def esec(s, g=g, kcs=kcs, u=u, st_t=st_t,
                                         mask_v=mask_v, exp_t=exp_t):
                                    s.wait_ge(sT.h, st_t[kcs[-1]])
                                    if kcs[-1] in mask_v:
                                        s.wait_ge(sV.h, mask_v[kcs[-1]])
                                    prev = a_state.get("ptg", 0)
                                    if prev:
                                        s.wait_ge(sT.h, prev)
                                    for kc in kcs:
                                        sSc.tag(s.activation(
                                            out=pt[kc % 4][:, :],
                                            in_=psST[kc % 4][:, :],
                                            func=AF.Exp,
                                            bias=bias_u[u][:, 0:1],
                                            scale=ISC))
                                        a_state[("st", kc % 4)] = sSc.n
                                        exp_t[kc] = sSc.n
                                Sq(esec)

                                if g == ngr - 1:
                                    def zsec(gp, kcs=kcs, qc=qc, exp_t=exp_t,
                                             zero_t=zero_t):
                                        gp.wait_ge(sSc.h, exp_t[kcs[-1]])
                                        for kc in kcs:
                                            jd = kc - 4 * qc
                                            if 1 <= jd <= 3:
                                                sG.tag(gp.memset(
                                                    pt[kc % 4][
                                                        :, 0:128 * jd],
                                                    0.0))
                                                zero_t[kc] = sG.n
                                    Gq(zsec)

                            def lastc(t, nkc=nkc, consume=consume):
                                for kc2 in range(nkc - 4, nkc):
                                    consume(t, kc2)
                                a_state["ptg"] = sT.n
                            Tq(lastc)
                            tdone = sT.n

                            def ep_v1(v, tdone=tdone):
                                v.wait_ge(sT.h, tdone)
                                prev = a_state.get("rb_t", 0)
                                if prev:
                                    v.wait_ge(sT.h, prev)
                                sV.tag(v.tensor_copy(srow[:, :], psSS[:, :]))
                                v.wait_ge(sV.h, sV.n)
                                sV.tag(v.reciprocal(srow[:, :], srow[:, :]))
                                v.wait_ge(sV.h, sV.n)
                                sV.tag(v.tensor_copy(rrow_a[:, :],
                                                     srow[:, :]))
                            Vq(ep_v1)
                            a_state["ss"] = sV.n
                            vrr = sV.n

                            def ep_t(t, vrr=vrr):
                                t.wait_ge(sV.h, vrr)
                                prev = a_state.get("rb_v", 0)
                                if prev:
                                    t.wait_ge(sV.h, prev)
                                sT.tag(t.matmul(psRB[:, :], ones_r[:, :],
                                                rrow_a[:, :],
                                                start=True, stop=True))
                            Tq(ep_t)
                            a_state["rb_t"] = sT.n
                            trb = sT.n

                            def ep_v2(v, trb=trb, u=u, qc=qc, tdone=tdone):
                                v.wait_ge(sT.h, trb)
                                v.wait_ge(sT.h, tdone)
                                sV.tag(v.tensor_copy(rb_s[:, :], psRB[:, :]))
                                v.wait_ge(sV.h, sV.n)
                                sV.tag(v.scalar_tensor_tensor(
                                    out=o_tm[u][:, 512 * qc:512 * (qc + 1)],
                                    in0=psAV[:, :], scalar=1.0,
                                    in1=rb_s[:, :],
                                    op0=ALU.mult, op1=ALU.mult))
                                a_state["av"] = sV.n
                            Vq(ep_v2)
                            a_state["rb_v"] = sV.n
                        if u % 2 == 1:
                            hl_ = u // 2
                            o_mk = sV.n

                            def a2aw(d, hl_=hl_, o_mk=o_mk):
                                d.wait_ge(sV.h, o_mk)
                                for rd in range(8):
                                    b_, sl = rd // 4, rd % 4
                                    sPz[0].tag(d.dma_start(
                                        out=a2a_i[hl_][128 * rd:
                                                       128 * (rd + 1), :],
                                        in_=o_tm[2 * hl_ + b_][
                                            :, 1024 * sl:
                                            1024 * (sl + 1)]), 16)
                            Dq(a2aw)
                            pz_mk = sPz[0].n

                            def a2ac(g, hl_=hl_, pz_mk=pz_mk):
                                g.wait_ge(sPz[0].h, pz_mk)
                                with allow_ici_all_to_all():
                                    sCC.tag(g.collective_compute(
                                        "AllToAll", ALU.bypass,
                                        replica_groups=[list(range(8))],
                                        ins=[a2a_i[hl_]],
                                        outs=[a2a_o[hl_]]), 1)
                            Gq(a2ac)
                            cc_a2a[hl_] = sCC.n
                barrier()
            barrier(ex=(sCC,))

        xr2_st = contextlib.ExitStack()
        x_res2 = xr2_st.enter_context(
            nc.sbuf_tensor("x_res2", [128, 16 * TC], F32))

        with contextlib.ExitStack() as stP9:
            EP = stP9.enter_context
            wob = EP(nc.sbuf_tensor("wob", [128, 16 * HID], BF16))
            og = [EP(nc.sbuf_tensor(f"og{i}", [128, 16 * 512], BF16))
                  for i in range(2)]
            psZ = [EP(nc.psum_tensor(f"psZ{i}", [128, 512], F32))
                   for i in range(2)]

            def wol(d):
                d.wait_ge(sXr[1].h, all_cast_total)
                for hh in range(16):
                    sWo.tag(d.dma_start(
                        out=wob[:, HID * hh:HID * (hh + 1)],
                        in_=wo_bf[128 * hh:128 * (hh + 1), :]), 16)
            Dq(wol)
            wo_marks = sWo.n

            def x2_load(d):
                for hc in range(16):
                    sLN[0].tag(d.dma_start(
                        out=x_res2[:, 1024 * hc:1024 * (hc + 1)],
                        in_=x_sp[128 * hc:128 * (hc + 1), :]), 16)
            Dq(x2_load)
            x2_marks = sLN[0].n

            ogd = {}
            for tb in range(2):
                def ogl(d, tb=tb):
                    for par in range(2):
                        d.wait_ge(sCC.h, cc_a2a[par])
                        for hh in range(par, 16, 2):
                            sXg[tb].tag(d.dma_start(
                                out=og[tb][:, 512 * hh:512 * (hh + 1)],
                                in_=a2a_o[par][128 * (hh // 2):
                                               128 * (hh // 2) + 128,
                                               512 * tb:512 * (tb + 1)]),
                                16)
                Sq(ogl)
                ogd[tb] = sXg[tb].n
            z_state = {}
            first_z = [True]
            for tb in range(2):
                for ho in range(16):
                    par = ho % 2
                    eprev = z_state.get(("v", par), 0)

                    def zmm(t, tb=tb, ho=ho, par=par, eprev=eprev,
                            wo_marks=wo_marks):
                        t.wait_ge(sXg[tb].h, ogd[tb])
                        t.wait_ge(sWo.h, wo_marks)
                        if eprev:
                            t.wait_ge(sV.h, eprev)
                        for hh in range(16):
                            i_ = t.matmul(
                                psZ[par][:, :],
                                wob[:, HID * hh + 128 * ho:
                                    HID * hh + 128 * (ho + 1)],
                                og[tb][:, 512 * hh:512 * (hh + 1)],
                                start=(hh == 0), stop=(hh == 15))
                            if hh == 15:
                                sT.tag(i_)
                    Tq(zmm)
                    tmark = sT.n

                    def zadd(v, tb=tb, ho=ho, par=par, tmark=tmark):
                        if first_z[0]:
                            first_z[0] = False
                            v.wait_ge(sLN[0].h, x2_marks)
                        v.wait_ge(sT.h, tmark)
                        sV.tag(v.tensor_tensor(
                            out=x_res2[:, 1024 * ho + 512 * tb:
                                       1024 * ho + 512 * (tb + 1)],
                            in0=psZ[par][:, :],
                            in1=x_res2[:, 1024 * ho + 512 * tb:
                                       1024 * ho + 512 * (tb + 1)],
                            op=ALU.add))
                    Vq(zadd)
                    z_state[("v", par)] = sV.n
            x2_v_done = sV.n
        barrier()

        ffn_phase(x_res2, out_t, [(sV, x2_v_done)])
        xr2_st.close()

        nout = [c.n for c in sOut]

        def fin(g, nout=nout):
            for i in range(4):
                g.wait_ge(sOut[i].h, nout[i])
        Gq(fin)

    return nc


_PROG = None


def kernel(L0_attn_x, L0_attn_w_qkv, L0_attn_w_out, L0_ffn_gate_up_packed,
           L0_ffn_gate_up_scales, L0_ffn_down_packed, L0_ffn_down_scales):
    global _PROG
    x = np.asarray(L0_attn_x, dtype=np.float32).reshape(T, HID)
    wqkv = np.asarray(L0_attn_w_qkv, dtype=np.float32)
    wout = np.asarray(L0_attn_w_out, dtype=np.float32)
    gup = np.asarray(L0_ffn_gate_up_packed, dtype=np.int32)
    gus = np.asarray(L0_ffn_gate_up_scales)
    dnp = np.asarray(L0_ffn_down_packed, dtype=np.int32)
    dns = np.asarray(L0_ffn_down_scales)

    in_maps = []
    for c in range(NCORES):
        cols = []
        for which in range(3):
            for hl in range(2):
                h = 2 * c + hl
                base = (which * NH + h) * HD
                cols.extend(range(base, base + HD))
        in_maps.append({
            "x_t": np.ascontiguousarray(x[TC * c:TC * (c + 1)].T),
            "wqkv_sl": np.ascontiguousarray(wqkv[:, cols]),
            "wout_sl": wout,
            "gup_sl": np.ascontiguousarray(gup[OGU_C * c:OGU_C * (c + 1)]),
            "gus_sl": np.ascontiguousarray(gus[:, OGU_C * c:OGU_C * (c + 1)]),
            "dnp_sl": np.ascontiguousarray(dnp[DH_C * c:DH_C * (c + 1)]),
            "dns_sl": np.ascontiguousarray(dns[:, DH_C * c:DH_C * (c + 1)]),
        })

    if _PROG is None:
        _PROG = build_program()
    res = run_bass_kernel_spmd(_PROG, in_maps, core_ids=list(range(NCORES)))
    out = np.empty((T, HID), dtype=np.float32)
    for c in range(NCORES):
        out[TC * c:TC * (c + 1)] = res.results[c]["out_t"].T
    return out.reshape(B, S, HID)


# version 17
# speedup vs baseline: 1.0716x; 1.0663x over previous
import contextlib
import numpy as np
import concourse.bass as bass
import concourse.mybir as mybir
from concourse.bass_utils import run_bass_kernel_spmd
from concourse.aot_env import allow_ici_all_to_all

F32 = mybir.dt.float32
BF16 = mybir.dt.bfloat16
F16 = mybir.dt.float16
I32 = mybir.dt.int32
AF = mybir.ActivationFunctionType
ALU = mybir.AluOpType
AXX = mybir.AxisListType.X

HID = 2048
NH = 16
HD = 128
IFF = 5632
B = 2
S = 4096
T = 8192
TC = 1024
NCORES = 8
OGU = 2 * IFF
OGU_C = OGU // 8
NGT = OGU_C // 128
DH_C = HID // 8
NDT = DH_C // 128
ISC = float(1.0 / np.sqrt(128.0))
NEG = -1e9


class Cnt:
    __slots__ = ("h", "n")

    def __init__(self, h):
        self.h = h
        self.n = 0

    def tag(self, instr, k=1):
        instr.then_inc(self.h, k)
        self.n += k
        return self.n


def build_program():
    nc = bass.Bass("TRN2", target_bir_lowering=False, debug=False)
    uid_box = [0]

    def uid(s):
        uid_box[0] += 1
        return f"{s}_{uid_box[0]}"

    x_t = nc.dram_tensor("x_t", [HID, TC], F32, kind="ExternalInput").ap()
    wqkv = nc.dram_tensor("wqkv_sl", [HID, 6 * HD], F32, kind="ExternalInput").ap()
    wout = nc.dram_tensor("wout_sl", [HID, HID], F32, kind="ExternalInput").ap()
    gup = nc.dram_tensor("gup_sl", [OGU_C, 256], I32, kind="ExternalInput").ap()
    gus = nc.dram_tensor("gus_sl", [16, OGU_C], F16, kind="ExternalInput").ap()
    dnp = nc.dram_tensor("dnp_sl", [DH_C, 704], I32, kind="ExternalInput").ap()
    dns = nc.dram_tensor("dns_sl", [44, DH_C], F16, kind="ExternalInput").ap()
    out_t = nc.dram_tensor("out_t", [HID, TC], F32, kind="ExternalOutput").ap()

    GU_PARTS = ((0, 4), (4, 8), (8, 11))
    wgu_piece = [nc.dram_tensor(f"wgu_piece{i}",
                                [128, (hi - lo) * 2048], BF16).ap()
                 for i, (lo, hi) in enumerate(GU_PARTS)]
    g_gu = [nc.dram_tensor(f"g_gu{i}", [8 * 128, (hi - lo) * 2048], BF16).ap()
            for i, (lo, hi) in enumerate(GU_PARTS)]
    wd_piece = nc.dram_tensor("wd_piece", [128, NDT * 5632], BF16).ap()
    g_d = nc.dram_tensor("g_d", [8 * 128, NDT * 5632], BF16).ap()
    xh_p = [nc.dram_tensor(f"xh_p{i}", [HID, 512], BF16).ap()
            for i in range(2)]
    g_xs = [nc.dram_tensor(f"g_x{i}", [8 * HID, 512], BF16).ap()
            for i in range(2)]
    a2a_i = [nc.dram_tensor(f"a2a_i{i}", [8 * HD, TC], BF16).ap()
             for i in range(2)]
    a2a_o = [nc.dram_tensor(f"a2a_o{i}", [8 * HD, TC], BF16).ap()
             for i in range(2)]
    x_sp = nc.dram_tensor("x_sp", [HID, TC], F32).ap()
    wq_bf = nc.dram_tensor("wq_bf", [HID, 6 * HD], BF16).ap()
    wo_bf = nc.dram_tensor("wo_bf", [HID, HID], BF16).ap()

    with contextlib.ExitStack() as top:
        E = top.enter_context
        blk = E(nc.Block())

        sD = Cnt(E(nc.semaphore("sD")))
        sDg = Cnt(E(nc.semaphore("sDg")))
        sDs = Cnt(E(nc.semaphore("sDs")))
        sT = Cnt(E(nc.semaphore("sT")))
        sV = Cnt(E(nc.semaphore("sV")))
        sSc = Cnt(E(nc.semaphore("sSc")))
        sG = Cnt(E(nc.semaphore("sG")))
        sCC = Cnt(E(nc.semaphore("sCC")))
        sOut = [Cnt(E(nc.semaphore(f"sOut{i}"))) for i in range(4)]
        sLN = [Cnt(E(nc.semaphore(f"sLN{i}"))) for i in range(4)]
        sWq = Cnt(E(nc.semaphore("sWq")))
        sWo = Cnt(E(nc.semaphore("sWo")))
        sWg = [Cnt(E(nc.semaphore(f"sWg{i}"))) for i in range(2)]
        sWd = [Cnt(E(nc.semaphore(f"sWd{i}"))) for i in range(2)]
        sXg = [Cnt(E(nc.semaphore(f"sXg{i}"))) for i in range(2)]
        sPp = [Cnt(E(nc.semaphore(f"sPp{i}"))) for i in range(2)]
        sPx = [Cnt(E(nc.semaphore(f"sPx{i}"))) for i in range(4)]
        sPz = [Cnt(E(nc.semaphore(f"sPz{i}"))) for i in range(2)]
        sXr = [Cnt(E(nc.semaphore(f"sXr{i}"))) for i in range(4)]
        sSt1 = [Cnt(E(nc.semaphore(f"sSt1{i}"))) for i in range(4)]
        sZr = [Cnt(E(nc.semaphore(f"sZr{i}"))) for i in range(4)]
        sSt2 = [Cnt(E(nc.semaphore(f"sSt2{i}"))) for i in range(4)]

        ALL_ENG = ("tensor", "vector", "scalar", "gpsimd", "sync")
        eng_cnt = {"tensor": sT, "vector": sV, "scalar": sSc, "gpsimd": sG,
                   "sync": sD}
        ALL_CNT_L = ([sD, sDg, sDs, sT, sV, sSc, sG, sCC, sWq, sWo]
                     + sOut + sWg + sWd + sXg + sPp + sPx + sPz + sXr
                     + sSt1 + sZr + sSt2 + sLN)

        def Tq(fn):
            blk.tensor(fn)

        def Vq(fn):
            blk.vector(fn)

        def Sq(fn):
            blk.scalar(fn)

        def Gq(fn):
            blk.gpsimd(fn)

        def Dq(fn):
            blk.sync(fn)


        def barrier(ex=()):
            snap = [(c, c.n) for c in ALL_CNT_L if c.n and c not in ex]
            for en in ALL_ENG:
                def wt(e, en=en, snap=snap):
                    own = eng_cnt[en]
                    for c, v in snap:
                        if c is not own:
                            e.wait_ge(c.h, v)
                getattr(blk, en)(wt)

        cc_box = [None, 0]

        ident_f = E(nc.sbuf_tensor("ident_f", [128, 128], F32))
        ident_b = E(nc.sbuf_tensor("ident_b", [128, 128], BF16))
        mask_t = E(nc.sbuf_tensor("mask_t", [128, 128], F32))
        ones_c = E(nc.sbuf_tensor("ones_c", [128, 1], BF16))
        ones_r = E(nc.sbuf_tensor("ones_r", [1, 128], BF16))
        epst = E(nc.sbuf_tensor("epst", [128, 2], F32))

        def consts(g):
            sG.tag(g.memset(ident_f[:, :], 0.0))
            g.wait_ge(sG.h, sG.n)
            sG.tag(g.affine_select(
                out=ident_f[:, :], in_=ident_f[:, :],
                compare_op=ALU.not_equal, fill=1.0, base=0,
                pattern=[[-1, 128]], channel_multiplier=1))
            sG.tag(g.memset(mask_t[:, :], 0.0))
            g.wait_ge(sG.h, sG.n)
            sG.tag(g.affine_select(
                out=mask_t[:, :], in_=mask_t[:, :],
                compare_op=ALU.is_ge, fill=NEG, base=0,
                pattern=[[1, 128]], channel_multiplier=-1))
            sG.tag(g.memset(ones_c[:, :], 1.0))
            sG.tag(g.memset(ones_r[:, :], 1.0))
            sG.tag(g.memset(epst[:, 0:1], 1e-5))
            sG.tag(g.memset(epst[:, 1:2], 4e-6))
            g.wait_ge(sG.h, sG.n)
        Gq(consts)
        g_consts = sG.n

        def cast_ident(v):
            v.wait_ge(sG.h, g_consts)
            sV.tag(v.tensor_copy(ident_b[:, :], ident_f[:, :]))
        Vq(cast_ident)
        g_after = sG.n
        v_after = sV.n

        def pre_t(t):
            t.wait_ge(sG.h, g_after)
            t.wait_ge(sV.h, v_after)
        Tq(pre_t)

        def pre_s(s):
            s.wait_ge(sG.h, g_after)
        Sq(pre_s)
        barrier()

        xr_st = contextlib.ExitStack()
        x_res = xr_st.enter_context(
            nc.sbuf_tensor("x_res", [128, 16 * TC], F32))

        def x0_load(d):
            for hc in range(16):
                sLN[hc % 4].tag(d.dma_start(
                    out=x_res[:, 1024 * hc:1024 * (hc + 1)],
                    in_=x_t[128 * hc:128 * (hc + 1), :]), 16)
        Dq(x0_load)
        x0_marks = [c.n for c in sLN]

        cst_st = contextlib.ExitStack()
        cwf = cst_st.enter_context(nc.sbuf_tensor("cwf", [128, 1024], F32))
        cwb = cst_st.enter_context(nc.sbuf_tensor("cwb", [128, 1024], BF16))

        with contextlib.ExitStack() as st2:
            E2 = st2.enter_context
            gup_s = E2(nc.sbuf_tensor("gup_s", [128, NGT * 256], I32))
            dnp_s = E2(nc.sbuf_tensor("dnp_s", [128, NDT * 704], I32))
            gus_s = E2(nc.sbuf_tensor("gus_s", [16, OGU_C], F16))
            dns_s = E2(nc.sbuf_tensor("dns_s", [44, DH_C], F16))
            gusf = E2(nc.sbuf_tensor("gusf", [16, OGU_C], F32))
            dnsf = E2(nc.sbuf_tensor("dnsf", [44, DH_C], F32))
            scT_gu = E2(nc.sbuf_tensor("scT_gu", [128, NGT * 16], F32))
            scT_d = E2(nc.sbuf_tensor("scT_d", [128, NDT * 44], F32))
            wnat = E2(nc.sbuf_tensor("wnat", [128, IFF], BF16))
            pstage = [E2(nc.sbuf_tensor(f"pstage{i}", [128, 512], BF16))
                      for i in range(2)]
            wrk = {}
            for en in range(3):
                wrk[en] = {k: E2(nc.sbuf_tensor(f"{k}_{en}", [128, 704], I32))
                           for k in ("w2", "fk2", "fb", "s2", "vs")}
                for k in ("m1", "aa", "fm", "vv"):
                    wrk[en][k] = E2(nc.sbuf_tensor(f"{k}_{en}", [128, 1408],
                                                   BF16))
            psSC = E2(nc.psum_tensor("psSC", [128, 64], F32))
            psW = [E2(nc.psum_tensor(f"psW{i}", [128, 512], F32))
                   for i in range(4)]

            cast_state = {"g": 0, "wq": 0, "wo": 0}

            def cast_job(src_ap, dst_ap, cols):
                def job(g):
                    if cast_state["g"]:
                        g.wait_ge(sG.h, cast_state["g"])
                    sXr[0].tag(g.dma_start(out=cwf[:, :cols], in_=src_ap),
                               16)
                    g.wait_ge(sXr[0].h, sXr[0].n)
                    if cast_state["wo"]:
                        g.wait_ge(sXr[1].h, cast_state["wo"])
                    sG.tag(g.tensor_copy(cwb[:, :cols], cwf[:, :cols]))
                    cast_state["g"] = sG.n
                    g.wait_ge(sG.h, cast_state["g"])
                    sXr[1].tag(g.dma_start(out=dst_ap, in_=cwb[:, :cols]),
                               16)
                    cast_state["wo"] = sXr[1].n
                return job

            filler = []
            for hc in range(16):
                filler.append(cast_job(wqkv[128 * hc:128 * (hc + 1), :],
                                       wq_bf[128 * hc:128 * (hc + 1), :],
                                       6 * HD))
            for hh in range(16):
                for cl in range(2):
                    filler.append(cast_job(
                        wout[128 * hh:128 * (hh + 1),
                             1024 * cl:1024 * (cl + 1)],
                        wo_bf[128 * hh:128 * (hh + 1),
                              1024 * cl:1024 * (cl + 1)], 1024))
            wq_cast_total = 16 * 16
            all_cast_total = 16 * len(filler)

            def fill_one():
                if filler:
                    Gq(filler.pop(0))

            def p2_in(d):
                for ot in range(NGT):
                    sD.tag(d.dma_start(out=gup_s[:, 256 * ot:256 * (ot + 1)],
                                       in_=gup[128 * ot:128 * (ot + 1), :]),
                           16)
                for ht in range(NDT):
                    sD.tag(d.dma_start(out=dnp_s[:, 704 * ht:704 * (ht + 1)],
                                       in_=dnp[128 * ht:128 * (ht + 1), :]),
                           16)
                sD.tag(d.dma_start(out=gus_s[:, :], in_=gus[:, :]), 16)
                sD.tag(d.dma_start(out=dns_s[:, :], in_=dns[:, :]), 16)
            Dq(p2_in)
            d_in = sD.n

            def p2_scv(v):
                v.wait_ge(sD.h, d_in)
                sV.tag(v.tensor_scalar(out=gusf[:, :], in0=gus_s[:, :],
                                       scalar1=256.0, scalar2=None,
                                       op0=ALU.mult))
                sV.tag(v.tensor_scalar(out=dnsf[:, :], in0=dns_s[:, :],
                                       scalar1=256.0, scalar2=None,
                                       op0=ALU.mult))
            Vq(p2_scv)
            v_scf = sV.n

            sct_jobs = ([("gu", ot) for ot in range(NGT)] +
                        [("d", ht) for ht in range(NDT)])
            sct_state = {"v": 0}
            for kind, idx in sct_jobs:
                vprev = sct_state["v"]

                def tmm(t, kind=kind, idx=idx, vprev=vprev):
                    t.wait_ge(sV.h, max(v_scf, vprev))
                    if kind == "gu":
                        sT.tag(t.matmul(psSC[:, :16],
                                        gusf[:, 128 * idx:128 * (idx + 1)],
                                        ident_f[:16, :16],
                                        start=True, stop=True))
                    else:
                        sT.tag(t.matmul(psSC[:, :44],
                                        dnsf[:, 128 * idx:128 * (idx + 1)],
                                        ident_f[:44, :44],
                                        start=True, stop=True))
                Tq(tmm)
                tmark = sT.n

                def vcp(v, kind=kind, idx=idx, tmark=tmark):
                    v.wait_ge(sT.h, tmark)
                    if kind == "gu":
                        sV.tag(v.tensor_copy(
                            scT_gu[:, 16 * idx:16 * (idx + 1)], psSC[:, :16]))
                    else:
                        sV.tag(v.tensor_copy(
                            scT_d[:, 44 * idx:44 * (idx + 1)], psSC[:, :44]))
                Vq(vcp)
                sct_state["v"] = sV.n

            def dq_chain(e, wtile, W, cnt):
                last = [0, 0, 0]

                def step(c, instr):
                    cnt.tag(instr)
                    last[c] = cnt.n

                def wt(c):
                    if last[c]:
                        e.wait_ge(cnt.h, last[c])

                def ops_for(jp):
                    c = jp % 3
                    w = wrk[c]
                    o = []
                    if jp == 0:
                        o.append(lambda: (wt(c), step(c, e.tensor_scalar(
                            out=w["w2"][:, :W], in0=wtile, scalar1=0x70007,
                            scalar2=None, op0=ALU.bitwise_and))))
                    else:
                        o.append(lambda: (wt(c), step(c, e.tensor_scalar(
                            out=w["w2"][:, :W], in0=wtile, scalar1=4 * jp,
                            scalar2=0x70007,
                            op0=ALU.logical_shift_right,
                            op1=ALU.bitwise_and))))
                    o.append(lambda: (wt(c), step(c, e.tensor_scalar(
                        out=w["fk2"][:, :W], in0=w["w2"][:, :W],
                        scalar1=0x43004300, scalar2=None,
                        op0=ALU.bitwise_or))))
                    o.append(lambda: (wt(c), step(c, e.tensor_scalar(
                        out=w["fb"][:, :W], in0=w["w2"][:, :W],
                        scalar1=64, scalar2=0x3B003B00,
                        op0=ALU.mult, op1=ALU.add))))
                    o.append(lambda: (wt(c), step(c, e.tensor_scalar(
                        out=w["s2"][:, :W], in0=wtile, scalar1=12 - 4 * jp,
                        scalar2=-0x7FFF8000,
                        op0=ALU.logical_shift_left,
                        op1=ALU.bitwise_and))))
                    fkb = w["fk2"][:, :W].bitcast(BF16)
                    o.append(lambda: (wt(c), step(c, e.tensor_scalar(
                        out=w["m1"][:, :2 * W], in0=fkb, scalar1=130.0,
                        scalar2=129.0, op0=ALU.min, op1=ALU.subtract))))
                    o.append(lambda: (wt(c), step(c, e.tensor_scalar(
                        out=w["aa"][:, :2 * W], in0=fkb,
                        scalar1=0.001953125, scalar2=0.25,
                        op0=ALU.mult, op1=ALU.subtract))))
                    o.append(lambda: (wt(c), step(c, e.tensor_tensor(
                        out=w["fm"][:, :2 * W],
                        in0=w["fb"][:, :W].bitcast(BF16),
                        in1=w["m1"][:, :2 * W], op=ALU.mult))))
                    o.append(lambda: (wt(c), step(c, e.tensor_tensor(
                        out=w["vv"][:, :2 * W], in0=w["fm"][:, :2 * W],
                        in1=w["aa"][:, :2 * W], op=ALU.max))))
                    o.append(lambda: (wt(c), step(c, e.tensor_tensor(
                        out=w["vs"][:, :W],
                        in0=w["vv"][:, :2 * W].bitcast(I32),
                        in1=w["s2"][:, :W], op=ALU.bitwise_or))))
                    vb = w["vs"][:, :W].bitcast(BF16)
                    o.append(lambda: (wt(c), step(c, e.tensor_copy(
                        wnat[:, jp:8 * W:8], vb[:, 0:2 * W:2]))))
                    o.append(lambda: (wt(c), step(c, e.tensor_copy(
                        wnat[:, jp + 4:8 * W:8], vb[:, 1:2 * W:2]))))
                    return o
                for grp in ((0, 1, 2), (3,)):
                    chains = [ops_for(jp) for jp in grp]
                    for s in range(max(len(ch) for ch in chains)):
                        for ch in chains:
                            if s < len(ch):
                                ch[s]()

            wt_state = {}

            def do_tile(wtile, W, scT, sc_off, ngrp, piece_ap, pcol,
                        tr_prev_v):
                def dv(v, wtile=wtile, W=W, tr_prev_v=tr_prev_v):
                    v.wait_ge(sT.h, tr_prev_v)
                    dq_chain(v, wtile, W, sV)
                Vq(dv)

                def sc(v, scT=scT, sc_off=sc_off, ngrp=ngrp):
                    v.wait_ge(sV.h, sV.n)
                    for g in range(ngrp):
                        sV.tag(v.tensor_scalar(
                            out=wnat[:, 128 * g:128 * (g + 1)],
                            in0=wnat[:, 128 * g:128 * (g + 1)],
                            scalar1=scT[:, sc_off + g:sc_off + g + 1],
                            scalar2=None, op0=ALU.mult))
                        v.wait_ge(sV.h, sV.n)
                Vq(sc)
                scdone = sV.n

                for g0 in range(0, ngrp, 4):
                    par = (g0 // 4) % 2
                    vprev = wt_state.get("evac", 0)

                    def tmm(t, g0=g0, vprev=vprev, scdone=scdone):
                        t.wait_ge(sV.h, max(vprev, scdone))
                        for k in range(4):
                            g = g0 + k
                            sT.tag(t.matmul(psW[k][:, :128],
                                            wnat[:, 128 * g:128 * (g + 1)],
                                            ident_b[:, :],
                                            start=True, stop=True))
                    Tq(tmm)
                    tmark = sT.n
                    dprev = wt_state.get(("dma", par), 0)

                    def vev(v, par=par, tmark=tmark, dprev=dprev):
                        v.wait_ge(sT.h, tmark)
                        if dprev:
                            v.wait_ge(sPp[par].h, dprev)
                        for k in range(4):
                            sV.tag(v.tensor_copy(
                                pstage[par][:, 128 * k:128 * (k + 1)],
                                psW[k][:, :128]))
                    Vq(vev)
                    wt_state["evac"] = sV.n
                    wt_state[("evac", par)] = sV.n
                    vmark = sV.n

                    def dst(d, g0=g0, par=par, vmark=vmark,
                            piece_ap=piece_ap, pcol=pcol):
                        d.wait_ge(sV.h, vmark)
                        sPp[par].tag(d.dma_start(
                            out=piece_ap[:, pcol + 128 * g0:
                                         pcol + 128 * g0 + 512],
                            in_=pstage[par][:, :]), 16)
                    Dq(dst)
                    wt_state[("dma", par)] = sPp[par].n
                return sT.n

            cc_gu_parts = []
            tr_prev = 0
            for ot in range(NGT):
                part = 0 if ot < 4 else (1 if ot < 8 else 2)
                lo_p = GU_PARTS[part][0]
                tr_prev = do_tile(gup_s[:, 256 * ot:256 * (ot + 1)], 256,
                                  scT_gu, 16 * ot, 16, wgu_piece[part],
                                  2048 * (ot - lo_p), tr_prev)
                if ot + 1 in (4, 8):
                    pi = 0 if ot + 1 == 4 else 1
                    pp = [sPp[0].n, sPp[1].n]

                    def p2_ccp(g, pi=pi, pp=pp):
                        g.wait_ge(sPp[0].h, pp[0])
                        g.wait_ge(sPp[1].h, pp[1])
                        sCC.tag(g.collective_compute(
                            "AllGather", ALU.bypass,
                            replica_groups=[list(range(8))],
                            ins=[wgu_piece[pi]], outs=[g_gu[pi]]), 1)
                    Gq(p2_ccp)
                    cc_gu_parts.append((ot + 1, sCC.n))
            gu_pieces = [sPp[0].n, sPp[1].n]

            def p2_cc2(g):
                g.wait_ge(sPp[0].h, gu_pieces[0])
                g.wait_ge(sPp[1].h, gu_pieces[1])
                sCC.tag(g.collective_compute(
                    "AllGather", ALU.bypass, replica_groups=[list(range(8))],
                    ins=[wgu_piece[2]], outs=[g_gu[2]]), 1)
            Gq(p2_cc2)
            cc_gu_parts.append((11, sCC.n))
            for ht in range(NDT):
                tr_prev = do_tile(dnp_s[:, 704 * ht:704 * (ht + 1)], 704,
                                  scT_d, 44 * ht, 44, wd_piece, 5632 * ht,
                                  tr_prev)
            dma_pieces = [sPp[0].n, sPp[1].n]

            def p2_cc(g):
                g.wait_ge(sPp[0].h, dma_pieces[0])
                g.wait_ge(sPp[1].h, dma_pieces[1])
                sCC.tag(g.collective_compute(
                    "AllGather", ALU.bypass, replica_groups=[list(range(8))],
                    ins=[wd_piece], outs=[g_d]), 1)
            Gq(p2_cc)
            cc_box[0] = cc_gu_parts
            cc_box[1] = sCC.n
        barrier(ex=(sCC,))
        def ln_pass(tb, xres, psX, psX2, psM, psR, deps, xh=None,
                    piece_ap=None, sg_T=0, pg_V=0, bg_V=0, ag_T=0):
            res = {}
            with contextlib.ExitStack() as stL:
                EL = stL.enter_context

                def xsl(hc):
                    return xres[:, 1024 * hc + 512 * tb:
                                1024 * hc + 512 * (tb + 1)]
                xb = [EL(nc.sbuf_tensor(uid(f"ln_xb{i}"), [128, 512], BF16))
                      for i in range(4)]
                sq = [EL(nc.sbuf_tensor(uid(f"ln_sq{i}"), [128, 512], BF16))
                      for i in range(4)]
                rw = [EL(nc.sbuf_tensor(uid(f"ln_rw{i}"), [1, 512], F32))
                      for i in range(4)]
                rwb = [EL(nc.sbuf_tensor(uid(f"ln_rwb{i}"), [1, 512], BF16))
                       for i in range(2)]
                tmpf = [EL(nc.sbuf_tensor(uid(f"ln_tmp{i}"), [128, 512], F32))
                        for i in range(2)]
                xst = ([EL(nc.sbuf_tensor(uid(f"ln_xst{i}"), [128, 512],
                                          BF16)) for i in range(2)]
                       if piece_ap is not None else None)
                stats_t = {}
                pc_state = {0: sPx[0].n, 1: sPx[1].n}
                first = {"v": True, "s": True, "t": True}
                for g4 in range(4):
                    hcs = list(range(4 * g4, 4 * g4 + 4))
                    tprev = stats_t.get(g4 - 1, 0)

                    def cst(v, hcs=hcs, tprev=tprev):
                        if first["v"]:
                            first["v"] = False
                            for cnt_, val_ in deps:
                                if val_:
                                    v.wait_ge(cnt_.h, val_)
                            if sg_T:
                                v.wait_ge(sT.h, sg_T)
                        if tprev:
                            v.wait_ge(sT.h, tprev)
                        for hc in hcs:
                            sV.tag(v.tensor_copy(xb[hc % 4][:, :], xsl(hc)))
                    Vq(cst)
                    vmark = sV.n

                    def sqr(s, hcs=hcs, tprev=tprev):
                        if first["s"]:
                            first["s"] = False
                            for cnt_, val_ in deps:
                                if val_:
                                    s.wait_ge(cnt_.h, val_)
                            if sg_T:
                                s.wait_ge(sT.h, sg_T)
                        if tprev:
                            s.wait_ge(sT.h, tprev)
                        for hc in hcs:
                            sSc.tag(s.activation(out=sq[hc % 4][:, :],
                                                 in_=xsl(hc),
                                                 func=AF.Square))
                    Sq(sqr)
                    smark = sSc.n

                    def smm(t, hcs=hcs, vmark=vmark, smark=smark):
                        if first["t"]:
                            first["t"] = False
                            if pg_V:
                                t.wait_ge(sV.h, pg_V)
                        t.wait_ge(sV.h, vmark)
                        t.wait_ge(sSc.h, smark)
                        for hc in hcs:
                            sT.tag(t.matmul(psX, ones_c[:, :],
                                            xb[hc % 4][:, :],
                                            start=(hc == 0),
                                            stop=(hc == 15)))
                            sT.tag(t.matmul(psX2, ones_c[:, :],
                                            sq[hc % 4][:, :],
                                            start=(hc == 0),
                                            stop=(hc == 15)))
                    Tq(smm)
                    stats_t[g4] = sT.n
                res["sq"] = sSc.n
                res["statsT"] = sT.n
                tstats = sT.n

                def cmb_v(v, tstats=tstats):
                    v.wait_ge(sT.h, tstats)
                    sV.tag(v.tensor_scalar(out=rw[0][:, :], in0=psX,
                                           scalar1=1.0 / 2048,
                                           scalar2=None, op0=ALU.mult))
                    sV.tag(v.tensor_scalar(out=rw[1][:, :], in0=psX2,
                                           scalar1=1.0 / 2048,
                                           scalar2=None, op0=ALU.mult))
                    v.wait_ge(sV.h, sV.n)
                    sV.tag(v.tensor_tensor(out=rw[2][:, :], in0=rw[0][:, :],
                                           in1=rw[0][:, :], op=ALU.mult))
                    v.wait_ge(sV.h, sV.n)
                    sV.tag(v.tensor_tensor(out=rw[2][:, :], in0=rw[1][:, :],
                                           in1=rw[2][:, :],
                                           op=ALU.subtract))
                Vq(cmb_v)
                vvar = sV.n

                def cmb_s(s, vvar=vvar):
                    s.wait_ge(sV.h, vvar)
                    sSc.tag(s.activation(out=rw[3][:, :], in_=rw[2][:, :],
                                         func=AF.Sqrt,
                                         bias=epst[:1, 0:1], scale=1.0))
                Sq(cmb_s)
                ssqrt = sSc.n

                def cmb_v2(v, ssqrt=ssqrt):
                    v.wait_ge(sSc.h, ssqrt)
                    sV.tag(v.reciprocal(rw[3][:, :], rw[3][:, :]))
                    v.wait_ge(sV.h, sV.n)
                    sV.tag(v.tensor_copy(rwb[0][:, :], rw[3][:, :]))
                    sV.tag(v.scalar_tensor_tensor(out=rw[0][:, :],
                                                  in0=rw[0][:, :],
                                                  scalar=-1.0,
                                                  in1=rw[3][:, :],
                                                  op0=ALU.mult,
                                                  op1=ALU.mult))
                    v.wait_ge(sV.h, sV.n)
                    sV.tag(v.tensor_copy(rwb[1][:, :], rw[0][:, :]))
                Vq(cmb_v2)
                vrows = sV.n
                res["cmb"] = vrows

                def bcast(t, vrows=vrows):
                    if bg_V:
                        t.wait_ge(sV.h, bg_V)
                    t.wait_ge(sV.h, vrows)
                    sT.tag(t.matmul(psR, ones_r[:, :], rwb[0][:, :],
                                    start=True, stop=True))
                    sT.tag(t.matmul(psM, ones_r[:, :], rwb[1][:, :],
                                    start=True, stop=True))
                Tq(bcast)
                tbc = sT.n

                firstA = [True]
                for hc in range(16):
                    par = hc % 2

                    def app(v, hc=hc, par=par, tbc=tbc,
                            pprev=pc_state.get(par, 0)):
                        if firstA[0]:
                            firstA[0] = False
                            if ag_T:
                                v.wait_ge(sT.h, ag_T)
                        v.wait_ge(sT.h, tbc)
                        if piece_ap is not None and pprev:
                            v.wait_ge(sPx[par].h, pprev)
                        sV.tag(v.tensor_tensor(
                            out=tmpf[par][:, :], in0=xsl(hc),
                            in1=psR, op=ALU.mult))
                        v.wait_ge(sV.h, sV.n)
                        if xh is not None:
                            dst = xh[:, 512 * hc:512 * (hc + 1)]
                        else:
                            dst = xst[par][:, :]
                        sV.tag(v.tensor_tensor(out=dst, in0=tmpf[par][:, :],
                                               in1=psM, op=ALU.add))
                    Vq(app)
                    vap = sV.n
                    if piece_ap is not None:
                        def pdma(d, hc=hc, par=par, vap=vap):
                            d.wait_ge(sV.h, vap)
                            sPx[par].tag(d.dma_start(
                                out=piece_ap[128 * hc:128 * (hc + 1), :],
                                in_=xst[par][:, :]), 16)
                        Sq(pdma)
                        pc_state[par] = sPx[par].n
                res["done"] = sV.n
            return res

        def ffn_phase(xres, out_ap, deps0):
            with contextlib.ExitStack() as stFF:
                EFF = stFF.enter_context
                xh = EFF(nc.sbuf_tensor(uid("xh"), [128, 16 * 512], BF16))
                h_all = EFF(nc.sbuf_tensor(uid("h"), [128, 44 * 512], BF16))
                wg = [EFF(nc.sbuf_tensor(uid(f"wg{i}"), [128, HID], BF16))
                      for i in range(4)]
                sg = [EFF(nc.sbuf_tensor(uid(f"sg{i}"), [128, 512], BF16))
                      for i in range(2)]
                wd = [EFF(nc.sbuf_tensor(uid(f"wdn{i}"), [128, IFF], BF16))
                      for i in range(2)]
                psGU = [EFF(nc.psum_tensor(uid(f"psGU{i}"), [128, 512], F32))
                        for i in range(4)]
                psD_ = [EFF(nc.psum_tensor(uid(f"psD{i}"), [128, 512], F32))
                        for i in range(2)]
                psSR = EFF(nc.psum_tensor(uid("psSR"), [128, 512], F32))

                GU_ORDER = ([i for i in range(44) if i % NGT < 4]
                            + [i for i in range(44) if 4 <= i % NGT < 8]
                            + [i for i in range(44) if i % NGT >= 8])
                wdma, mmt, ept = {}, {}, {}
                ddma, dmmt, dept = {}, {}, {}
                ln_res, h_done, state = {}, {}, {}

                def wdma_it(d, pos):
                    par = pos % 2
                    if pos >= 2:
                        d.wait_ge(sT.h, mmt[pos - 2])
                    itp = GU_ORDER[pos % 44]
                    need = 0
                    for half in range(2):
                        j = itp + 44 * half
                        ol = j % NGT
                        for hi, mark in cc_box[0]:
                            if ol < hi:
                                need = max(need, mark)
                                break
                    d.wait_ge(sCC.h, need)
                    for half in range(2):
                        j = itp + 44 * half
                        r, ol = j // NGT, j % NGT
                        part = 0 if ol < 4 else (1 if ol < 8 else 2)
                        olp = ol - (0, 4, 8)[part]
                        sWg[par].tag(d.dma_start(
                            out=wg[2 * par + half][:, :],
                            in_=g_gu[part][128 * r:128 * (r + 1),
                                           2048 * olp:2048 * (olp + 1)]), 16)
                    wdma[pos] = sWg[par].n

                def ddma_m(d, m):
                    par = m % 2
                    if m < 2:
                        d.wait_ge(sCC.h, cc_box[1])
                    if m >= 2:
                        d.wait_ge(sT.h, dmmt[m - 2])
                    mp = m % 16
                    r, hl = mp // 2, mp % 2
                    sWd[par].tag(d.dma_start(
                        out=wd[par][:, :],
                        in_=g_d[128 * r:128 * (r + 1),
                                5632 * hl:5632 * (hl + 1)]), 16)
                    ddma[m] = sWd[par].n

                def wq_emit(it):
                    (Dq if it % 2 == 0 else Gq)(
                        lambda d, it=it: wdma_it(d, it))

                def dq_emit(m):
                    (Dq if m % 2 == 0 else Gq)(
                        lambda d, m=m: ddma_m(d, m))

                def gu_pass(p):
                    ln_done = ln_res[p]["done"]
                    h_guard = state.get("down_T", 0)
                    if p == 0:
                        wq_emit(0)
                        wq_emit(1)
                    for posp in range(44):
                        pos = 44 * p + posp
                        itp = GU_ORDER[posp]
                        par = pos % 2

                        def mm(t, pos=pos, posp=posp, par=par,
                               ln_done=ln_done):
                            t.wait_ge(sWg[par].h, wdma[pos])
                            if posp < 2:
                                t.wait_ge(sV.h, ln_done)
                            if pos >= 2:
                                t.wait_ge(sV.h, ept[pos - 2])
                            for hc in range(16):
                                for half in range(2):
                                    i_ = t.matmul(
                                        psGU[2 * par + half][:, :],
                                        wg[2 * par + half][
                                            :, 128 * hc:128 * (hc + 1)],
                                        xh[:, 512 * hc:512 * (hc + 1)],
                                        start=(hc == 0), stop=(hc == 15))
                                    if hc == 15:
                                        sT.tag(i_)
                            mmt[pos] = sT.n
                        Tq(mm)
                        nxt = pos + 2
                        if (p == 0 and nxt < 46) or (p == 1 and
                                                     46 <= nxt < 88):
                            wq_emit(nxt)
                        if pos % 2 == 1:
                            fill_one()
                        if p == 1 and posp == 10:
                            dq_emit(16)
                            dq_emit(17)

                        def ep_s(s, pos=pos, par=par):
                            s.wait_ge(sT.h, mmt[pos])
                            sSc.tag(s.activation(out=sg[par][:, :],
                                                 in_=psGU[2 * par][:, :],
                                                 func=AF.Silu))
                        Sq(ep_s)
                        smark = sSc.n

                        def ep_v(v, pos=pos, posp=posp, itp=itp, par=par,
                                 smark=smark, h_guard=h_guard):
                            v.wait_ge(sSc.h, smark)
                            if posp == 0 and h_guard:
                                v.wait_ge(sT.h, h_guard)
                            sV.tag(v.tensor_tensor(
                                out=h_all[:, 512 * itp:512 * (itp + 1)],
                                in0=sg[par][:, :],
                                in1=psGU[2 * par + 1][:, :], op=ALU.mult))
                        Vq(ep_v)
                        ept[pos] = sV.n
                    h_done[p] = sV.n

                def down_pass(p):
                    m0 = 16 * p
                    if p == 0:
                        dq_emit(0)
                        dq_emit(1)
                    stq = {}
                    for mp in range(16):
                        m = m0 + mp
                        par = m % 2

                        def mm(t, m=m, mp=mp, par=par, p=p):
                            t.wait_ge(sWd[par].h, ddma[m])
                            if mp == 0:
                                t.wait_ge(sV.h, h_done[p])
                                if p == 0:
                                    t.wait_ge(sV.h, ln_res[1]["cmb"])
                            if m >= 2:
                                t.wait_ge(sV.h, dept[m - 2][0])
                                t.wait_ge(sSc.h, dept[m - 2][1])
                            for ic in range(44):
                                i_ = t.matmul(
                                    psD_[par][:, :],
                                    wd[par][:, 128 * ic:128 * (ic + 1)],
                                    h_all[:, 512 * ic:512 * (ic + 1)],
                                    start=(ic == 0), stop=(ic == 43))
                                if ic == 43:
                                    sT.tag(i_)
                            dmmt[m] = sT.n
                        Tq(mm)
                        nxt = m + 2
                        if (p == 0 and nxt < 16) or (p == 1 and
                                                     18 <= nxt < 32):
                            dq_emit(nxt)

                        def ep_s(s, m=m, par=par):
                            s.wait_ge(sT.h, dmmt[m])
                            sSc.tag(s.activation(out=sqd[par][:, :],
                                                 in_=psD_[par][:, :],
                                                 func=AF.Square))
                        Sq(ep_s)
                        smark = sSc.n

                        def ep_v(v, m=m, mp=mp, par=par,
                                 rg=state.get("radd_G", 0) if p else 0):
                            v.wait_ge(sT.h, dmmt[m])
                            if mp == 0 and rg:
                                v.wait_ge(sG.h, rg)
                            sV.tag(v.tensor_copy(
                                y_all[:, 512 * mp:512 * (mp + 1)],
                                psD_[par][:, :]))
                        Vq(ep_v)
                        dept[m] = (sV.n, smark)
                        stq[mp] = (par, smark)

                        if mp >= 1:
                            def st_mm(t, mp=mp, stq=stq):
                                par_, smark_ = stq[mp - 1]
                                t.wait_ge(sSc.h, smark_)
                                if mp - 1 == 0:
                                    rg = state.get("rr_ev", 0)
                                    if rg:
                                        t.wait_ge(sV.h, rg)
                                sT.tag(t.matmul(psSR[0:1, :], ones_c[:, :],
                                                sqd[par_][:, :],
                                                start=(mp - 1 == 0),
                                                stop=False))
                            Tq(st_mm)

                    def st_last(t, stq=stq):
                        par_, smark_ = stq[15]
                        t.wait_ge(sSc.h, smark_)
                        sT.tag(t.matmul(psSR[0:1, :], ones_c[:, :],
                                        sqd[par_][:, :],
                                        start=False, stop=True))
                    Tq(st_last)
                    tstats = sT.n
                    state["down_T"] = dmmt[m0 + 15]

                    def rms_s(s, tstats=tstats):
                        s.wait_ge(sT.h, tstats)
                        sSc.tag(s.activation(out=rrow[:, :],
                                             in_=psSR[0:1, :],
                                             func=AF.Sqrt,
                                             bias=epst[:1, 1:2],
                                             scale=4.0 / 2048))
                    Sq(rms_s)
                    srms = sSc.n

                    def rms_v(v, srms=srms):
                        v.wait_ge(sSc.h, srms)
                        sV.tag(v.reciprocal(rrow[:, :], rrow[:, :]))
                        v.wait_ge(sV.h, sV.n)
                        sV.tag(v.tensor_copy(rrowb[:, :], rrow[:, :]))
                    Vq(rms_v)
                    vrms = sV.n

                    def rms_t(t, vrms=vrms):
                        t.wait_ge(sV.h, vrms)
                        sT.tag(t.matmul(psSR[:, :], ones_r[:, :],
                                        rrowb[:, :], start=True, stop=True))
                    Tq(rms_t)
                    trr = sT.n

                    def rr_ev(v, trr=trr):
                        v.wait_ge(sT.h, trr)
                        sV.tag(v.tensor_copy(rrb_s[:, :], psSR[:, :]))
                    Vq(rr_ev)
                    state["rr_ev"] = sV.n

                def radd_pass(p):
                    vrr2 = state["rr_ev"]
                    lsq = ln_res[p]["sq"]
                    firste = {"v": True, "g": True}
                    for hc in range(16):
                        use_g = (hc % 2 == 1)
                        ecnt = sG if use_g else sV
                        tp = tpfG if use_g else tpfV

                        def radd(e, hc=hc, p=p, use_g=use_g, ecnt=ecnt,
                                 tp=tp, vrr2=vrr2, lsq=lsq):
                            kk = "g" if use_g else "v"
                            if firste[kk]:
                                firste[kk] = False
                                e.wait_ge(sSc.h, lsq)
                                if use_g:
                                    e.wait_ge(sV.h, vrr2)
                            else:
                                e.wait_ge(ecnt.h, ecnt.n)
                            ecnt.tag(e.tensor_tensor(
                                out=tp[:, :],
                                in0=y_all[:, 512 * hc:512 * (hc + 1)],
                                in1=rrb_s[:, :], op=ALU.mult))
                            e.wait_ge(ecnt.h, ecnt.n)
                            ecnt.tag(e.tensor_tensor(
                                out=xres[:, 1024 * hc + 512 * p:
                                         1024 * hc + 512 * (p + 1)],
                                in0=tp[:, :],
                                in1=xres[:, 1024 * hc + 512 * p:
                                         1024 * hc + 512 * (p + 1)],
                                op=ALU.add))
                        (Gq if use_g else Vq)(radd)
                        emark = ecnt.n
                        if out_ap is not None:
                            def xs(d, hc=hc, p=p, emark=emark, ecnt=ecnt):
                                d.wait_ge(ecnt.h, emark)
                                sOut[hc % 4].tag(d.dma_start(
                                    out=out_ap[128 * hc:128 * (hc + 1),
                                               512 * p:512 * (p + 1)],
                                    in_=xres[:, 1024 * hc + 512 * p:
                                             1024 * hc + 512 * (p + 1)]),
                                    16)
                            Sq(xs)
                    state["radd_G"] = sG.n

                ln_res[0] = ln_pass(0, xres, psD_[0][0:1, :],
                                    psD_[1][0:1, :], psGU[2][:, :],
                                    psGU[3][:, :], deps0, xh=xh)
                gu_pass(0)
                ln_res[1] = ln_pass(1, xres, psD_[0][0:1, :],
                                    psD_[1][0:1, :], psGU[2][:, :],
                                    psGU[3][:, :], deps0, xh=xh,
                                    sg_T=ln_res[0]["statsT"],
                                    pg_V=ln_res[0]["cmb"],
                                    bg_V=ept[43], ag_T=mmt[43])
                with contextlib.ExitStack() as stY:
                    EY = stY.enter_context
                    y_all = EY(nc.sbuf_tensor(uid("y"), [128, 16 * 512],
                                              BF16))
                    sqd = [EY(nc.sbuf_tensor(uid(f"sqd{i}"), [128, 512],
                                             BF16)) for i in range(2)]
                    rrow = EY(nc.sbuf_tensor(uid("rrow"), [1, 512], F32))
                    rrowb = EY(nc.sbuf_tensor(uid("rrowb"), [1, 512], BF16))
                    rrb_s = EY(nc.sbuf_tensor(uid("rrb"), [128, 512], F32))
                    tpfV = EY(nc.sbuf_tensor(uid("tpfV"), [128, 512], F32))
                    tpfG = EY(nc.sbuf_tensor(uid("tpfG"), [128, 512], F32))
                    down_pass(0)
                    radd_pass(0)
                    gu_pass(1)
                    down_pass(1)
                    radd_pass(1)
                return sV.n, sG.n

        res1_v, res1_g = ffn_phase(
            x_res, None, [(sLN[i], x0_marks[i]) for i in range(4)])
        while filler:
            fill_one()
        cst_st.close()
        barrier()

        def spill(d):
            for hc in range(16):
                sPz[1].tag(d.dma_start(
                    out=x_sp[128 * hc:128 * (hc + 1), :],
                    in_=x_res[:, 1024 * hc:1024 * (hc + 1)]), 16)
        Dq(spill)

        cc_x = [0, 0]
        with contextlib.ExitStack() as stL2:
            EL2 = stL2.enter_context
            psX_ = EL2(nc.psum_tensor("ps2X", [1, 512], F32))
            psX2_ = EL2(nc.psum_tensor("ps2X2", [1, 512], F32))
            psM_ = EL2(nc.psum_tensor("ps2M", [128, 512], F32))
            psR_ = EL2(nc.psum_tensor("ps2R", [128, 512], F32))
            l20 = ln_pass(0, x_res, psX_[:, :], psX2_[:, :], psM_[:, :],
                          psR_[:, :], [], piece_ap=xh_p[0])
            dp0 = [sPx[0].n, sPx[1].n]

            def p6_cc0(g):
                for i_ in range(2):
                    g.wait_ge(sPx[i_].h, dp0[i_])
                sCC.tag(g.collective_compute(
                    "AllGather", ALU.bypass,
                    replica_groups=[list(range(8))],
                    ins=[xh_p[0]], outs=[g_xs[0]]), 1)
            Gq(p6_cc0)
            cc_x[0] = sCC.n
            l21 = ln_pass(1, x_res, psX_[:, :], psX2_[:, :], psM_[:, :],
                          psR_[:, :], [], piece_ap=xh_p[1],
                          sg_T=l20["statsT"], pg_V=l20["cmb"],
                          bg_V=l20["done"])
        dp1 = [sPx[0].n, sPx[1].n]

        def p6_cc1(g):
            for i_ in range(2):
                g.wait_ge(sPx[i_].h, dp1[i_])
            sCC.tag(g.collective_compute(
                "AllGather", ALU.bypass, replica_groups=[list(range(8))],
                ins=[xh_p[1]], outs=[g_xs[1]]), 1)
        Gq(p6_cc1)
        cc_x[1] = sCC.n
        barrier(ex=(sCC,))
        xr_st.close()

        cc_a2a = [0, 0]
        with contextlib.ExitStack() as stO:
            EO = stO.enter_context
            o_tm = [EO(nc.sbuf_tensor(f"otm{i}", [128, 4096], BF16))
                    for i in range(4)]
            with contextlib.ExitStack() as stAtt:
                EA = stAtt.enter_context
                qt = [EA(nc.sbuf_tensor(f"qt{i}", [128, T], BF16))
                      for i in range(2)]
                kt = [EA(nc.sbuf_tensor(f"kt{i}", [128, T], BF16))
                      for i in range(2)]
                vtm = [EA(nc.sbuf_tensor(f"vtm{i}", [128, T], BF16))
                       for i in range(2)]
                bias_u = [EA(nc.sbuf_tensor(f"biasu{i}", [128, 1], F32))
                          for i in range(4)]

                with contextlib.ExitStack() as stC1:
                    EC = stC1.enter_context
                    wqb = EC(nc.sbuf_tensor("wqb", [128, 16 * 6 * HD], BF16))
                    xg = [EC(nc.sbuf_tensor(f"xg{i}", [128, 16 * 512], BF16))
                          for i in range(2)]
                    vstg = EC(nc.sbuf_tensor("vstg", [128, 1024], BF16))
                    psQ = [EC(nc.psum_tensor(f"psQ{i}", [128, 512], F32))
                           for i in range(2)]
                    psVT = [EC(nc.psum_tensor(f"psVT{i}", [128, 512], F32))
                            for i in range(4)]

                    def wql(d):
                        d.wait_ge(sXr[1].h, wq_cast_total)
                        for hc in range(16):
                            sWq.tag(d.dma_start(
                                out=wqb[:, 6 * HD * hc:6 * HD * (hc + 1)],
                                in_=wq_bf[128 * hc:128 * (hc + 1), :]), 16)
                    Dq(wql)
                    wq_marks = sWq.n

                    qkv_mm = {}
                    ev_state = {}
                    xg_dma = {}
                    XG_ORDER = [2 * r + tb for tb in range(2)
                                for r in range(8)]

                    def xg_tt(d, pos):
                        par = pos % 2
                        if pos >= 2:
                            d.wait_ge(sT.h, qkv_mm[pos - 2])
                        tt = XG_ORDER[pos]
                        r, tb = tt // 2, tt % 2
                        d.wait_ge(sCC.h, cc_x[tb])
                        for hc in range(16):
                            sXg[par].tag(d.dma_start(
                                out=xg[par][:, 512 * hc:512 * (hc + 1)],
                                in_=g_xs[tb][2048 * r + 128 * hc:
                                             2048 * r + 128 * (hc + 1),
                                             :]), 16)
                        xg_dma[pos] = sXg[par].n

                    def xg_pre(d):
                        xg_tt(d, 0)
                        xg_tt(d, 1)
                    Dq(xg_pre)

                    for pos in range(16):
                        tt = XG_ORDER[pos]
                        par = pos % 2
                        for o in range(6):
                            opar = o % 2
                            eprev = ev_state.get(opar, 0)

                            def mm(t, tt=tt, par=par, o=o, opar=opar,
                                   eprev=eprev, wq_marks=wq_marks,
                                   pos=pos):
                                t.wait_ge(sXg[par].h, xg_dma[pos])
                                t.wait_ge(sWq.h, wq_marks)
                                if eprev:
                                    t.wait_ge(sV.h, eprev)
                                for hc in range(16):
                                    i_ = t.matmul(
                                        psQ[opar][:, :],
                                        wqb[:, 6 * HD * hc + 128 * o:
                                            6 * HD * hc + 128 * (o + 1)],
                                        xg[par][:, 512 * hc:512 * (hc + 1)],
                                        start=(hc == 0), stop=(hc == 15))
                                    if hc == 15:
                                        sT.tag(i_)
                            Tq(mm)
                            tmark = sT.n
                            if o == 5:
                                qkv_mm[pos] = tmark

                            def ev(v, tt=tt, o=o, opar=opar, tmark=tmark):
                                v.wait_ge(sT.h, tmark)
                                hl = o % 2
                                if o < 2:
                                    dd = qt[hl][:, 512 * tt:512 * (tt + 1)]
                                elif o < 4:
                                    dd = kt[hl][:, 512 * tt:512 * (tt + 1)]
                                else:
                                    dd = vstg[:, 512 * hl:512 * (hl + 1)]
                                sV.tag(v.tensor_copy(dd, psQ[opar][:, :]))
                            Vq(ev)
                            ev_state[opar] = sV.n
                        if pos + 2 < 16:
                            Dq(lambda d, pos=pos: xg_tt(d, pos + 2))
                        vmark = sV.n
                        for hl in range(2):
                            vtprev = ev_state.get("vt", 0)

                            def vtr(t, tt=tt, hl=hl, vmark=vmark,
                                    vtprev=vtprev):
                                t.wait_ge(sV.h, max(vmark, vtprev))
                                for ss in range(4):
                                    sT.tag(t.matmul(
                                        psVT[ss][:, :128],
                                        vstg[:, 512 * hl + 128 * ss:
                                             512 * hl + 128 * (ss + 1)],
                                        ident_b[:, :],
                                        start=True, stop=True))
                            Tq(vtr)
                            tmark2 = sT.n

                            def vev(v, tt=tt, hl=hl, tmark2=tmark2):
                                v.wait_ge(sT.h, tmark2)
                                for ss in range(4):
                                    sV.tag(v.tensor_copy(
                                        vtm[hl][:, 512 * tt + 128 * ss:
                                                512 * tt + 128 * (ss + 1)],
                                        psVT[ss][:, :128]))
                            Vq(vev)
                            ev_state["vt"] = sV.n
                    barrier()

                with contextlib.ExitStack() as stN:
                    EN = stN.enter_context
                    nsq = [EN(nc.sbuf_tensor(f"nsq{i}", [128, 512], BF16))
                           for i in range(4)]
                    nrow = EN(nc.sbuf_tensor("nrow", [1, 8], F32))
                    psN4 = [EN(nc.psum_tensor(f"psN4_{i}", [1, 512], F32))
                            for i in range(4)]
                    nmx = EN(nc.sbuf_tensor("nmx", [1, 16], F32))
                    ub1 = EN(nc.sbuf_tensor("ub1", [1, 4], F32))
                    ub1b = EN(nc.sbuf_tensor("ub1b", [1, 4], BF16))
                    psN = EN(nc.psum_tensor("psN", [1, 512], F32))
                    psB = EN(nc.psum_tensor("psB", [128, 4], F32))
                    n_state = {}
                    for u in range(4):
                        hl, b = u // 2, u % 2
                        for which, tens in ((0, qt[hl]), (1, kt[hl])):
                            for g4 in range(2):
                                kbs = list(range(4 * g4, 4 * g4 + 4))
                                tprev = n_state.get("nT", 0)

                                def nq(s, kbs=kbs, tens=tens, b=b,
                                       tprev=tprev):
                                    if tprev:
                                        s.wait_ge(sT.h, tprev)
                                    for kblk in kbs:
                                        sSc.tag(s.activation(
                                            out=nsq[kblk % 4][:, :],
                                            in_=tens[:, 4096 * b +
                                                     512 * kblk:
                                                     4096 * b +
                                                     512 * (kblk + 1)],
                                            func=AF.Square))
                                Sq(nq)
                                smark = sSc.n
                                vprev = n_state.get("nmax", 0)

                                def nt(t, kbs=kbs, smark=smark,
                                       vprev=vprev):
                                    t.wait_ge(sSc.h, smark)
                                    if vprev:
                                        t.wait_ge(sV.h, vprev)
                                    for kblk in kbs:
                                        sT.tag(t.matmul(
                                            psN4[kblk % 4][:, :],
                                            ones_c[:, :],
                                            nsq[kblk % 4][:, :],
                                            start=True, stop=True))
                                Tq(nt)
                                n_state["nT"] = sT.n
                                tmark = sT.n

                                def nv(v, kbs=kbs, tmark=tmark):
                                    v.wait_ge(sT.h, tmark)
                                    v.wait_ge(sV.h, sV.n)
                                    for kblk in kbs:
                                        sV.tag(v.reduce_max(
                                            nrow[:, kblk:kblk + 1],
                                            psN4[kblk % 4][:, :],
                                            axis=AXX))
                                Vq(nv)
                                n_state["nmax"] = sV.n

                            def nred(v, u=u, which=which):
                                v.wait_ge(sV.h, sV.n)
                                sV.tag(v.reduce_max(
                                    nmx[:, 8 * which + u:8 * which + u + 1],
                                    nrow[:, :], axis=AXX))
                            Vq(nred)
                    vn = sV.n

                    def ubv(v, vn=vn):
                        v.wait_ge(sV.h, vn)
                        sV.tag(v.tensor_tensor(out=ub1[:, :], in0=nmx[:, 0:4],
                                               in1=nmx[:, 8:12], op=ALU.mult))
                    Vq(ubv)
                    vub = sV.n

                    def ubs(s, vub=vub):
                        s.wait_ge(sV.h, vub)
                        sSc.tag(s.activation(out=ub1[:, :], in_=ub1[:, :],
                                             func=AF.Sqrt))
                    Sq(ubs)
                    sub = sSc.n

                    def ubv2(v, sub=sub):
                        v.wait_ge(sSc.h, sub)
                        sV.tag(v.tensor_scalar(out=ub1b[:, :], in0=ub1[:, :],
                                               scalar1=-ISC, scalar2=None,
                                               op0=ALU.mult))
                    Vq(ubv2)
                    vub2 = sV.n

                    def ubt(t, vub2=vub2):
                        t.wait_ge(sV.h, vub2)
                        sT.tag(t.matmul(psB[:, :4], ones_r[:, :],
                                        ub1b[:, :], start=True, stop=True))
                    Tq(ubt)
                    tub = sT.n

                    def ubv3(v, tub=tub):
                        v.wait_ge(sT.h, tub)
                        for u in range(4):
                            sV.tag(v.tensor_copy(bias_u[u][:, :],
                                                 psB[:, u:u + 1]))
                    Vq(ubv3)
                barrier()

                with contextlib.ExitStack() as stA:
                    EA2 = stA.enter_context
                    pt = [EA2(nc.sbuf_tensor(f"pt{i}", [128, 512], BF16))
                          for i in range(4)]
                    srow = EA2(nc.sbuf_tensor("srow", [1, 512], F32))
                    rrow_a = EA2(nc.sbuf_tensor("rrow_a", [1, 512], BF16))
                    rb_s = EA2(nc.sbuf_tensor("rb_s", [128, 512], BF16))
                    psST = [EA2(nc.psum_tensor(f"psST{i}", [128, 512], F32))
                            for i in range(4)]
                    psAV = EA2(nc.psum_tensor("psAV", [128, 512], F32))
                    psSS = EA2(nc.psum_tensor("psSS", [1, 512], F32))
                    psRB = EA2(nc.psum_tensor("psRB", [128, 512], F32))
                    a_state = {}

                    for u in range(4):
                        hl, b = u // 2, u % 2
                        for qc in range(8):
                            nkc = 4 * qc + 4
                            ngr = nkc // 4
                            st_t = {}
                            exp_t = {}
                            zero_t = {}
                            mask_v = {}

                            def consume(t, kc, u=u, qc=qc, hl=hl, b=b,
                                        nkc=nkc, st_t=st_t, exp_t=exp_t,
                                        zero_t=zero_t):
                                par = kc % 4
                                t.wait_ge(sSc.h, exp_t[kc])
                                if kc in zero_t:
                                    t.wait_ge(sG.h, zero_t[kc])
                                if kc == 0:
                                    pv = a_state.get("av", 0)
                                    if pv:
                                        t.wait_ge(sV.h, pv)
                                    pv = a_state.get("ss", 0)
                                    if pv:
                                        t.wait_ge(sV.h, pv)
                                sT.tag(t.matmul(psSS[:, :], ones_c[:, :],
                                                pt[par][:, :],
                                                start=(kc == 0),
                                                stop=(kc == nkc - 1)))
                                sT.tag(t.matmul(
                                    psAV[:, :],
                                    vtm[hl][:, 4096 * b + 128 * kc:
                                            4096 * b + 128 * (kc + 1)],
                                    pt[par][:, :],
                                    start=(kc == 0),
                                    stop=(kc == nkc - 1)))

                            for g in range(ngr):
                                kcs = list(range(4 * g, 4 * g + 4))

                                def tsec(t, g=g, kcs=kcs, hl=hl, b=b, qc=qc,
                                         st_t=st_t):
                                    if g >= 1:
                                        for kc2 in range(4 * (g - 1), 4 * g):
                                            consume(t, kc2)
                                        a_state["ptg"] = sT.n
                                    for kc in kcs:
                                        prev = a_state.get(("st", kc % 4), 0)
                                        if prev:
                                            t.wait_ge(sSc.h, prev)
                                        sT.tag(t.matmul(
                                            psST[kc % 4][:, :],
                                            kt[hl][:, 4096 * b + 128 * kc:
                                                   4096 * b +
                                                   128 * (kc + 1)],
                                            qt[hl][:, 4096 * b + 512 * qc:
                                                   4096 * b +
                                                   512 * (qc + 1)],
                                            start=True, stop=True))
                                        st_t[kc] = sT.n
                                Tq(tsec)

                                if g == ngr - 1:
                                    def msec(v, kcs=kcs, qc=qc, st_t=st_t,
                                             mask_v=mask_v):
                                        for kc in kcs:
                                            v.wait_ge(sT.h, st_t[kc])
                                            jd = kc - 4 * qc
                                            sV.tag(v.tensor_tensor(
                                                out=psST[kc % 4][
                                                    :, 128 * jd:
                                                    128 * (jd + 1)],
                                                in0=psST[kc % 4][
                                                    :, 128 * jd:
                                                    128 * (jd + 1)],
                                                in1=mask_t[:, :],
                                                op=ALU.add))
                                            mask_v[kc] = sV.n
                                    Vq(msec)

                                def esec(s, g=g, kcs=kcs, u=u, st_t=st_t,
                                         mask_v=mask_v, exp_t=exp_t):
                                    prev = a_state.get("ptg", 0)
                                    if prev:
                                        s.wait_ge(sT.h, prev)
                                    for kc in kcs:
                                        s.wait_ge(sT.h, st_t[kc])
                                        if kc in mask_v:
                                            s.wait_ge(sV.h, mask_v[kc])
                                        sSc.tag(s.activation(
                                            out=pt[kc % 4][:, :],
                                            in_=psST[kc % 4][:, :],
                                            func=AF.Exp,
                                            bias=bias_u[u][:, 0:1],
                                            scale=ISC))
                                        a_state[("st", kc % 4)] = sSc.n
                                        exp_t[kc] = sSc.n
                                Sq(esec)

                                if g == ngr - 1:
                                    def zsec(gp, kcs=kcs, qc=qc, exp_t=exp_t,
                                             zero_t=zero_t):
                                        for kc in kcs:
                                            jd = kc - 4 * qc
                                            if 1 <= jd <= 3:
                                                gp.wait_ge(sSc.h,
                                                           exp_t[kc])
                                                sG.tag(gp.memset(
                                                    pt[kc % 4][
                                                        :, 0:128 * jd],
                                                    0.0))
                                                zero_t[kc] = sG.n
                                    Gq(zsec)

                            def lastc(t, nkc=nkc, consume=consume):
                                for kc2 in range(nkc - 4, nkc):
                                    consume(t, kc2)
                                a_state["ptg"] = sT.n
                            Tq(lastc)
                            tdone = sT.n

                            def ep_v1(v, tdone=tdone):
                                v.wait_ge(sT.h, tdone)
                                prev = a_state.get("rb_t", 0)
                                if prev:
                                    v.wait_ge(sT.h, prev)
                                sV.tag(v.tensor_copy(srow[:, :], psSS[:, :]))
                                v.wait_ge(sV.h, sV.n)
                                sV.tag(v.reciprocal(srow[:, :], srow[:, :]))
                                v.wait_ge(sV.h, sV.n)
                                sV.tag(v.tensor_copy(rrow_a[:, :],
                                                     srow[:, :]))
                            Vq(ep_v1)
                            a_state["ss"] = sV.n
                            vrr = sV.n

                            def ep_t(t, vrr=vrr):
                                t.wait_ge(sV.h, vrr)
                                prev = a_state.get("rb_v", 0)
                                if prev:
                                    t.wait_ge(sV.h, prev)
                                sT.tag(t.matmul(psRB[:, :], ones_r[:, :],
                                                rrow_a[:, :],
                                                start=True, stop=True))
                            Tq(ep_t)
                            a_state["rb_t"] = sT.n
                            trb = sT.n

                            def ep_v2(v, trb=trb, u=u, qc=qc, tdone=tdone):
                                v.wait_ge(sT.h, trb)
                                v.wait_ge(sT.h, tdone)
                                sV.tag(v.tensor_copy(rb_s[:, :], psRB[:, :]))
                                v.wait_ge(sV.h, sV.n)
                                sV.tag(v.scalar_tensor_tensor(
                                    out=o_tm[u][:, 512 * qc:512 * (qc + 1)],
                                    in0=psAV[:, :], scalar=1.0,
                                    in1=rb_s[:, :],
                                    op0=ALU.mult, op1=ALU.mult))
                                a_state["av"] = sV.n
                            Vq(ep_v2)
                            a_state["rb_v"] = sV.n
                        if u % 2 == 1:
                            hl_ = u // 2
                            o_mk = sV.n

                            def a2aw(d, hl_=hl_, o_mk=o_mk):
                                d.wait_ge(sV.h, o_mk)
                                for rd in range(8):
                                    b_, sl = rd // 4, rd % 4
                                    sPz[0].tag(d.dma_start(
                                        out=a2a_i[hl_][128 * rd:
                                                       128 * (rd + 1), :],
                                        in_=o_tm[2 * hl_ + b_][
                                            :, 1024 * sl:
                                            1024 * (sl + 1)]), 16)
                            Dq(a2aw)
                            pz_mk = sPz[0].n

                            def a2ac(g, hl_=hl_, pz_mk=pz_mk):
                                g.wait_ge(sPz[0].h, pz_mk)
                                with allow_ici_all_to_all():
                                    sCC.tag(g.collective_compute(
                                        "AllToAll", ALU.bypass,
                                        replica_groups=[list(range(8))],
                                        ins=[a2a_i[hl_]],
                                        outs=[a2a_o[hl_]]), 1)
                            Gq(a2ac)
                            cc_a2a[hl_] = sCC.n
                barrier()
            barrier(ex=(sCC,))

        xr2_st = contextlib.ExitStack()
        x_res2 = xr2_st.enter_context(
            nc.sbuf_tensor("x_res2", [128, 16 * TC], F32))

        with contextlib.ExitStack() as stP9:
            EP = stP9.enter_context
            wob = EP(nc.sbuf_tensor("wob", [128, 16 * HID], BF16))
            og = [EP(nc.sbuf_tensor(f"og{i}", [128, 16 * 512], BF16))
                  for i in range(2)]
            psZ = [EP(nc.psum_tensor(f"psZ{i}", [128, 512], F32))
                   for i in range(2)]

            def wol(d):
                d.wait_ge(sXr[1].h, all_cast_total)
                for hh in range(16):
                    sWo.tag(d.dma_start(
                        out=wob[:, HID * hh:HID * (hh + 1)],
                        in_=wo_bf[128 * hh:128 * (hh + 1), :]), 16)
            Dq(wol)
            wo_marks = sWo.n

            def x2_load(d):
                for hc in range(16):
                    sLN[0].tag(d.dma_start(
                        out=x_res2[:, 1024 * hc:1024 * (hc + 1)],
                        in_=x_sp[128 * hc:128 * (hc + 1), :]), 16)
            Dq(x2_load)
            x2_marks = sLN[0].n

            ogd = {}
            for tb in range(2):
                def ogl(d, tb=tb):
                    for par in range(2):
                        d.wait_ge(sCC.h, cc_a2a[par])
                        for hh in range(par, 16, 2):
                            sXg[tb].tag(d.dma_start(
                                out=og[tb][:, 512 * hh:512 * (hh + 1)],
                                in_=a2a_o[par][128 * (hh // 2):
                                               128 * (hh // 2) + 128,
                                               512 * tb:512 * (tb + 1)]),
                                16)
                Sq(ogl)
                ogd[tb] = sXg[tb].n
            z_state = {}
            first_z = [True]
            for tb in range(2):
                for ho in range(16):
                    par = ho % 2
                    eprev = z_state.get(("v", par), 0)

                    def zmm(t, tb=tb, ho=ho, par=par, eprev=eprev,
                            wo_marks=wo_marks):
                        t.wait_ge(sXg[tb].h, ogd[tb])
                        t.wait_ge(sWo.h, wo_marks)
                        if eprev:
                            t.wait_ge(sV.h, eprev)
                        for hh in range(16):
                            i_ = t.matmul(
                                psZ[par][:, :],
                                wob[:, HID * hh + 128 * ho:
                                    HID * hh + 128 * (ho + 1)],
                                og[tb][:, 512 * hh:512 * (hh + 1)],
                                start=(hh == 0), stop=(hh == 15))
                            if hh == 15:
                                sT.tag(i_)
                    Tq(zmm)
                    tmark = sT.n

                    def zadd(v, tb=tb, ho=ho, par=par, tmark=tmark):
                        if first_z[0]:
                            first_z[0] = False
                            v.wait_ge(sLN[0].h, x2_marks)
                        v.wait_ge(sT.h, tmark)
                        sV.tag(v.tensor_tensor(
                            out=x_res2[:, 1024 * ho + 512 * tb:
                                       1024 * ho + 512 * (tb + 1)],
                            in0=psZ[par][:, :],
                            in1=x_res2[:, 1024 * ho + 512 * tb:
                                       1024 * ho + 512 * (tb + 1)],
                            op=ALU.add))
                    Vq(zadd)
                    z_state[("v", par)] = sV.n
            x2_v_done = sV.n
        barrier()

        ffn_phase(x_res2, out_t, [(sV, x2_v_done)])
        xr2_st.close()

        nout = [c.n for c in sOut]

        def fin(g, nout=nout):
            for i in range(4):
                g.wait_ge(sOut[i].h, nout[i])
        Gq(fin)

    return nc


_PROG = None


def kernel(L0_attn_x, L0_attn_w_qkv, L0_attn_w_out, L0_ffn_gate_up_packed,
           L0_ffn_gate_up_scales, L0_ffn_down_packed, L0_ffn_down_scales):
    global _PROG
    x = np.asarray(L0_attn_x, dtype=np.float32).reshape(T, HID)
    wqkv = np.asarray(L0_attn_w_qkv, dtype=np.float32)
    wout = np.asarray(L0_attn_w_out, dtype=np.float32)
    gup = np.asarray(L0_ffn_gate_up_packed, dtype=np.int32)
    gus = np.asarray(L0_ffn_gate_up_scales)
    dnp = np.asarray(L0_ffn_down_packed, dtype=np.int32)
    dns = np.asarray(L0_ffn_down_scales)

    in_maps = []
    for c in range(NCORES):
        cols = []
        for which in range(3):
            for hl in range(2):
                h = 2 * c + hl
                base = (which * NH + h) * HD
                cols.extend(range(base, base + HD))
        in_maps.append({
            "x_t": np.ascontiguousarray(x[TC * c:TC * (c + 1)].T),
            "wqkv_sl": np.ascontiguousarray(wqkv[:, cols]),
            "wout_sl": wout,
            "gup_sl": np.ascontiguousarray(gup[OGU_C * c:OGU_C * (c + 1)]),
            "gus_sl": np.ascontiguousarray(gus[:, OGU_C * c:OGU_C * (c + 1)]),
            "dnp_sl": np.ascontiguousarray(dnp[DH_C * c:DH_C * (c + 1)]),
            "dns_sl": np.ascontiguousarray(dns[:, DH_C * c:DH_C * (c + 1)]),
        })

    if _PROG is None:
        _PROG = build_program()
    res = run_bass_kernel_spmd(_PROG, in_maps, core_ids=list(range(NCORES)))
    out = np.empty((T, HID), dtype=np.float32)
    for c in range(NCORES):
        out[TC * c:TC * (c + 1)] = res.results[c]["out_t"].T
    return out.reshape(B, S, HID)


# version 19 (final)
# speedup vs baseline: 1.1048x; 1.0310x over previous
import contextlib
import numpy as np
import concourse.bass as bass
import concourse.mybir as mybir
from concourse.bass_utils import run_bass_kernel_spmd
from concourse.aot_env import allow_ici_all_to_all

F32 = mybir.dt.float32
BF16 = mybir.dt.bfloat16
F16 = mybir.dt.float16
I32 = mybir.dt.int32
AF = mybir.ActivationFunctionType
ALU = mybir.AluOpType
AXX = mybir.AxisListType.X

HID = 2048
NH = 16
HD = 128
IFF = 5632
B = 2
S = 4096
T = 8192
TC = 1024
NCORES = 8
OGU = 2 * IFF
OGU_C = OGU // 8
NGT = OGU_C // 128
DH_C = HID // 8
NDT = DH_C // 128
ISC = float(1.0 / np.sqrt(128.0))
NEG = -1e9


class Cnt:
    __slots__ = ("h", "n")

    def __init__(self, h):
        self.h = h
        self.n = 0

    def tag(self, instr, k=1):
        instr.then_inc(self.h, k)
        self.n += k
        return self.n


def build_program():
    nc = bass.Bass("TRN2", target_bir_lowering=False, debug=False)
    uid_box = [0]

    def uid(s):
        uid_box[0] += 1
        return f"{s}_{uid_box[0]}"

    x_t = nc.dram_tensor("x_t", [HID, TC], F32, kind="ExternalInput").ap()
    wqkv = nc.dram_tensor("wqkv_sl", [HID, 6 * HD], F32, kind="ExternalInput").ap()
    wout = nc.dram_tensor("wout_sl", [HID, HID], F32, kind="ExternalInput").ap()
    gup = nc.dram_tensor("gup_sl", [OGU_C, 256], I32, kind="ExternalInput").ap()
    gus = nc.dram_tensor("gus_sl", [16, OGU_C], F16, kind="ExternalInput").ap()
    dnp = nc.dram_tensor("dnp_sl", [DH_C, 704], I32, kind="ExternalInput").ap()
    dns = nc.dram_tensor("dns_sl", [44, DH_C], F16, kind="ExternalInput").ap()
    out_t = nc.dram_tensor("out_t", [HID, TC], F32, kind="ExternalOutput").ap()

    GU_PARTS = ((0, 4), (4, 8), (8, 11))
    wgu_piece = [nc.dram_tensor(f"wgu_piece{i}",
                                [128, (hi - lo) * 2048], BF16).ap()
                 for i, (lo, hi) in enumerate(GU_PARTS)]
    g_gu = [nc.dram_tensor(f"g_gu{i}", [8 * 128, (hi - lo) * 2048], BF16).ap()
            for i, (lo, hi) in enumerate(GU_PARTS)]
    wd_piece = nc.dram_tensor("wd_piece", [128, NDT * 5632], BF16).ap()
    g_d = nc.dram_tensor("g_d", [8 * 128, NDT * 5632], BF16).ap()
    xh_p = [nc.dram_tensor(f"xh_p{i}", [HID, 512], BF16).ap()
            for i in range(2)]
    g_xs = [nc.dram_tensor(f"g_x{i}", [8 * HID, 512], BF16).ap()
            for i in range(2)]
    a2a_i = [nc.dram_tensor(f"a2a_i{i}", [8 * HD, TC], BF16).ap()
             for i in range(2)]
    a2a_o = [nc.dram_tensor(f"a2a_o{i}", [8 * HD, TC], BF16).ap()
             for i in range(2)]
    x_sp = nc.dram_tensor("x_sp", [HID, TC], F32).ap()
    wq_bf = nc.dram_tensor("wq_bf", [HID, 6 * HD], BF16).ap()
    wo_bf = nc.dram_tensor("wo_bf", [HID, HID], BF16).ap()

    with contextlib.ExitStack() as top:
        E = top.enter_context
        blk = E(nc.Block())

        sD = Cnt(E(nc.semaphore("sD")))
        sDg = Cnt(E(nc.semaphore("sDg")))
        sDs = Cnt(E(nc.semaphore("sDs")))
        sT = Cnt(E(nc.semaphore("sT")))
        sV = Cnt(E(nc.semaphore("sV")))
        sSc = Cnt(E(nc.semaphore("sSc")))
        sG = Cnt(E(nc.semaphore("sG")))
        sCC = Cnt(E(nc.semaphore("sCC")))
        sOut = [Cnt(E(nc.semaphore(f"sOut{i}"))) for i in range(4)]
        sLN = [Cnt(E(nc.semaphore(f"sLN{i}"))) for i in range(4)]
        sWq = Cnt(E(nc.semaphore("sWq")))
        sWo = Cnt(E(nc.semaphore("sWo")))
        sWg = [Cnt(E(nc.semaphore(f"sWg{i}"))) for i in range(2)]
        sWd = [Cnt(E(nc.semaphore(f"sWd{i}"))) for i in range(2)]
        sXg = [Cnt(E(nc.semaphore(f"sXg{i}"))) for i in range(2)]
        sPp = [Cnt(E(nc.semaphore(f"sPp{i}"))) for i in range(2)]
        sPx = [Cnt(E(nc.semaphore(f"sPx{i}"))) for i in range(4)]
        sPz = [Cnt(E(nc.semaphore(f"sPz{i}"))) for i in range(2)]
        sXr = [Cnt(E(nc.semaphore(f"sXr{i}"))) for i in range(4)]
        sSt1 = [Cnt(E(nc.semaphore(f"sSt1{i}"))) for i in range(4)]
        sZr = [Cnt(E(nc.semaphore(f"sZr{i}"))) for i in range(4)]
        sSt2 = [Cnt(E(nc.semaphore(f"sSt2{i}"))) for i in range(4)]

        ALL_ENG = ("tensor", "vector", "scalar", "gpsimd", "sync")
        eng_cnt = {"tensor": sT, "vector": sV, "scalar": sSc, "gpsimd": sG,
                   "sync": sD}
        ALL_CNT_L = ([sD, sDg, sDs, sT, sV, sSc, sG, sCC, sWq, sWo]
                     + sOut + sWg + sWd + sXg + sPp + sPx + sPz + sXr
                     + sSt1 + sZr + sSt2 + sLN)

        def Tq(fn):
            blk.tensor(fn)

        def Vq(fn):
            blk.vector(fn)

        def Sq(fn):
            blk.scalar(fn)

        def Gq(fn):
            blk.gpsimd(fn)

        def Dq(fn):
            blk.sync(fn)


        def barrier(ex=()):
            snap = [(c, c.n) for c in ALL_CNT_L if c.n and c not in ex]
            for en in ALL_ENG:
                def wt(e, en=en, snap=snap):
                    own = eng_cnt[en]
                    for c, v in snap:
                        if c is not own:
                            e.wait_ge(c.h, v)
                getattr(blk, en)(wt)

        cc_box = [None, 0]

        ident_f = E(nc.sbuf_tensor("ident_f", [128, 128], F32))
        ident_b = E(nc.sbuf_tensor("ident_b", [128, 128], BF16))
        mask_t = E(nc.sbuf_tensor("mask_t", [128, 128], F32))
        ones_c = E(nc.sbuf_tensor("ones_c", [128, 1], BF16))
        ones_r = E(nc.sbuf_tensor("ones_r", [1, 128], BF16))
        epst = E(nc.sbuf_tensor("epst", [128, 2], F32))

        def consts(g):
            sG.tag(g.memset(ident_f[:, :], 0.0))
            g.wait_ge(sG.h, sG.n)
            sG.tag(g.affine_select(
                out=ident_f[:, :], in_=ident_f[:, :],
                compare_op=ALU.not_equal, fill=1.0, base=0,
                pattern=[[-1, 128]], channel_multiplier=1))
            sG.tag(g.memset(mask_t[:, :], 0.0))
            g.wait_ge(sG.h, sG.n)
            sG.tag(g.affine_select(
                out=mask_t[:, :], in_=mask_t[:, :],
                compare_op=ALU.is_ge, fill=NEG, base=0,
                pattern=[[1, 128]], channel_multiplier=-1))
            sG.tag(g.memset(ones_c[:, :], 1.0))
            sG.tag(g.memset(ones_r[:, :], 1.0))
            sG.tag(g.memset(epst[:, 0:1], 1e-5))
            sG.tag(g.memset(epst[:, 1:2], 4e-6))
            g.wait_ge(sG.h, sG.n)
        Gq(consts)
        g_consts = sG.n

        def cast_ident(v):
            v.wait_ge(sG.h, g_consts)
            sV.tag(v.tensor_copy(ident_b[:, :], ident_f[:, :]))
        Vq(cast_ident)
        g_after = sG.n
        v_after = sV.n

        def pre_t(t):
            t.wait_ge(sG.h, g_after)
            t.wait_ge(sV.h, v_after)
        Tq(pre_t)

        def pre_s(s):
            s.wait_ge(sG.h, g_after)
        Sq(pre_s)
        barrier()

        xr_st = contextlib.ExitStack()
        x_res = xr_st.enter_context(
            nc.sbuf_tensor("x_res", [128, 16 * TC], F32))

        def x0_load(d):
            for hc in range(16):
                sLN[hc % 4].tag(d.dma_start(
                    out=x_res[:, 1024 * hc:1024 * (hc + 1)],
                    in_=x_t[128 * hc:128 * (hc + 1), :]), 16)
        Dq(x0_load)
        x0_marks = [c.n for c in sLN]

        cst_st = contextlib.ExitStack()
        cwf = cst_st.enter_context(nc.sbuf_tensor("cwf", [128, 1024], F32))
        cwb = cst_st.enter_context(nc.sbuf_tensor("cwb", [128, 1024], BF16))

        with contextlib.ExitStack() as st2:
            E2 = st2.enter_context
            gup_s = E2(nc.sbuf_tensor("gup_s", [128, NGT * 256], I32))
            dnp_s = E2(nc.sbuf_tensor("dnp_s", [128, NDT * 704], I32))
            gus_s = E2(nc.sbuf_tensor("gus_s", [16, OGU_C], F16))
            dns_s = E2(nc.sbuf_tensor("dns_s", [44, DH_C], F16))
            gusf = E2(nc.sbuf_tensor("gusf", [16, OGU_C], F32))
            dnsf = E2(nc.sbuf_tensor("dnsf", [44, DH_C], F32))
            scT_gu = E2(nc.sbuf_tensor("scT_gu", [128, NGT * 16], F32))
            scT_d = E2(nc.sbuf_tensor("scT_d", [128, NDT * 44], F32))
            wnat = E2(nc.sbuf_tensor("wnat", [128, IFF], BF16))
            pstage = [E2(nc.sbuf_tensor(f"pstage{i}", [128, 512], BF16))
                      for i in range(2)]
            wrk = {}
            for en in range(3):
                wrk[en] = {k: E2(nc.sbuf_tensor(f"{k}_{en}", [128, 704], I32))
                           for k in ("w2", "fk2", "fb", "s2", "vs")}
                for k in ("m1", "aa", "fm", "vv"):
                    wrk[en][k] = E2(nc.sbuf_tensor(f"{k}_{en}", [128, 1408],
                                                   BF16))
            psSC = E2(nc.psum_tensor("psSC", [128, 64], F32))
            psW = [E2(nc.psum_tensor(f"psW{i}", [128, 512], F32))
                   for i in range(4)]

            cast_state = {"g": 0, "wq": 0, "wo": 0}

            def cast_job(src_ap, dst_ap, cols):
                def job(g):
                    if cast_state["g"]:
                        g.wait_ge(sG.h, cast_state["g"])
                    sXr[0].tag(g.dma_start(out=cwf[:, :cols], in_=src_ap),
                               16)
                    g.wait_ge(sXr[0].h, sXr[0].n)
                    if cast_state["wo"]:
                        g.wait_ge(sXr[1].h, cast_state["wo"])
                    sG.tag(g.tensor_copy(cwb[:, :cols], cwf[:, :cols]))
                    cast_state["g"] = sG.n
                    g.wait_ge(sG.h, cast_state["g"])
                    sXr[1].tag(g.dma_start(out=dst_ap, in_=cwb[:, :cols]),
                               16)
                    cast_state["wo"] = sXr[1].n
                return job

            filler = []
            for hc in range(16):
                filler.append(cast_job(wqkv[128 * hc:128 * (hc + 1), :],
                                       wq_bf[128 * hc:128 * (hc + 1), :],
                                       6 * HD))
            for hh in range(16):
                for cl in range(2):
                    filler.append(cast_job(
                        wout[128 * hh:128 * (hh + 1),
                             1024 * cl:1024 * (cl + 1)],
                        wo_bf[128 * hh:128 * (hh + 1),
                              1024 * cl:1024 * (cl + 1)], 1024))
            wq_cast_total = 16 * 16
            all_cast_total = 16 * len(filler)

            def fill_one():
                if filler:
                    Gq(filler.pop(0))

            def p2_in(d):
                for ot in range(NGT):
                    sD.tag(d.dma_start(out=gup_s[:, 256 * ot:256 * (ot + 1)],
                                       in_=gup[128 * ot:128 * (ot + 1), :]),
                           16)
                for ht in range(NDT):
                    sD.tag(d.dma_start(out=dnp_s[:, 704 * ht:704 * (ht + 1)],
                                       in_=dnp[128 * ht:128 * (ht + 1), :]),
                           16)
                sD.tag(d.dma_start(out=gus_s[:, :], in_=gus[:, :]), 16)
                sD.tag(d.dma_start(out=dns_s[:, :], in_=dns[:, :]), 16)
            Dq(p2_in)
            d_in = sD.n

            def p2_scv(v):
                v.wait_ge(sD.h, d_in)
                sV.tag(v.tensor_scalar(out=gusf[:, :], in0=gus_s[:, :],
                                       scalar1=256.0, scalar2=None,
                                       op0=ALU.mult))
                sV.tag(v.tensor_scalar(out=dnsf[:, :], in0=dns_s[:, :],
                                       scalar1=256.0, scalar2=None,
                                       op0=ALU.mult))
            Vq(p2_scv)
            v_scf = sV.n

            sct_jobs = ([("gu", ot) for ot in range(NGT)] +
                        [("d", ht) for ht in range(NDT)])
            sct_state = {"v": 0}
            for kind, idx in sct_jobs:
                vprev = sct_state["v"]

                def tmm(t, kind=kind, idx=idx, vprev=vprev):
                    t.wait_ge(sV.h, max(v_scf, vprev))
                    if kind == "gu":
                        sT.tag(t.matmul(psSC[:, :16],
                                        gusf[:, 128 * idx:128 * (idx + 1)],
                                        ident_f[:16, :16],
                                        start=True, stop=True))
                    else:
                        sT.tag(t.matmul(psSC[:, :44],
                                        dnsf[:, 128 * idx:128 * (idx + 1)],
                                        ident_f[:44, :44],
                                        start=True, stop=True))
                Tq(tmm)
                tmark = sT.n

                def vcp(v, kind=kind, idx=idx, tmark=tmark):
                    v.wait_ge(sT.h, tmark)
                    if kind == "gu":
                        sV.tag(v.tensor_copy(
                            scT_gu[:, 16 * idx:16 * (idx + 1)], psSC[:, :16]))
                    else:
                        sV.tag(v.tensor_copy(
                            scT_d[:, 44 * idx:44 * (idx + 1)], psSC[:, :44]))
                Vq(vcp)
                sct_state["v"] = sV.n

            def dq_chain(e, wtile, W, cnt):
                last = [0, 0, 0]

                def step(c, instr):
                    cnt.tag(instr)
                    last[c] = cnt.n

                def wt(c):
                    if last[c]:
                        e.wait_ge(cnt.h, last[c])

                def ops_for(jp):
                    c = jp % 3
                    w = wrk[c]
                    o = []
                    if jp == 0:
                        o.append(lambda: (wt(c), step(c, e.tensor_scalar(
                            out=w["w2"][:, :W], in0=wtile, scalar1=0x70007,
                            scalar2=None, op0=ALU.bitwise_and))))
                    else:
                        o.append(lambda: (wt(c), step(c, e.tensor_scalar(
                            out=w["w2"][:, :W], in0=wtile, scalar1=4 * jp,
                            scalar2=0x70007,
                            op0=ALU.logical_shift_right,
                            op1=ALU.bitwise_and))))
                    o.append(lambda: (wt(c), step(c, e.tensor_scalar(
                        out=w["fk2"][:, :W], in0=w["w2"][:, :W],
                        scalar1=0x43004300, scalar2=None,
                        op0=ALU.bitwise_or))))
                    o.append(lambda: (wt(c), step(c, e.tensor_scalar(
                        out=w["fb"][:, :W], in0=w["w2"][:, :W],
                        scalar1=64, scalar2=0x3B003B00,
                        op0=ALU.mult, op1=ALU.add))))
                    o.append(lambda: (wt(c), step(c, e.tensor_scalar(
                        out=w["s2"][:, :W], in0=wtile, scalar1=12 - 4 * jp,
                        scalar2=-0x7FFF8000,
                        op0=ALU.logical_shift_left,
                        op1=ALU.bitwise_and))))
                    fkb = w["fk2"][:, :W].bitcast(BF16)
                    o.append(lambda: (wt(c), step(c, e.tensor_scalar(
                        out=w["m1"][:, :2 * W], in0=fkb, scalar1=130.0,
                        scalar2=129.0, op0=ALU.min, op1=ALU.subtract))))
                    o.append(lambda: (wt(c), step(c, e.tensor_scalar(
                        out=w["aa"][:, :2 * W], in0=fkb,
                        scalar1=0.001953125, scalar2=0.25,
                        op0=ALU.mult, op1=ALU.subtract))))
                    o.append(lambda: (wt(c), step(c, e.tensor_tensor(
                        out=w["fm"][:, :2 * W],
                        in0=w["fb"][:, :W].bitcast(BF16),
                        in1=w["m1"][:, :2 * W], op=ALU.mult))))
                    o.append(lambda: (wt(c), step(c, e.tensor_tensor(
                        out=w["vv"][:, :2 * W], in0=w["fm"][:, :2 * W],
                        in1=w["aa"][:, :2 * W], op=ALU.max))))
                    o.append(lambda: (wt(c), step(c, e.tensor_tensor(
                        out=w["vs"][:, :W],
                        in0=w["vv"][:, :2 * W].bitcast(I32),
                        in1=w["s2"][:, :W], op=ALU.bitwise_or))))
                    vb = w["vs"][:, :W].bitcast(BF16)
                    o.append(lambda: (wt(c), step(c, e.tensor_copy(
                        wnat[:, jp:8 * W:8], vb[:, 0:2 * W:2]))))
                    o.append(lambda: (wt(c), step(c, e.tensor_copy(
                        wnat[:, jp + 4:8 * W:8], vb[:, 1:2 * W:2]))))
                    return o
                for grp in ((0, 1, 2), (3,)):
                    chains = [ops_for(jp) for jp in grp]
                    for s in range(max(len(ch) for ch in chains)):
                        for ch in chains:
                            if s < len(ch):
                                ch[s]()

            wt_state = {}

            def do_tile(wtile, W, scT, sc_off, ngrp, piece_ap, pcol,
                        tr_prev_v):
                def dv(v, wtile=wtile, W=W, tr_prev_v=tr_prev_v):
                    v.wait_ge(sT.h, tr_prev_v)
                    dq_chain(v, wtile, W, sV)
                Vq(dv)

                def sc(v, scT=scT, sc_off=sc_off, ngrp=ngrp):
                    v.wait_ge(sV.h, sV.n)
                    for g in range(ngrp):
                        sV.tag(v.tensor_scalar(
                            out=wnat[:, 128 * g:128 * (g + 1)],
                            in0=wnat[:, 128 * g:128 * (g + 1)],
                            scalar1=scT[:, sc_off + g:sc_off + g + 1],
                            scalar2=None, op0=ALU.mult))
                        v.wait_ge(sV.h, sV.n)
                Vq(sc)
                scdone = sV.n

                for g0 in range(0, ngrp, 4):
                    par = (g0 // 4) % 2
                    vprev = wt_state.get("evac", 0)

                    def tmm(t, g0=g0, vprev=vprev, scdone=scdone):
                        t.wait_ge(sV.h, max(vprev, scdone))
                        for k in range(4):
                            g = g0 + k
                            sT.tag(t.matmul(psW[k][:, :128],
                                            wnat[:, 128 * g:128 * (g + 1)],
                                            ident_b[:, :],
                                            start=True, stop=True))
                    Tq(tmm)
                    tmark = sT.n
                    dprev = wt_state.get(("dma", par), 0)

                    def vev(v, par=par, tmark=tmark, dprev=dprev):
                        v.wait_ge(sT.h, tmark)
                        if dprev:
                            v.wait_ge(sPp[par].h, dprev)
                        for k in range(4):
                            sV.tag(v.tensor_copy(
                                pstage[par][:, 128 * k:128 * (k + 1)],
                                psW[k][:, :128]))
                    Vq(vev)
                    wt_state["evac"] = sV.n
                    wt_state[("evac", par)] = sV.n
                    vmark = sV.n

                    def dst(d, g0=g0, par=par, vmark=vmark,
                            piece_ap=piece_ap, pcol=pcol):
                        d.wait_ge(sV.h, vmark)
                        sPp[par].tag(d.dma_start(
                            out=piece_ap[:, pcol + 128 * g0:
                                         pcol + 128 * g0 + 512],
                            in_=pstage[par][:, :]), 16)
                    Dq(dst)
                    wt_state[("dma", par)] = sPp[par].n
                return sT.n

            cc_gu_parts = []
            tr_prev = 0
            for ot in range(NGT):
                part = 0 if ot < 4 else (1 if ot < 8 else 2)
                lo_p = GU_PARTS[part][0]
                tr_prev = do_tile(gup_s[:, 256 * ot:256 * (ot + 1)], 256,
                                  scT_gu, 16 * ot, 16, wgu_piece[part],
                                  2048 * (ot - lo_p), tr_prev)
                if ot + 1 in (4, 8):
                    pi = 0 if ot + 1 == 4 else 1
                    pp = [sPp[0].n, sPp[1].n]

                    def p2_ccp(g, pi=pi, pp=pp):
                        g.wait_ge(sPp[0].h, pp[0])
                        g.wait_ge(sPp[1].h, pp[1])
                        sCC.tag(g.collective_compute(
                            "AllGather", ALU.bypass,
                            replica_groups=[list(range(8))],
                            ins=[wgu_piece[pi]], outs=[g_gu[pi]]), 1)
                    Gq(p2_ccp)
                    cc_gu_parts.append((ot + 1, sCC.n))
            gu_pieces = [sPp[0].n, sPp[1].n]

            def p2_cc2(g):
                g.wait_ge(sPp[0].h, gu_pieces[0])
                g.wait_ge(sPp[1].h, gu_pieces[1])
                sCC.tag(g.collective_compute(
                    "AllGather", ALU.bypass, replica_groups=[list(range(8))],
                    ins=[wgu_piece[2]], outs=[g_gu[2]]), 1)
            Gq(p2_cc2)
            cc_gu_parts.append((11, sCC.n))
            for ht in range(NDT):
                tr_prev = do_tile(dnp_s[:, 704 * ht:704 * (ht + 1)], 704,
                                  scT_d, 44 * ht, 44, wd_piece, 5632 * ht,
                                  tr_prev)
            dma_pieces = [sPp[0].n, sPp[1].n]

            def p2_cc(g):
                g.wait_ge(sPp[0].h, dma_pieces[0])
                g.wait_ge(sPp[1].h, dma_pieces[1])
                sCC.tag(g.collective_compute(
                    "AllGather", ALU.bypass, replica_groups=[list(range(8))],
                    ins=[wd_piece], outs=[g_d]), 1)
            Gq(p2_cc)
            cc_box[0] = cc_gu_parts
            cc_box[1] = sCC.n
        barrier(ex=(sCC,))
        def ln_pass(tb, xres, psX, psX2, psM, psR, deps, xh=None,
                    piece_ap=None, sg_T=0, pg_V=0, bg_V=0, ag_T=0):
            res = {}
            with contextlib.ExitStack() as stL:
                EL = stL.enter_context

                def xsl(hc):
                    return xres[:, 1024 * hc + 512 * tb:
                                1024 * hc + 512 * (tb + 1)]
                xb = [EL(nc.sbuf_tensor(uid(f"ln_xb{i}"), [128, 512], BF16))
                      for i in range(4)]
                sq = [EL(nc.sbuf_tensor(uid(f"ln_sq{i}"), [128, 512], BF16))
                      for i in range(4)]
                rw = [EL(nc.sbuf_tensor(uid(f"ln_rw{i}"), [1, 512], F32))
                      for i in range(4)]
                rwb = [EL(nc.sbuf_tensor(uid(f"ln_rwb{i}"), [1, 512], BF16))
                       for i in range(2)]
                tmpf = [EL(nc.sbuf_tensor(uid(f"ln_tmp{i}"), [128, 512], F32))
                        for i in range(2)]
                xst = ([EL(nc.sbuf_tensor(uid(f"ln_xst{i}"), [128, 512],
                                          BF16)) for i in range(2)]
                       if piece_ap is not None else None)
                stats_t = {}
                pc_state = {0: sPx[0].n, 1: sPx[1].n}
                first = {"v": True, "s": True, "t": True}
                for g4 in range(4):
                    hcs = list(range(4 * g4, 4 * g4 + 4))
                    tprev = stats_t.get(g4 - 1, 0)

                    def cst(v, hcs=hcs, tprev=tprev):
                        if first["v"]:
                            first["v"] = False
                            for cnt_, val_ in deps:
                                if val_:
                                    v.wait_ge(cnt_.h, val_)
                            if sg_T:
                                v.wait_ge(sT.h, sg_T)
                        if tprev:
                            v.wait_ge(sT.h, tprev)
                        for hc in hcs:
                            sV.tag(v.tensor_copy(xb[hc % 4][:, :], xsl(hc)))
                    Vq(cst)
                    vmark = sV.n

                    def sqr(s, hcs=hcs, tprev=tprev):
                        if first["s"]:
                            first["s"] = False
                            for cnt_, val_ in deps:
                                if val_:
                                    s.wait_ge(cnt_.h, val_)
                            if sg_T:
                                s.wait_ge(sT.h, sg_T)
                        if tprev:
                            s.wait_ge(sT.h, tprev)
                        for hc in hcs:
                            sSc.tag(s.activation(out=sq[hc % 4][:, :],
                                                 in_=xsl(hc),
                                                 func=AF.Square))
                    Sq(sqr)
                    smark = sSc.n

                    def smm(t, hcs=hcs, vmark=vmark, smark=smark):
                        if first["t"]:
                            first["t"] = False
                            if pg_V:
                                t.wait_ge(sV.h, pg_V)
                        t.wait_ge(sV.h, vmark)
                        t.wait_ge(sSc.h, smark)
                        for hc in hcs:
                            sT.tag(t.matmul(psX, ones_c[:, :],
                                            xb[hc % 4][:, :],
                                            start=(hc == 0),
                                            stop=(hc == 15)))
                            sT.tag(t.matmul(psX2, ones_c[:, :],
                                            sq[hc % 4][:, :],
                                            start=(hc == 0),
                                            stop=(hc == 15)))
                    Tq(smm)
                    stats_t[g4] = sT.n
                res["sq"] = sSc.n
                res["statsT"] = sT.n
                tstats = sT.n

                def cmb_v(v, tstats=tstats):
                    v.wait_ge(sT.h, tstats)
                    sV.tag(v.tensor_scalar(out=rw[0][:, :], in0=psX,
                                           scalar1=1.0 / 2048,
                                           scalar2=None, op0=ALU.mult))
                    sV.tag(v.tensor_scalar(out=rw[1][:, :], in0=psX2,
                                           scalar1=1.0 / 2048,
                                           scalar2=None, op0=ALU.mult))
                    v.wait_ge(sV.h, sV.n)
                    sV.tag(v.tensor_tensor(out=rw[2][:, :], in0=rw[0][:, :],
                                           in1=rw[0][:, :], op=ALU.mult))
                    v.wait_ge(sV.h, sV.n)
                    sV.tag(v.tensor_tensor(out=rw[2][:, :], in0=rw[1][:, :],
                                           in1=rw[2][:, :],
                                           op=ALU.subtract))
                Vq(cmb_v)
                vvar = sV.n

                def cmb_s(s, vvar=vvar):
                    s.wait_ge(sV.h, vvar)
                    sSc.tag(s.activation(out=rw[3][:, :], in_=rw[2][:, :],
                                         func=AF.Sqrt,
                                         bias=epst[:1, 0:1], scale=1.0))
                Sq(cmb_s)
                ssqrt = sSc.n

                def cmb_v2(v, ssqrt=ssqrt):
                    v.wait_ge(sSc.h, ssqrt)
                    sV.tag(v.reciprocal(rw[3][:, :], rw[3][:, :]))
                    v.wait_ge(sV.h, sV.n)
                    sV.tag(v.tensor_copy(rwb[0][:, :], rw[3][:, :]))
                    sV.tag(v.scalar_tensor_tensor(out=rw[0][:, :],
                                                  in0=rw[0][:, :],
                                                  scalar=-1.0,
                                                  in1=rw[3][:, :],
                                                  op0=ALU.mult,
                                                  op1=ALU.mult))
                    v.wait_ge(sV.h, sV.n)
                    sV.tag(v.tensor_copy(rwb[1][:, :], rw[0][:, :]))
                Vq(cmb_v2)
                vrows = sV.n
                res["cmb"] = vrows

                def bcast(t, vrows=vrows):
                    if bg_V:
                        t.wait_ge(sV.h, bg_V)
                    t.wait_ge(sV.h, vrows)
                    sT.tag(t.matmul(psR, ones_r[:, :], rwb[0][:, :],
                                    start=True, stop=True))
                    sT.tag(t.matmul(psM, ones_r[:, :], rwb[1][:, :],
                                    start=True, stop=True))
                Tq(bcast)
                tbc = sT.n

                firstA = [True]
                for hc in range(16):
                    par = hc % 2

                    def app(v, hc=hc, par=par, tbc=tbc,
                            pprev=pc_state.get(par, 0)):
                        if firstA[0]:
                            firstA[0] = False
                            if ag_T:
                                v.wait_ge(sT.h, ag_T)
                        v.wait_ge(sT.h, tbc)
                        if piece_ap is not None and pprev:
                            v.wait_ge(sPx[par].h, pprev)
                        sV.tag(v.tensor_tensor(
                            out=tmpf[par][:, :], in0=xsl(hc),
                            in1=psR, op=ALU.mult))
                        v.wait_ge(sV.h, sV.n)
                        if xh is not None:
                            dst = xh[:, 512 * hc:512 * (hc + 1)]
                        else:
                            dst = xst[par][:, :]
                        sV.tag(v.tensor_tensor(out=dst, in0=tmpf[par][:, :],
                                               in1=psM, op=ALU.add))
                    Vq(app)
                    vap = sV.n
                    if piece_ap is not None:
                        def pdma(d, hc=hc, par=par, vap=vap):
                            d.wait_ge(sV.h, vap)
                            sPx[par].tag(d.dma_start(
                                out=piece_ap[128 * hc:128 * (hc + 1), :],
                                in_=xst[par][:, :]), 16)
                        Sq(pdma)
                        pc_state[par] = sPx[par].n
                res["done"] = sV.n
            return res

        def ffn_phase(xres, out_ap, deps0):
            with contextlib.ExitStack() as stFF:
                EFF = stFF.enter_context
                xh = EFF(nc.sbuf_tensor(uid("xh"), [128, 16 * 512], BF16))
                h_all = EFF(nc.sbuf_tensor(uid("h"), [128, 44 * 512], BF16))
                wg = [EFF(nc.sbuf_tensor(uid(f"wg{i}"), [128, HID], BF16))
                      for i in range(4)]
                sg = [EFF(nc.sbuf_tensor(uid(f"sg{i}"), [128, 512], BF16))
                      for i in range(2)]
                wd = [EFF(nc.sbuf_tensor(uid(f"wdn{i}"), [128, IFF], BF16))
                      for i in range(2)]
                psGU = [EFF(nc.psum_tensor(uid(f"psGU{i}"), [128, 512], F32))
                        for i in range(4)]
                psD_ = [EFF(nc.psum_tensor(uid(f"psD{i}"), [128, 512], F32))
                        for i in range(2)]
                psSR = EFF(nc.psum_tensor(uid("psSR"), [128, 512], F32))

                GU_ORDER = ([i for i in range(44) if i % NGT < 4]
                            + [i for i in range(44) if 4 <= i % NGT < 8]
                            + [i for i in range(44) if i % NGT >= 8])
                wdma, mmt, ept = {}, {}, {}
                ddma, dmmt, dept = {}, {}, {}
                ln_res, h_done, state = {}, {}, {}

                def wdma_it(d, pos):
                    par = pos % 2
                    if pos >= 2:
                        d.wait_ge(sT.h, mmt[pos - 2])
                    itp = GU_ORDER[pos % 44]
                    need = 0
                    for half in range(2):
                        j = itp + 44 * half
                        ol = j % NGT
                        for hi, mark in cc_box[0]:
                            if ol < hi:
                                need = max(need, mark)
                                break
                    d.wait_ge(sCC.h, need)
                    for half in range(2):
                        j = itp + 44 * half
                        r, ol = j // NGT, j % NGT
                        part = 0 if ol < 4 else (1 if ol < 8 else 2)
                        olp = ol - (0, 4, 8)[part]
                        sWg[par].tag(d.dma_start(
                            out=wg[2 * par + half][:, :],
                            in_=g_gu[part][128 * r:128 * (r + 1),
                                           2048 * olp:2048 * (olp + 1)]), 16)
                    wdma[pos] = sWg[par].n

                def ddma_m(d, m):
                    par = m % 2
                    if m < 2:
                        d.wait_ge(sCC.h, cc_box[1])
                    if m >= 2:
                        d.wait_ge(sT.h, dmmt[m - 2])
                    mp = m % 16
                    r, hl = mp // 2, mp % 2
                    sWd[par].tag(d.dma_start(
                        out=wd[par][:, :],
                        in_=g_d[128 * r:128 * (r + 1),
                                5632 * hl:5632 * (hl + 1)]), 16)
                    ddma[m] = sWd[par].n

                def wq_emit(it):
                    (Dq if it % 2 == 0 else Gq)(
                        lambda d, it=it: wdma_it(d, it))

                def dq_emit(m):
                    (Dq if m % 2 == 0 else Gq)(
                        lambda d, m=m: ddma_m(d, m))

                def gu_pass(p):
                    ln_done = ln_res[p]["done"]
                    h_guard = state.get("down_T", 0)
                    if p == 0:
                        wq_emit(0)
                        wq_emit(1)
                    for posp in range(44):
                        pos = 44 * p + posp
                        itp = GU_ORDER[posp]
                        par = pos % 2

                        def mm(t, pos=pos, posp=posp, par=par,
                               ln_done=ln_done):
                            t.wait_ge(sWg[par].h, wdma[pos])
                            if posp < 2:
                                t.wait_ge(sV.h, ln_done)
                            if pos >= 2:
                                t.wait_ge(sV.h, ept[pos - 2])
                            for hc in range(16):
                                for half in range(2):
                                    i_ = t.matmul(
                                        psGU[2 * par + half][:, :],
                                        wg[2 * par + half][
                                            :, 128 * hc:128 * (hc + 1)],
                                        xh[:, 512 * hc:512 * (hc + 1)],
                                        start=(hc == 0), stop=(hc == 15))
                                    if hc == 15:
                                        sT.tag(i_)
                            mmt[pos] = sT.n
                        Tq(mm)
                        nxt = pos + 2
                        if (p == 0 and nxt < 46) or (p == 1 and
                                                     46 <= nxt < 88):
                            wq_emit(nxt)
                        if pos % 2 == 1:
                            fill_one()
                        if p == 1 and posp == 10:
                            dq_emit(16)
                            dq_emit(17)

                        def ep_s(s, pos=pos, par=par):
                            s.wait_ge(sT.h, mmt[pos])
                            sSc.tag(s.activation(out=sg[par][:, :],
                                                 in_=psGU[2 * par][:, :],
                                                 func=AF.Silu))
                        Sq(ep_s)
                        smark = sSc.n

                        def ep_v(v, pos=pos, posp=posp, itp=itp, par=par,
                                 smark=smark, h_guard=h_guard):
                            v.wait_ge(sSc.h, smark)
                            if posp == 0 and h_guard:
                                v.wait_ge(sT.h, h_guard)
                            sV.tag(v.tensor_tensor(
                                out=h_all[:, 512 * itp:512 * (itp + 1)],
                                in0=sg[par][:, :],
                                in1=psGU[2 * par + 1][:, :], op=ALU.mult))
                        Vq(ep_v)
                        ept[pos] = sV.n
                    h_done[p] = sV.n

                def down_pass(p):
                    m0 = 16 * p
                    if p == 0:
                        dq_emit(0)
                        dq_emit(1)
                    stq = {}
                    for mp in range(16):
                        m = m0 + mp
                        par = m % 2

                        def mm(t, m=m, mp=mp, par=par, p=p):
                            t.wait_ge(sWd[par].h, ddma[m])
                            if mp == 0:
                                t.wait_ge(sV.h, h_done[p])
                                if p == 0:
                                    t.wait_ge(sV.h, ln_res[1]["cmb"])
                            if m >= 2:
                                t.wait_ge(sV.h, dept[m - 2][0])
                                t.wait_ge(sSc.h, dept[m - 2][1])
                            for ic in range(44):
                                i_ = t.matmul(
                                    psD_[par][:, :],
                                    wd[par][:, 128 * ic:128 * (ic + 1)],
                                    h_all[:, 512 * ic:512 * (ic + 1)],
                                    start=(ic == 0), stop=(ic == 43))
                                if ic == 43:
                                    sT.tag(i_)
                            dmmt[m] = sT.n
                        Tq(mm)
                        nxt = m + 2
                        if (p == 0 and nxt < 16) or (p == 1 and
                                                     18 <= nxt < 32):
                            dq_emit(nxt)

                        def ep_s(s, m=m, par=par):
                            s.wait_ge(sT.h, dmmt[m])
                            sSc.tag(s.activation(out=sqd[par][:, :],
                                                 in_=psD_[par][:, :],
                                                 func=AF.Square))
                        Sq(ep_s)
                        smark = sSc.n

                        def ep_v(v, m=m, mp=mp, par=par,
                                 rg=state.get("radd_G", 0) if p else 0):
                            v.wait_ge(sT.h, dmmt[m])
                            if mp == 0 and rg:
                                v.wait_ge(sG.h, rg)
                            sV.tag(v.tensor_copy(
                                y_all[:, 512 * mp:512 * (mp + 1)],
                                psD_[par][:, :]))
                        Vq(ep_v)
                        dept[m] = (sV.n, smark)
                        stq[mp] = (par, smark)

                        if mp >= 1:
                            def st_mm(t, mp=mp, stq=stq):
                                par_, smark_ = stq[mp - 1]
                                t.wait_ge(sSc.h, smark_)
                                if mp - 1 == 0:
                                    rg = state.get("rr_ev", 0)
                                    if rg:
                                        t.wait_ge(sV.h, rg)
                                sT.tag(t.matmul(psSR[0:1, :], ones_c[:, :],
                                                sqd[par_][:, :],
                                                start=(mp - 1 == 0),
                                                stop=False))
                            Tq(st_mm)

                    def st_last(t, stq=stq):
                        par_, smark_ = stq[15]
                        t.wait_ge(sSc.h, smark_)
                        sT.tag(t.matmul(psSR[0:1, :], ones_c[:, :],
                                        sqd[par_][:, :],
                                        start=False, stop=True))
                    Tq(st_last)
                    tstats = sT.n
                    state["down_T"] = dmmt[m0 + 15]

                    def rms_s(s, tstats=tstats):
                        s.wait_ge(sT.h, tstats)
                        sSc.tag(s.activation(out=rrow[:, :],
                                             in_=psSR[0:1, :],
                                             func=AF.Sqrt,
                                             bias=epst[:1, 1:2],
                                             scale=4.0 / 2048))
                    Sq(rms_s)
                    srms = sSc.n

                    def rms_v(v, srms=srms):
                        v.wait_ge(sSc.h, srms)
                        sV.tag(v.reciprocal(rrow[:, :], rrow[:, :]))
                        v.wait_ge(sV.h, sV.n)
                        sV.tag(v.tensor_copy(rrowb[:, :], rrow[:, :]))
                    Vq(rms_v)
                    vrms = sV.n

                    def rms_t(t, vrms=vrms):
                        t.wait_ge(sV.h, vrms)
                        sT.tag(t.matmul(psSR[:, :], ones_r[:, :],
                                        rrowb[:, :], start=True, stop=True))
                    Tq(rms_t)
                    trr = sT.n

                    def rr_ev(v, trr=trr):
                        v.wait_ge(sT.h, trr)
                        sV.tag(v.tensor_copy(rrb_s[:, :], psSR[:, :]))
                    Vq(rr_ev)
                    state["rr_ev"] = sV.n

                def radd_pass(p):
                    vrr2 = state["rr_ev"]
                    lsq = ln_res[p]["sq"]
                    firste = {"v": True, "g": True}
                    for hc in range(16):
                        use_g = (hc % 2 == 1)
                        ecnt = sG if use_g else sV
                        tp = tpfG if use_g else tpfV

                        def radd(e, hc=hc, p=p, use_g=use_g, ecnt=ecnt,
                                 tp=tp, vrr2=vrr2, lsq=lsq):
                            kk = "g" if use_g else "v"
                            if firste[kk]:
                                firste[kk] = False
                                e.wait_ge(sSc.h, lsq)
                                if use_g:
                                    e.wait_ge(sV.h, vrr2)
                            else:
                                e.wait_ge(ecnt.h, ecnt.n)
                            ecnt.tag(e.tensor_tensor(
                                out=tp[:, :],
                                in0=y_all[:, 512 * hc:512 * (hc + 1)],
                                in1=rrb_s[:, :], op=ALU.mult))
                            e.wait_ge(ecnt.h, ecnt.n)
                            ecnt.tag(e.tensor_tensor(
                                out=xres[:, 1024 * hc + 512 * p:
                                         1024 * hc + 512 * (p + 1)],
                                in0=tp[:, :],
                                in1=xres[:, 1024 * hc + 512 * p:
                                         1024 * hc + 512 * (p + 1)],
                                op=ALU.add))
                        (Gq if use_g else Vq)(radd)
                        emark = ecnt.n
                        if out_ap is not None:
                            def xs(d, hc=hc, p=p, emark=emark, ecnt=ecnt):
                                d.wait_ge(ecnt.h, emark)
                                sOut[hc % 4].tag(d.dma_start(
                                    out=out_ap[128 * hc:128 * (hc + 1),
                                               512 * p:512 * (p + 1)],
                                    in_=xres[:, 1024 * hc + 512 * p:
                                             1024 * hc + 512 * (p + 1)]),
                                    16)
                            Sq(xs)
                    state["radd_G"] = sG.n

                ln_res[0] = ln_pass(0, xres, psD_[0][0:1, :],
                                    psD_[1][0:1, :], psGU[2][:, :],
                                    psGU[3][:, :], deps0, xh=xh)
                gu_pass(0)
                ln_res[1] = ln_pass(1, xres, psD_[0][0:1, :],
                                    psD_[1][0:1, :], psGU[2][:, :],
                                    psGU[3][:, :], deps0, xh=xh,
                                    sg_T=ln_res[0]["statsT"],
                                    pg_V=ln_res[0]["cmb"],
                                    bg_V=ept[43], ag_T=mmt[43])
                with contextlib.ExitStack() as stY:
                    EY = stY.enter_context
                    y_all = EY(nc.sbuf_tensor(uid("y"), [128, 16 * 512],
                                              BF16))
                    sqd = [EY(nc.sbuf_tensor(uid(f"sqd{i}"), [128, 512],
                                             BF16)) for i in range(2)]
                    rrow = EY(nc.sbuf_tensor(uid("rrow"), [1, 512], F32))
                    rrowb = EY(nc.sbuf_tensor(uid("rrowb"), [1, 512], BF16))
                    rrb_s = EY(nc.sbuf_tensor(uid("rrb"), [128, 512], F32))
                    tpfV = EY(nc.sbuf_tensor(uid("tpfV"), [128, 512], F32))
                    tpfG = EY(nc.sbuf_tensor(uid("tpfG"), [128, 512], F32))
                    down_pass(0)
                    radd_pass(0)
                    gu_pass(1)
                    down_pass(1)
                    radd_pass(1)
                return sV.n, sG.n

        res1_v, res1_g = ffn_phase(
            x_res, None, [(sLN[i], x0_marks[i]) for i in range(4)])
        while filler:
            fill_one()
        cst_st.close()
        barrier()

        def spill(d):
            for hc in range(16):
                sPz[1].tag(d.dma_start(
                    out=x_sp[128 * hc:128 * (hc + 1), :],
                    in_=x_res[:, 1024 * hc:1024 * (hc + 1)]), 16)
        Dq(spill)

        cc_x = [0, 0]
        with contextlib.ExitStack() as stL2:
            EL2 = stL2.enter_context
            psX_ = EL2(nc.psum_tensor("ps2X", [1, 512], F32))
            psX2_ = EL2(nc.psum_tensor("ps2X2", [1, 512], F32))
            psM_ = EL2(nc.psum_tensor("ps2M", [128, 512], F32))
            psR_ = EL2(nc.psum_tensor("ps2R", [128, 512], F32))
            l20 = ln_pass(0, x_res, psX_[:, :], psX2_[:, :], psM_[:, :],
                          psR_[:, :], [], piece_ap=xh_p[0])
            dp0 = [sPx[0].n, sPx[1].n]

            def p6_cc0(g):
                for i_ in range(2):
                    g.wait_ge(sPx[i_].h, dp0[i_])
                sCC.tag(g.collective_compute(
                    "AllGather", ALU.bypass,
                    replica_groups=[list(range(8))],
                    ins=[xh_p[0]], outs=[g_xs[0]]), 1)
            Gq(p6_cc0)
            cc_x[0] = sCC.n
            l21 = ln_pass(1, x_res, psX_[:, :], psX2_[:, :], psM_[:, :],
                          psR_[:, :], [], piece_ap=xh_p[1],
                          sg_T=l20["statsT"], pg_V=l20["cmb"],
                          bg_V=l20["done"])
        dp1 = [sPx[0].n, sPx[1].n]

        def p6_cc1(g):
            for i_ in range(2):
                g.wait_ge(sPx[i_].h, dp1[i_])
            sCC.tag(g.collective_compute(
                "AllGather", ALU.bypass, replica_groups=[list(range(8))],
                ins=[xh_p[1]], outs=[g_xs[1]]), 1)
        Gq(p6_cc1)
        cc_x[1] = sCC.n
        barrier(ex=(sCC,))
        xr_st.close()

        cc_a2a = [0, 0]
        with contextlib.ExitStack() as stO:
            EO = stO.enter_context
            o_tm = [EO(nc.sbuf_tensor(f"otm{i}", [128, 4096], BF16))
                    for i in range(4)]
            with contextlib.ExitStack() as stAtt:
                EA = stAtt.enter_context
                qt = [EA(nc.sbuf_tensor(f"qt{i}", [128, T], BF16))
                      for i in range(2)]
                kt = [EA(nc.sbuf_tensor(f"kt{i}", [128, T], BF16))
                      for i in range(2)]
                vtm = [EA(nc.sbuf_tensor(f"vtm{i}", [128, T], BF16))
                       for i in range(2)]
                bias_u = [EA(nc.sbuf_tensor(f"biasu{i}", [128, 1], F32))
                          for i in range(4)]

                with contextlib.ExitStack() as stC1:
                    EC = stC1.enter_context
                    wqb = EC(nc.sbuf_tensor("wqb", [128, 16 * 6 * HD], BF16))
                    xg = [EC(nc.sbuf_tensor(f"xg{i}", [128, 16 * 512], BF16))
                          for i in range(2)]
                    vstg = EC(nc.sbuf_tensor("vstg", [128, 1024], BF16))
                    psQ = [EC(nc.psum_tensor(f"psQ{i}", [128, 512], F32))
                           for i in range(2)]
                    psVT = [EC(nc.psum_tensor(f"psVT{i}", [128, 512], F32))
                            for i in range(4)]

                    def wql(d):
                        d.wait_ge(sXr[1].h, wq_cast_total)
                        for hc in range(16):
                            sWq.tag(d.dma_start(
                                out=wqb[:, 6 * HD * hc:6 * HD * (hc + 1)],
                                in_=wq_bf[128 * hc:128 * (hc + 1), :]), 16)
                    Dq(wql)
                    wq_marks = sWq.n

                    qkv_mm = {}
                    ev_state = {}
                    xg_dma = {}
                    XG_ORDER = [2 * r + tb for tb in range(2)
                                for r in range(8)]

                    def xg_tt(d, pos):
                        par = pos % 2
                        if pos >= 2:
                            d.wait_ge(sT.h, qkv_mm[pos - 2])
                        tt = XG_ORDER[pos]
                        r, tb = tt // 2, tt % 2
                        d.wait_ge(sCC.h, cc_x[tb])
                        for hc in range(16):
                            sXg[par].tag(d.dma_start(
                                out=xg[par][:, 512 * hc:512 * (hc + 1)],
                                in_=g_xs[tb][2048 * r + 128 * hc:
                                             2048 * r + 128 * (hc + 1),
                                             :]), 16)
                        xg_dma[pos] = sXg[par].n

                    def xg_pre(d):
                        xg_tt(d, 0)
                        xg_tt(d, 1)
                    Dq(xg_pre)

                    for pos in range(16):
                        tt = XG_ORDER[pos]
                        par = pos % 2
                        for o in range(6):
                            opar = o % 2
                            eprev = ev_state.get(opar, 0)

                            def mm(t, tt=tt, par=par, o=o, opar=opar,
                                   eprev=eprev, wq_marks=wq_marks,
                                   pos=pos):
                                t.wait_ge(sXg[par].h, xg_dma[pos])
                                t.wait_ge(sWq.h, wq_marks)
                                if eprev:
                                    t.wait_ge(sV.h, eprev)
                                for hc in range(16):
                                    i_ = t.matmul(
                                        psQ[opar][:, :],
                                        wqb[:, 6 * HD * hc + 128 * o:
                                            6 * HD * hc + 128 * (o + 1)],
                                        xg[par][:, 512 * hc:512 * (hc + 1)],
                                        start=(hc == 0), stop=(hc == 15))
                                    if hc == 15:
                                        sT.tag(i_)
                            Tq(mm)
                            tmark = sT.n
                            if o == 5:
                                qkv_mm[pos] = tmark

                            def ev(v, tt=tt, o=o, opar=opar, tmark=tmark):
                                v.wait_ge(sT.h, tmark)
                                hl = o % 2
                                if o < 2:
                                    dd = qt[hl][:, 512 * tt:512 * (tt + 1)]
                                elif o < 4:
                                    dd = kt[hl][:, 512 * tt:512 * (tt + 1)]
                                else:
                                    dd = vstg[:, 512 * hl:512 * (hl + 1)]
                                sV.tag(v.tensor_copy(dd, psQ[opar][:, :]))
                            Vq(ev)
                            ev_state[opar] = sV.n
                        if pos + 2 < 16:
                            Dq(lambda d, pos=pos: xg_tt(d, pos + 2))
                        vmark = sV.n
                        for hl in range(2):
                            vtprev = ev_state.get("vt", 0)

                            def vtr(t, tt=tt, hl=hl, vmark=vmark,
                                    vtprev=vtprev):
                                t.wait_ge(sV.h, max(vmark, vtprev))
                                for ss in range(4):
                                    sT.tag(t.matmul(
                                        psVT[ss][:, :128],
                                        vstg[:, 512 * hl + 128 * ss:
                                             512 * hl + 128 * (ss + 1)],
                                        ident_b[:, :],
                                        start=True, stop=True))
                            Tq(vtr)
                            tmark2 = sT.n

                            def vev(v, tt=tt, hl=hl, tmark2=tmark2):
                                v.wait_ge(sT.h, tmark2)
                                for ss in range(4):
                                    sV.tag(v.tensor_copy(
                                        vtm[hl][:, 512 * tt + 128 * ss:
                                                512 * tt + 128 * (ss + 1)],
                                        psVT[ss][:, :128]))
                            Vq(vev)
                            ev_state["vt"] = sV.n
                    barrier()

                with contextlib.ExitStack() as stN:
                    EN = stN.enter_context
                    nsq = [EN(nc.sbuf_tensor(f"nsq{i}", [128, 512], BF16))
                           for i in range(4)]
                    nrow = EN(nc.sbuf_tensor("nrow", [1, 8], F32))
                    psN4 = [EN(nc.psum_tensor(f"psN4_{i}", [1, 512], F32))
                            for i in range(4)]
                    nmx = EN(nc.sbuf_tensor("nmx", [1, 16], F32))
                    ub1 = EN(nc.sbuf_tensor("ub1", [1, 4], F32))
                    ub1b = EN(nc.sbuf_tensor("ub1b", [1, 4], BF16))
                    psN = EN(nc.psum_tensor("psN", [1, 512], F32))
                    psB = EN(nc.psum_tensor("psB", [128, 4], F32))
                    n_state = {}
                    for u in range(4):
                        hl, b = u // 2, u % 2
                        for which, tens in ((0, qt[hl]), (1, kt[hl])):
                            for g4 in range(2):
                                kbs = list(range(4 * g4, 4 * g4 + 4))
                                tprev = n_state.get("nT", 0)

                                def nq(s, kbs=kbs, tens=tens, b=b,
                                       tprev=tprev):
                                    if tprev:
                                        s.wait_ge(sT.h, tprev)
                                    for kblk in kbs:
                                        sSc.tag(s.activation(
                                            out=nsq[kblk % 4][:, :],
                                            in_=tens[:, 4096 * b +
                                                     512 * kblk:
                                                     4096 * b +
                                                     512 * (kblk + 1)],
                                            func=AF.Square))
                                Sq(nq)
                                smark = sSc.n
                                vprev = n_state.get("nmax", 0)

                                def nt(t, kbs=kbs, smark=smark,
                                       vprev=vprev):
                                    t.wait_ge(sSc.h, smark)
                                    if vprev:
                                        t.wait_ge(sV.h, vprev)
                                    for kblk in kbs:
                                        sT.tag(t.matmul(
                                            psN4[kblk % 4][:, :],
                                            ones_c[:, :],
                                            nsq[kblk % 4][:, :],
                                            start=True, stop=True))
                                Tq(nt)
                                n_state["nT"] = sT.n
                                tmark = sT.n

                                def nv(v, kbs=kbs, tmark=tmark):
                                    v.wait_ge(sT.h, tmark)
                                    v.wait_ge(sV.h, sV.n)
                                    for kblk in kbs:
                                        sV.tag(v.reduce_max(
                                            nrow[:, kblk:kblk + 1],
                                            psN4[kblk % 4][:, :],
                                            axis=AXX))
                                Vq(nv)
                                n_state["nmax"] = sV.n

                            def nred(v, u=u, which=which):
                                v.wait_ge(sV.h, sV.n)
                                sV.tag(v.reduce_max(
                                    nmx[:, 8 * which + u:8 * which + u + 1],
                                    nrow[:, :], axis=AXX))
                            Vq(nred)
                    vn = sV.n

                    def ubv(v, vn=vn):
                        v.wait_ge(sV.h, vn)
                        sV.tag(v.tensor_tensor(out=ub1[:, :], in0=nmx[:, 0:4],
                                               in1=nmx[:, 8:12], op=ALU.mult))
                    Vq(ubv)
                    vub = sV.n

                    def ubs(s, vub=vub):
                        s.wait_ge(sV.h, vub)
                        sSc.tag(s.activation(out=ub1[:, :], in_=ub1[:, :],
                                             func=AF.Sqrt))
                    Sq(ubs)
                    sub = sSc.n

                    def ubv2(v, sub=sub):
                        v.wait_ge(sSc.h, sub)
                        sV.tag(v.tensor_scalar(out=ub1b[:, :], in0=ub1[:, :],
                                               scalar1=-ISC, scalar2=None,
                                               op0=ALU.mult))
                    Vq(ubv2)
                    vub2 = sV.n

                    def ubt(t, vub2=vub2):
                        t.wait_ge(sV.h, vub2)
                        sT.tag(t.matmul(psB[:, :4], ones_r[:, :],
                                        ub1b[:, :], start=True, stop=True))
                    Tq(ubt)
                    tub = sT.n

                    def ubv3(v, tub=tub):
                        v.wait_ge(sT.h, tub)
                        for u in range(4):
                            sV.tag(v.tensor_copy(bias_u[u][:, :],
                                                 psB[:, u:u + 1]))
                    Vq(ubv3)
                barrier()

                with contextlib.ExitStack() as stA:
                    EA2 = stA.enter_context
                    pt = [EA2(nc.sbuf_tensor(f"pt{i}", [128, 512], BF16))
                          for i in range(8)]
                    srow = EA2(nc.sbuf_tensor("srow", [1, 512], F32))
                    rrow_a = EA2(nc.sbuf_tensor("rrow_a", [1, 512], BF16))
                    rb_s = EA2(nc.sbuf_tensor("rb_s", [128, 512], BF16))
                    psST = [EA2(nc.psum_tensor(f"psST{i}", [128, 512], F32))
                            for i in range(4)]
                    psAV = EA2(nc.psum_tensor("psAV", [128, 512], F32))
                    psSS = EA2(nc.psum_tensor("psSS", [1, 512], F32))
                    psRB = EA2(nc.psum_tensor("psRB", [128, 512], F32))
                    a_state = {}

                    for u in range(4):
                        hl, b = u // 2, u % 2
                        for qc in range(8):
                            nkc = 4 * qc + 4
                            ngr = nkc // 4
                            st_t = {}
                            exp_t = {}
                            zero_t = {}
                            mask_v = {}

                            def consume(t, kc, u=u, qc=qc, hl=hl, b=b,
                                        nkc=nkc, st_t=st_t, exp_t=exp_t,
                                        zero_t=zero_t):
                                par = kc % 8
                                t.wait_ge(sSc.h, exp_t[kc])
                                if kc in zero_t:
                                    t.wait_ge(sG.h, zero_t[kc])
                                if kc == 0:
                                    pv = a_state.get("av", 0)
                                    if pv:
                                        t.wait_ge(sV.h, pv)
                                    pv = a_state.get("ss", 0)
                                    if pv:
                                        t.wait_ge(sV.h, pv)
                                sT.tag(t.matmul(psSS[:, :], ones_c[:, :],
                                                pt[par][:, :],
                                                start=(kc == 0),
                                                stop=(kc == nkc - 1)))
                                sT.tag(t.matmul(
                                    psAV[:, :],
                                    vtm[hl][:, 4096 * b + 128 * kc:
                                            4096 * b + 128 * (kc + 1)],
                                    pt[par][:, :],
                                    start=(kc == 0),
                                    stop=(kc == nkc - 1)))

                            ptg_end = {}
                            for g in range(ngr):
                                kcs = list(range(4 * g, 4 * g + 4))

                                def tsec(t, g=g, kcs=kcs, hl=hl, b=b, qc=qc,
                                         st_t=st_t, ptg_end=ptg_end):
                                    for kc in kcs:
                                        prev = a_state.get(("st", kc % 4), 0)
                                        if prev:
                                            t.wait_ge(sSc.h, prev)
                                        sT.tag(t.matmul(
                                            psST[kc % 4][:, :],
                                            kt[hl][:, 4096 * b + 128 * kc:
                                                   4096 * b +
                                                   128 * (kc + 1)],
                                            qt[hl][:, 4096 * b + 512 * qc:
                                                   4096 * b +
                                                   512 * (qc + 1)],
                                            start=True, stop=True))
                                        st_t[kc] = sT.n
                                    if g >= 1:
                                        for kc2 in range(4 * (g - 1), 4 * g):
                                            consume(t, kc2)
                                    ptg_end[g] = sT.n
                                Tq(tsec)

                                if g == ngr - 1:
                                    def msec(v, kcs=kcs, qc=qc, st_t=st_t,
                                             mask_v=mask_v):
                                        for kc in kcs:
                                            v.wait_ge(sT.h, st_t[kc])
                                            jd = kc - 4 * qc
                                            sV.tag(v.tensor_tensor(
                                                out=psST[kc % 4][
                                                    :, 128 * jd:
                                                    128 * (jd + 1)],
                                                in0=psST[kc % 4][
                                                    :, 128 * jd:
                                                    128 * (jd + 1)],
                                                in1=mask_t[:, :],
                                                op=ALU.add))
                                            mask_v[kc] = sV.n
                                    Vq(msec)

                                def esec(s, g=g, kcs=kcs, u=u, st_t=st_t,
                                         mask_v=mask_v, exp_t=exp_t,
                                         ptg_end=ptg_end):
                                    prev = (ptg_end[g - 1] if g >= 2
                                            else a_state.get("qc_prev", 0))
                                    if prev:
                                        s.wait_ge(sT.h, prev)
                                    for kc in kcs:
                                        s.wait_ge(sT.h, st_t[kc])
                                        if kc in mask_v:
                                            s.wait_ge(sV.h, mask_v[kc])
                                        sSc.tag(s.activation(
                                            out=pt[kc % 8][:, :],
                                            in_=psST[kc % 4][:, :],
                                            func=AF.Exp,
                                            bias=bias_u[u][:, 0:1],
                                            scale=ISC))
                                        a_state[("st", kc % 4)] = sSc.n
                                        exp_t[kc] = sSc.n
                                Sq(esec)

                                if g == ngr - 1:
                                    def zsec(gp, kcs=kcs, qc=qc, exp_t=exp_t,
                                             zero_t=zero_t):
                                        for kc in kcs:
                                            jd = kc - 4 * qc
                                            if 1 <= jd <= 3:
                                                gp.wait_ge(sSc.h,
                                                           exp_t[kc])
                                                sG.tag(gp.memset(
                                                    pt[kc % 8][
                                                        :, 0:128 * jd],
                                                    0.0))
                                                zero_t[kc] = sG.n
                                    Gq(zsec)

                            def lastc(t, nkc=nkc, consume=consume):
                                for kc2 in range(nkc - 4, nkc):
                                    consume(t, kc2)
                                a_state["qc_prev"] = sT.n
                            Tq(lastc)
                            tdone = sT.n

                            def ep_v1(v, tdone=tdone):
                                v.wait_ge(sT.h, tdone)
                                prev = a_state.get("rb_t", 0)
                                if prev:
                                    v.wait_ge(sT.h, prev)
                                sV.tag(v.tensor_copy(srow[:, :], psSS[:, :]))
                                v.wait_ge(sV.h, sV.n)
                                sV.tag(v.reciprocal(srow[:, :], srow[:, :]))
                                v.wait_ge(sV.h, sV.n)
                                sV.tag(v.tensor_copy(rrow_a[:, :],
                                                     srow[:, :]))
                            Vq(ep_v1)
                            a_state["ss"] = sV.n
                            vrr = sV.n

                            def ep_t(t, vrr=vrr):
                                t.wait_ge(sV.h, vrr)
                                prev = a_state.get("rb_v", 0)
                                if prev:
                                    t.wait_ge(sV.h, prev)
                                sT.tag(t.matmul(psRB[:, :], ones_r[:, :],
                                                rrow_a[:, :],
                                                start=True, stop=True))
                            Tq(ep_t)
                            a_state["rb_t"] = sT.n
                            trb = sT.n

                            def ep_v2(v, trb=trb, u=u, qc=qc, tdone=tdone):
                                v.wait_ge(sT.h, trb)
                                v.wait_ge(sT.h, tdone)
                                sV.tag(v.tensor_copy(rb_s[:, :], psRB[:, :]))
                                v.wait_ge(sV.h, sV.n)
                                sV.tag(v.scalar_tensor_tensor(
                                    out=o_tm[u][:, 512 * qc:512 * (qc + 1)],
                                    in0=psAV[:, :], scalar=1.0,
                                    in1=rb_s[:, :],
                                    op0=ALU.mult, op1=ALU.mult))
                                a_state["av"] = sV.n
                            Vq(ep_v2)
                            a_state["rb_v"] = sV.n
                        if u % 2 == 1:
                            hl_ = u // 2
                            o_mk = sV.n

                            def a2aw(d, hl_=hl_, o_mk=o_mk):
                                d.wait_ge(sV.h, o_mk)
                                for rd in range(8):
                                    b_, sl = rd // 4, rd % 4
                                    sPz[0].tag(d.dma_start(
                                        out=a2a_i[hl_][128 * rd:
                                                       128 * (rd + 1), :],
                                        in_=o_tm[2 * hl_ + b_][
                                            :, 1024 * sl:
                                            1024 * (sl + 1)]), 16)
                            Dq(a2aw)
                            pz_mk = sPz[0].n

                            def a2ac(g, hl_=hl_, pz_mk=pz_mk):
                                g.wait_ge(sPz[0].h, pz_mk)
                                with allow_ici_all_to_all():
                                    sCC.tag(g.collective_compute(
                                        "AllToAll", ALU.bypass,
                                        replica_groups=[list(range(8))],
                                        ins=[a2a_i[hl_]],
                                        outs=[a2a_o[hl_]]), 1)
                            Gq(a2ac)
                            cc_a2a[hl_] = sCC.n
                barrier()
            barrier(ex=(sCC,))

        xr2_st = contextlib.ExitStack()
        x_res2 = xr2_st.enter_context(
            nc.sbuf_tensor("x_res2", [128, 16 * TC], F32))

        with contextlib.ExitStack() as stP9:
            EP = stP9.enter_context
            wob = EP(nc.sbuf_tensor("wob", [128, 16 * HID], BF16))
            og = [EP(nc.sbuf_tensor(f"og{i}", [128, 16 * 512], BF16))
                  for i in range(2)]
            psZ = [EP(nc.psum_tensor(f"psZ{i}", [128, 512], F32))
                   for i in range(2)]

            def wol(d):
                d.wait_ge(sXr[1].h, all_cast_total)
                for hh in range(16):
                    sWo.tag(d.dma_start(
                        out=wob[:, HID * hh:HID * (hh + 1)],
                        in_=wo_bf[128 * hh:128 * (hh + 1), :]), 16)
            Dq(wol)
            wo_marks = sWo.n

            def x2_load(d):
                for hc in range(16):
                    sLN[0].tag(d.dma_start(
                        out=x_res2[:, 1024 * hc:1024 * (hc + 1)],
                        in_=x_sp[128 * hc:128 * (hc + 1), :]), 16)
            Dq(x2_load)
            x2_marks = sLN[0].n

            ogd = {}
            for tb in range(2):
                def ogl(d, tb=tb):
                    for par in range(2):
                        d.wait_ge(sCC.h, cc_a2a[par])
                        for hh in range(par, 16, 2):
                            sXg[tb].tag(d.dma_start(
                                out=og[tb][:, 512 * hh:512 * (hh + 1)],
                                in_=a2a_o[par][128 * (hh // 2):
                                               128 * (hh // 2) + 128,
                                               512 * tb:512 * (tb + 1)]),
                                16)
                Sq(ogl)
                ogd[tb] = sXg[tb].n
            z_state = {}
            first_z = [True]
            for tb in range(2):
                for ho in range(16):
                    par = ho % 2
                    eprev = z_state.get(("v", par), 0)

                    def zmm(t, tb=tb, ho=ho, par=par, eprev=eprev,
                            wo_marks=wo_marks):
                        t.wait_ge(sXg[tb].h, ogd[tb])
                        t.wait_ge(sWo.h, wo_marks)
                        if eprev:
                            t.wait_ge(sV.h, eprev)
                        for hh in range(16):
                            i_ = t.matmul(
                                psZ[par][:, :],
                                wob[:, HID * hh + 128 * ho:
                                    HID * hh + 128 * (ho + 1)],
                                og[tb][:, 512 * hh:512 * (hh + 1)],
                                start=(hh == 0), stop=(hh == 15))
                            if hh == 15:
                                sT.tag(i_)
                    Tq(zmm)
                    tmark = sT.n

                    def zadd(v, tb=tb, ho=ho, par=par, tmark=tmark):
                        if first_z[0]:
                            first_z[0] = False
                            v.wait_ge(sLN[0].h, x2_marks)
                        v.wait_ge(sT.h, tmark)
                        sV.tag(v.tensor_tensor(
                            out=x_res2[:, 1024 * ho + 512 * tb:
                                       1024 * ho + 512 * (tb + 1)],
                            in0=psZ[par][:, :],
                            in1=x_res2[:, 1024 * ho + 512 * tb:
                                       1024 * ho + 512 * (tb + 1)],
                            op=ALU.add))
                    Vq(zadd)
                    z_state[("v", par)] = sV.n
            x2_v_done = sV.n
        barrier()

        ffn_phase(x_res2, out_t, [(sV, x2_v_done)])
        xr2_st.close()

        nout = [c.n for c in sOut]

        def fin(g, nout=nout):
            for i in range(4):
                g.wait_ge(sOut[i].h, nout[i])
        Gq(fin)

    return nc


_PROG = None


def kernel(L0_attn_x, L0_attn_w_qkv, L0_attn_w_out, L0_ffn_gate_up_packed,
           L0_ffn_gate_up_scales, L0_ffn_down_packed, L0_ffn_down_scales):
    global _PROG
    x = np.asarray(L0_attn_x, dtype=np.float32).reshape(T, HID)
    wqkv = np.asarray(L0_attn_w_qkv, dtype=np.float32)
    wout = np.asarray(L0_attn_w_out, dtype=np.float32)
    gup = np.asarray(L0_ffn_gate_up_packed, dtype=np.int32)
    gus = np.asarray(L0_ffn_gate_up_scales)
    dnp = np.asarray(L0_ffn_down_packed, dtype=np.int32)
    dns = np.asarray(L0_ffn_down_scales)

    in_maps = []
    for c in range(NCORES):
        cols = []
        for which in range(3):
            for hl in range(2):
                h = 2 * c + hl
                base = (which * NH + h) * HD
                cols.extend(range(base, base + HD))
        in_maps.append({
            "x_t": np.ascontiguousarray(x[TC * c:TC * (c + 1)].T),
            "wqkv_sl": np.ascontiguousarray(wqkv[:, cols]),
            "wout_sl": wout,
            "gup_sl": np.ascontiguousarray(gup[OGU_C * c:OGU_C * (c + 1)]),
            "gus_sl": np.ascontiguousarray(gus[:, OGU_C * c:OGU_C * (c + 1)]),
            "dnp_sl": np.ascontiguousarray(dnp[DH_C * c:DH_C * (c + 1)]),
            "dns_sl": np.ascontiguousarray(dns[:, DH_C * c:DH_C * (c + 1)]),
        })

    if _PROG is None:
        _PROG = build_program()
    res = run_bass_kernel_spmd(_PROG, in_maps, core_ids=list(range(NCORES)))
    out = np.empty((T, HID), dtype=np.float32)
    for c in range(NCORES):
        out[TC * c:TC * (c + 1)] = res.results[c]["out_t"].T
    return out.reshape(B, S, HID)
